# version 22 (final)
# baseline (speedup 1.0000x reference)
import numpy as np
import ml_dtypes
from contextlib import ExitStack

import concourse.bass as bass
import concourse.mybir as mybir
from concourse.bass_utils import run_bass_kernel_spmd

F32 = mybir.dt.float32
BF16 = mybir.dt.bfloat16
AF = mybir.ActivationFunctionType
OP = mybir.AluOpType
AX = mybir.AxisListType.X
SP_ENG = mybir.EngineType.SP

D = 2048
DQ, DV = 128, 256
E_, H_ = 8, 1408
NHT = H_ // 128
NKD = D // 128
NKH = E_ * NHT
QKSCALE = float(DQ) ** -0.5
LNSC = float(np.log(QKSCALE))
EPS1, EPS2, EPSD = 1e-5, 1e-6, 1e-6
NEG = -1e30

W1BLK = NKD * 128 * 128
W2BLK = 128 * D
W1TOT = E_ * NHT * W1BLK
W2TOT = E_ * NHT * W2BLK
WOUTOFF = W1TOT + W2TOT
WFTOT = WOUTOFF + 128 * 16 * 2048
WFSL = WFTOT // 8

(TI0, TF0, TI1, TF1, FL0, FL1, I50, I51, BL0, BL1, ML0, ML1,
 BG0, BG1, SC0, SC1, UU0, UU1, GG0, GG1, MM0, MM1, LR0, LR1) = range(24)
NSLOT = 24
EXPU, S1Q, S2Q, AAQ, LAMQ, EMQ = range(6)


class CFG:
    def __init__(self, S):
        self.S = S
        self.TOK = S // 4
        self.BLK = 256
        self.NBLK = S // self.BLK
        self.CPB = 2
        self.NTT = self.TOK // 128
        self.TTH = self.NTT // 2
        self.THALF = self.TOK // 2


def build_program(cfg, debug=False):
    S, TOK, BLK, NBLK, CPB, NTT, TTH, THALF = (cfg.S, cfg.TOK, cfg.BLK, cfg.NBLK,
                                               cfg.CPB, cfg.NTT, cfg.TTH, cfg.THALF)
    nc = bass.Bass("TRN2", target_bir_lowering=False)

    def din(name, shape, dt=F32):
        return nc.dram_tensor(name, shape, dt, kind="ExternalInput").ap()

    xb = din("xb", [S, D])
    xresd = din("xresd", [TOK, D])
    wqT = din("wqT", [128, NKD * 256])
    wkT = din("wkT", [128, NKD * 256])
    wvT = din("wvT", [128, NKD * 512])
    wogT = din("wogT", [128, NKD * 512])
    wifT = din("wifT", [128, NKD * 4])
    bifr = din("bifr", [1, 4])
    gammag = din("gammag", [1, 512])
    wmod_sl = din("wmod_sl", [D, 6 * 512])
    gwTd = din("gwTd", [D, E_])
    gb2d = din("gb2d", [1, E_])
    gbiasd = din("gbiasd", [1, E_])
    b1e = din("b1e", [E_, H_])
    b2e = din("b2e", [E_, D])
    cvecd = din("cvecd", [1, D])
    wffn = din("wffn", [WFSL])

    out = nc.dram_tensor("out", [TOK, D], F32, kind="ExternalOutput").ap()
    dbg = {}
    if debug:
        for nm, shp, dt in [("dbg_xhat", [128, D], F32),
                            ("dbg_hq", [128, 257], F32),
                            ("dbg_x2", [128, D], F32),
                            ("dbg_col", [128, 12, 2], F32),
                            ("dbg_gw3", [2, NSLOT, 128], F32),
                            ("dbg_state", [128, 257], F32),
                            ("dbg_gq", [2, 12, 128], F32),
                            ("dbg_comb", [128, E_], F32)]:
            dbg[nm] = nc.dram_tensor(nm, shp, dt, kind="ExternalOutput").ap()

    mod_in = nc.dram_tensor("mod_in", [1, 6 * 512], F32, kind="Internal")
    mod_out = nc.dram_tensor("mod_out", [4, 6 * 512], F32, kind="Internal")
    hout_in = nc.dram_tensor("hout_in", [512, S], BF16, kind="Internal")
    hout_out = nc.dram_tensor("hout_out", [2048, S], BF16, kind="Internal")
    x2d = nc.dram_tensor("x2d", [cfg.TOK, D], F32, kind="Internal")
    wffn_my = nc.dram_tensor("wffn_my", [WFSL], BF16, kind="Internal")
    wffn_bf = nc.dram_tensor("wffn_bf", [WFTOT], BF16, kind="Internal")

    ident_f = nc.inline_tensor(np.eye(128, dtype=np.float32), "ident_f")
    ident_b = nc.inline_tensor(np.eye(128, dtype=np.float32).astype(ml_dtypes.bfloat16), "ident_b")
    masksc_t = nc.inline_tensor(np.triu(np.full((128, 128), QKSCALE, np.float32)), "masksc")

    with ExitStack() as ctx:
        Et = ctx.enter_context

        class Sem:
            def __init__(self, name):
                self.h = Et(nc.semaphore(name))
                self.n = 0

        _sem_names = ["LC", "LCP", "LX0", "LX1", "EX", "LH", "LT0", "LT1",
                      "SX0", "SX1", "SO0", "SO1", "GC", "GM0", "GM1",
                      "W10", "W11", "W20", "W21", "TE", "VE", "SE", "CC", "DB"]
        SEMS = {k: Sem(k) for k in _sem_names}
        (LC, LCP, LX0, LX1, EX, LH, LT0, LT1, SX0, SX1, SO0, SO1, GC, GM0,
         GM1, W10, W11, W20, W21, TE, VE, SE, CC, DB) = (SEMS[k] for k in _sem_names)
        LX = [LX0, LX1]
        LT = [LT0, LT1]
        SX = [SX0, SX1]
        SO = [SO0, SO1]
        GM = [GM0, GM1]
        W1S = [W10, W11]
        W2S = [W20, W21]

        def snap():
            return {k: s.n for k, s in SEMS.items()}

        def waits(eng, ss, only):
            for k in only:
                if ss[k] > 0:
                    eng.wait_ge(SEMS[k].h, ss[k])

        def I(sem, instr, k=1):
            instr.then_inc(sem.h, k)
            sem.n += k
            return instr

        onesf = Et(nc.sbuf_tensor("onesf", [1, 512], F32))
        onesb = Et(nc.sbuf_tensor("onesb", [1, 128], BF16))
        ones2 = Et(nc.sbuf_tensor("ones2", [2, 128], F32))
        epsc = Et(nc.sbuf_tensor("epsc", [128, 2], F32))
        ident_fs = Et(nc.sbuf_tensor("ident_fs", [128, 128], F32))
        ident_bs = Et(nc.sbuf_tensor("ident_bs", [128, 128], BF16))
        modcol = Et(nc.sbuf_tensor("modcol", [128, 4, NKD], F32))
        garep = Et(nc.sbuf_tensor("garep", [128, D], F32))
        gbrep = Et(nc.sbuf_tensor("gbrep", [128, D], F32))
        b1col = Et(nc.sbuf_tensor("b1col", [128, E_, NHT], F32))
        b2p = Et(nc.sbuf_tensor("b2p", [E_, D], BF16))
        gbs = Et(nc.sbuf_tensor("gbs", [1, 2 * E_], F32))
        gw_s = Et(nc.sbuf_tensor("gw_s", [128, NKD, E_], BF16))
        garow = Et(nc.sbuf_tensor("garow", [1, 512], F32))
        gascr = Et(nc.sbuf_tensor("gascr", [1, 512], F32))
        garo2 = Et(nc.sbuf_tensor("garo2", [1, 512], F32))
        gamrow = Et(nc.sbuf_tensor("gamrow", [1, 512], F32))
        rowcb = Et(nc.sbuf_tensor("rowcb", [1, 4], BF16))

        P = [Et(nc.psum_tensor(f"P{i}", [128, 512], F32)) for i in range(7)]

        blk = Et(nc.Block())

        @blk.vector
        def _(v):
            I(VE, v.memset(onesf[:, :], 1.0))
            I(VE, v.memset(onesb[:, :], 1.0))
            I(VE, v.memset(ones2[:, :], 1.0))
            I(VE, v.memset(epsc[:, 0:1], EPS1))
            I(VE, v.memset(epsc[:, 1:2], EPS2))

        rowc = Et(nc.sbuf_tensor("rowc", [1, 4], F32))

        @blk.sync
        def _(sy):
            I(LC, sy.dma_start(out=ident_fs[:, :], in_=ident_f.ap()), 16)
            I(LC, sy.dma_start(out=ident_bs[:, :], in_=ident_b.ap()), 16)
            I(LC, sy.dma_start(out=gbs[0:1, 0:E_], in_=gb2d[:, :]), 16)
            I(LC, sy.dma_start(out=gbs[0:1, E_:2 * E_], in_=gbiasd[:, :]), 16)
            I(LC, sy.dma_start(out=gamrow[0:1, :], in_=gammag[:, :]), 16)
            I(LC, sy.dma_start(out=rowc[:, :], in_=bifr[:, :]), 16)

        cvcol = Et(nc.sbuf_tensor("cvcol", [128, NKD], F32))

        @blk.sync
        def _(sy):
            I(LC, sy.dma_start(out=cvcol[:, :],
                               in_=cvecd[0, :].rearrange("(k p) -> p k", p=128),
                               allow_slow_non_contiguous=True), 16)

        @blk.sync
        def _(sy):
            for e in range(E_):
                I(LC, sy.dma_start(out=b1col[:, e, :],
                                   in_=b1e[e, :].rearrange("(h p) -> p h", p=128),
                                   allow_slow_non_contiguous=True), 16)
        ss_consts = snap()

        @blk.vector
        def _(v):
            waits(v, ss_consts, ["LC"])
            I(VE, v.tensor_copy(rowcb[:, :], rowc[:, :]))


        siluc = Et(nc.sbuf_tensor("siluc", [128, NKD], F32))
        modmy = Et(nc.sbuf_tensor("modmy", [1, 2, 512], F32))

        @blk.scalar
        def _(s):
            waits(s, ss_consts, ["LC"])
            I(SE, s.activation(siluc[:, :], cvcol[:, :], AF.Exp, bias=0.0, scale=-1.0))
        ss1 = snap()

        @blk.vector
        def _(v):
            waits(v, ss1, ["SE", "LC"])
            I(VE, v.tensor_scalar(siluc[:, :], siluc[:, :], 1.0, None, OP.add))
            v.wait_ge(VE.h, VE.n)
            I(VE, v.reciprocal(siluc[:, :], siluc[:, :]))
            v.wait_ge(VE.h, VE.n)
            I(VE, v.tensor_mul(siluc[:, :], siluc[:, :], cvcol[:, :]))
        ss_silu = snap()

        with nc.sbuf_tensor("modw0", [128, 6 * 512], F32) as modw0, \
             nc.sbuf_tensor("modw1", [128, 6 * 512], F32) as modw1:
            modw = [modw0, modw1]
            te0m = TE.n

            @blk.sync
            def _(sy):
                for k in range(NKD):
                    if k >= 2:
                        sy.wait_ge(TE.h, te0m + (k - 1))
                    I(GM[k % 2], sy.dma_start(out=modw[k % 2][:, :],
                                              in_=wmod_sl[k * 128:(k + 1) * 128, :]), 16)

            @blk.tensor
            def _(t):
                waits(t, ss_silu, ["VE"])
                for k in range(NKD):
                    t.wait_ge(GM[k % 2].h, 16 * (k // 2 + 1))
                    for cc in range(6):
                        mi = t.matmul(P[cc][0:1, :], siluc[:, k:k + 1],
                                      modw[k % 2][:, cc * 512:(cc + 1) * 512],
                                      start=(k == 0), stop=(k == NKD - 1))
                        if cc == 5:
                            I(TE, mi)
            ss_modmm = snap()

            for cc in range(6):
                @blk.vector
                def _(v, cc=cc, sw=snap()):
                    waits(v, sw, ["TE"])
                    if cc >= 2:
                        v.wait_ge(LX[cc % 2].h, mod_st_ck[cc - 2])
                    I(VE, v.tensor_copy(modmy[0:1, cc % 2, :], P[cc][0:1, :]))
                sse = snap()

                @blk.sync
                def _(sy, cc=cc, sse=sse):
                    waits(sy, sse, ["VE"])
                    I(LX[cc % 2], sy.dma_start(out=mod_in.ap()[0, cc * 512:(cc + 1) * 512][None, :],
                                               in_=modmy[0:1, cc % 2, :]), 16)
                if cc == 0:
                    mod_st_ck = {}
                mod_st_ck[cc] = LX[cc % 2].n
            ss_modst = snap()

            @blk.gpsimd
            def _(gp):
                gp.wait_ge(LX0.h, ss_modst["LX0"])
                gp.wait_ge(LX1.h, ss_modst["LX1"])
                I(CC, gp.collective_compute("AllGather", OP.bypass,
                                            replica_groups=[[0, 1, 2, 3], [4, 5, 6, 7]],
                                            ins=[mod_in.ap().opt()], outs=[mod_out.ap().opt()]))
            ss_modag = snap()

            @blk.sync
            def _(sy):
                waits(sy, ss_modag, ["CC"])
                for slot, q in [(0, 0), (1, 1), (2, 3), (3, 4)]:
                    for jj in range(4):
                        I(LC, sy.dma_start(out=modcol[:, slot, jj * 4:(jj + 1) * 4],
                                           in_=mod_out.ap()[jj, q * 512:(q + 1) * 512]
                                           .rearrange("(k p) -> p k", p=128),
                                           allow_slow_non_contiguous=True), 16)
            prev_mq = snap()

            @blk.vector
            def _(v, pm=prev_mq):
                waits(v, pm, ["VE", "LC"])
                I(VE, v.tensor_scalar(modcol[:, 1, :], modcol[:, 1, :], 1.0, None, OP.add))
                I(VE, v.tensor_scalar(modcol[:, 3, :], modcol[:, 3, :], 1.0, None, OP.add))
            ss_ca1 = snap()

            prev_g = ss_ca1
            for qi, q in [(0, 2), (1, 5)]:
                for jj in range(4):
                    @blk.sync
                    def _(sy, q=q, jj=jj, pg=prev_g):
                        waits(sy, pg, ["VE"])
                        I(LC, sy.dma_start(out=garow[0:1, :],
                                           in_=mod_out.ap()[jj, q * 512:(q + 1) * 512][None, :]), 16)
                    ssg1 = snap()

                    @blk.vector
                    def _(v, ssg1=ssg1):
                        waits(v, ssg1, ["LC"])
                        I(VE, v.tensor_scalar(gascr[0:1, :], garow[0:1, :], 0.0, None, OP.is_ge))
                        v.wait_ge(VE.h, VE.n)
                        I(VE, v.tensor_scalar(gascr[0:1, :], gascr[0:1, :], 2.0, None, OP.mult))
                        v.wait_ge(VE.h, VE.n)
                        I(VE, v.tensor_scalar(gascr[0:1, :], gascr[0:1, :], -1.0, None, OP.add))
                        I(VE, v.tensor_scalar(garo2[0:1, :], garow[0:1, :], -1.0, None, OP.mult))
                        v.wait_ge(VE.h, VE.n)
                        I(VE, v.tensor_max(garow[0:1, :], garow[0:1, :], garo2[0:1, :]))
                        v.wait_ge(VE.h, VE.n)
                        I(VE, v.tensor_scalar(garow[0:1, :], garow[0:1, :], 0.25, None, OP.max))
                        v.wait_ge(VE.h, VE.n)
                        I(VE, v.tensor_mul(garow[0:1, :], garow[0:1, :], gascr[0:1, :]))
                    ssg2 = snap()

                    @blk.tensor
                    def _(t, ssg2=ssg2):
                        waits(t, ssg2, ["VE"])
                        I(TE, t.matmul(P[0][:, :], onesf[0:1, 0:128], garow[0:1, :],
                                       start=True, stop=True))
                    ssg3 = snap()

                    @blk.vector
                    def _(v, qi=qi, jj=jj, ssg3=ssg3):
                        waits(v, ssg3, ["TE"])
                        if qi == 0:
                            I(VE, v.tensor_copy(garep[:, jj * 512:(jj + 1) * 512], P[0][:, :]))
                        else:
                            I(VE, v.tensor_copy(gbrep[:, jj * 512:(jj + 1) * 512], P[0][:, :]))
                    prev_g = snap()
            ss_garep = prev_g

            @blk.tensor
            def _(t):
                waits(t, ss_garep, ["VE"])
                I(TE, t.matmul(P[1][:, :], onesf[0:1, 0:128], gamrow[0:1, :],
                               start=True, stop=True))
            ss_rep2 = snap()

        gamr = Et(nc.sbuf_tensor("gamr", [128, 512], F32))

        @blk.vector
        def _(v):
            waits(v, ss_rep2, ["TE"])
            I(VE, v.tensor_copy(gamr[:, :], P[1][:, :]))
            I(VE, v.tensor_add(gbs[0:1, 0:E_], gbs[0:1, 0:E_], gbs[0:1, E_:2 * E_]))
            v.wait_ge(VE.h, VE.n)
        ss_prol = snap()

        with ExitStack() as p1:
            E1 = p1.enter_context
            b2f = E1(nc.sbuf_tensor("b2f", [E_, D], F32))

            @blk.sync
            def _(sy, sw=snap()):
                waits(sy, sw, ["LC"])
                I(LC, sy.dma_start(out=b2f[:, :], in_=b2e[:, :]), 16)
            ss_b2l = snap()

            @blk.vector
            def _(v, ss_b2l=ss_b2l):
                waits(v, ss_b2l, ["LC"])
                waits(v, ss_prol, ["VE"])
                I(VE, v.tensor_mul(b2p[:, :], b2f[:, :], gbrep[0:E_, :]))

            xblk = [E1(nc.sbuf_tensor(f"xblk{i}", [128, CPB, D], F32)) for i in range(2)]
            hx = [E1(nc.sbuf_tensor(f"hx{i}", [128, NKD, BLK], BF16)) for i in range(2)]
            masks = E1(nc.sbuf_tensor("masks", [128, 128], F32))
            wq_s = E1(nc.sbuf_tensor("wq_s", [128, NKD, 256], BF16))
            wk_s = E1(nc.sbuf_tensor("wk_s", [128, NKD, 256], BF16))
            wv_s = E1(nc.sbuf_tensor("wv_s", [128, NKD, 512], BF16))
            wo_s = E1(nc.sbuf_tensor("wo_s", [128, NKD, 512], BF16))
            wif_s = E1(nc.sbuf_tensor("wif_s", [128, NKD, 4], BF16))
            qb = E1(nc.sbuf_tensor("qb", [128, 2, BLK], F32))
            kb = E1(nc.sbuf_tensor("kb", [128, 2, BLK], F32))
            vb = E1(nc.sbuf_tensor("vb", [128, CPB, 2, 257], F32))
            sgb = E1(nc.sbuf_tensor("sgb", [128, CPB, 2, DV], F32))
            sqs = E1(nc.sbuf_tensor("sqs", [128, D], BF16))
            sts = E1(nc.sbuf_tensor("sts", [128, 10], F32))
            trin = E1(nc.sbuf_tensor("trin", [128, 128], F32))
            gw3 = E1(nc.sbuf_tensor("gw3", [2, NSLOT, 128], F32))
            gq = E1(nc.sbuf_tensor("gq", [2, 12, 128], F32))
            col_s = E1(nc.sbuf_tensor("col_s", [128, 12, CPB], F32))
            cin = E1(nc.sbuf_tensor("cin", [2, 2, 4], F32))
            r1 = E1(nc.sbuf_tensor("r1", [1, 2, 4], F32))
            scr0 = E1(nc.sbuf_tensor("scr0", [1, 2, 8], F32))
            colv = E1(nc.sbuf_tensor("colv", [2, 2, 4], F32))
            carry = E1(nc.sbuf_tensor("carry", [1, 2, 2], F32))
            state = E1(nc.sbuf_tensor("state", [128, 2, 257], F32))
            pp_s = E1(nc.sbuf_tensor("pp_s", [128, 128], F32))
            ak_s = E1(nc.sbuf_tensor("ak_s", [128, 128], F32))
            hq_s = E1(nc.sbuf_tensor("hq_s", [128, 257], F32))
            tmp1 = E1(nc.sbuf_tensor("tmp1", [128, 257], F32))
            hNb = E1(nc.sbuf_tensor("hNb", [128, CPB, 2, DV], F32))
            nsqb = E1(nc.sbuf_tensor("nsqb", [128, CPB * 2 * DV], F32))
            nstat = E1(nc.sbuf_tensor("nstat", [128, 8], F32))
            nst2 = E1(nc.sbuf_tensor("nst2", [128, 16], F32))
            exb = E1(nc.sbuf_tensor("exb", [128, 4, BLK], BF16))
            PBt = E1(nc.psum_tensor("PBt", [128, 256], F32))

            @blk.sync
            def _(sy):
                I(LCP, sy.dma_start(out=masks[:, :], in_=masksc_t.ap()), 16)

            @blk.gpsimd
            def _(gp):
                for dst, src in [(wq_s, wqT), (wk_s, wkT), (wv_s, wvT),
                                 (wo_s, wogT), (wif_s, wifT)]:
                    I(GC, gp.dma_start(out=dst[:, :, :],
                                       in_=src.rearrange("p (k w) -> p k w", k=NKD)), 16)
                I(GC, gp.dma_start(out=gw_s[:, :, :],
                                   in_=gwTd.rearrange("(k p) w -> p k w", p=128)), 16)
            ss_wattn = snap()

            @blk.gpsimd
            def _(gp):
                waits(gp, ss_wattn, ["GC"])
                q4 = WFSL // 4
                for i in range(4):
                    I(GC, gp.dma_start(out=wffn_my.ap()[i * q4:(i + 1) * q4],
                                       in_=wffn[i * q4:(i + 1) * q4]), 16)
            ss_wcast2 = snap()

            @blk.gpsimd
            def _(gp):
                gp.wait_ge(GC.h, ss_wcast2["GC"])
                I(CC, gp.collective_compute("AllGather", OP.bypass,
                                            replica_groups=[list(range(8))],
                                            ins=[wffn_my.ap().opt()], outs=[wffn_bf.ap().opt()]))
            ss_wfag = snap()

            @blk.vector
            def _(v):
                I(VE, v.memset(gw3[0:2, :, :], 0.0))
                I(VE, v.memset(gq[0:2, :, :], 0.0))
                I(VE, v.memset(trin[:, :], 0.0))
                I(VE, v.memset(carry[0:1, :, :], 0.0))
                I(VE, v.memset(state[:, :, :], 0.0))
                I(VE, v.memset(cin[0:2, :, :], 0.0))
                for j in range(CPB):
                    for h in range(2):
                        I(VE, v.memset(vb[:, j, h, 256:257], 1.0))

            xld_ck = {}

            @blk.sync
            def _(sy):
                for j in range(CPB):
                    I(LX0, sy.dma_start(out=xblk[0][:, j, :], in_=xb[j * 128:(j + 1) * 128, :]), 16)
            xld_ck[0] = LX0.n
            ss_p1init = snap()


            Fsn = {}

            def emit_F1(k, full_ssb=None):
                par = k % 2

                @blk.scalar
                def _(s, par=par, k=k, full_ssb=full_ssb):
                    if full_ssb is not None:
                        waits(s, full_ssb, ["LC", "LCP", "VE", "TE", "EX"])
                    s.wait_ge(LX[k % 2].h, xld_ck[k])
                    for j in range(CPB):
                        I(SE, s.activation(sqs[:, :], xblk[par][:, j, :], AF.Square,
                                           accum_out=sts[:, j:j + 1]))
                        s.wait_ge(SE.h, SE.n)
                ss_ac = snap()

                @blk.vector
                def _(v, par=par, k=k, ss_ac=ss_ac, full_ssb=full_ssb):
                    if full_ssb is not None:
                        waits(v, full_ssb, ["LC", "LCP", "SE", "TE"])
                    v.wait_ge(LX[k % 2].h, xld_ck[k])
                    for j in range(CPB):
                        I(VE, v.reduce_sum(sts[:, 2 + j:3 + j], xblk[par][:, j, :], axis=AX))
                    v.wait_ge(VE.h, VE.n)
                    waits(v, ss_ac, ["SE"])
                    I(VE, v.tensor_scalar(sts[:, 4:6], sts[:, 2:4], 1.0 / D, None, OP.mult))
                    I(VE, v.tensor_scalar(sts[:, 8:10], sts[:, 0:2], 1.0 / D, None, OP.mult))
                    v.wait_ge(VE.h, VE.n)
                    I(VE, v.tensor_mul(sts[:, 6:8], sts[:, 4:6], sts[:, 4:6]))
                    v.wait_ge(VE.h, VE.n)
                    I(VE, v.tensor_sub(sts[:, 6:8], sts[:, 6:8], sts[:, 8:10]))
                    v.wait_ge(VE.h, VE.n)
                ss_mv = snap()

                @blk.scalar
                def _(s, ss_mv=ss_mv):
                    waits(s, ss_mv, ["VE"])
                    I(SE, s.activation(sts[:, 6:8], sts[:, 6:8],
                                       AF.Ln, bias=epsc[:, 0:1], scale=-1.0))
                    s.wait_ge(SE.h, SE.n)
                    I(SE, s.activation(sts[:, 6:8], sts[:, 6:8],
                                       AF.Exp, bias=0.0, scale=-0.5))
                    s.wait_ge(SE.h, SE.n)
                ss_r = snap()

                @blk.vector
                def _(v, par=par, ss_r=ss_r):
                    waits(v, ss_r, ["SE"])
                    for j in range(CPB):
                        I(VE, v.tensor_scalar(xblk[par][:, j, :], xblk[par][:, j, :],
                                              sts[:, 4 + j:5 + j],
                                              sts[:, 6 + j:7 + j],
                                              OP.subtract, OP.mult))
                Fsn[k] = {"xh": snap()}

            def emit_F2(k, ss_colev):
                par = k % 2
                ss_xh = Fsn[k]["xh"]

                if debug and k == 0:
                    @blk.sync
                    def _(sy, ss_xh=ss_xh):
                        waits(sy, ss_xh, ["VE"])
                        I(DB, sy.dma_start(out=dbg["dbg_xhat"][:, :], in_=xblk[0][:, 0, :]), 16)
                    _ssdbg = snap()

                    @blk.vector
                    def _(v, _ssdbg=_ssdbg):
                        waits(v, _ssdbg, ["DB"])
                        I(VE, v.engine_nop())

                te0 = TE.n
                se0 = SE.n

                @blk.tensor
                def _(t, par=par, ss_xh=ss_xh, ss_colev=ss_colev, te0=te0, se0=se0):
                    waits(t, ss_xh, ["VE", "SE"])
                    waits(t, ss_colev, ["VE"])
                    i = 0
                    for j in range(CPB):
                        for kk in range(NKD):
                            if i >= 2:
                                t.wait_ge(SE.h, se0 + i - 1)
                            I(TE, t.transpose(P[i % 2][0:128, 0:128],
                                              xblk[par][:, j, kk * 128:(kk + 1) * 128],
                                              ident_fs[:, :]))
                            i += 1

                @blk.scalar
                def _(s, par=par, te0=te0):
                    i = 0
                    for j in range(CPB):
                        for kk in range(NKD):
                            s.wait_ge(TE.h, te0 + i + 1)
                            I(SE, s.activation(hx[par][:, kk, j * 128:(j + 1) * 128],
                                               P[i % 2][0:128, 0:128], AF.Identity,
                                               bias=modcol[:, 0, kk:kk + 1],
                                               scale=modcol[:, 1, kk:kk + 1]))
                            i += 1
                assert SE.n == se0 + CPB * NKD
                Fsn[k]["hx"] = snap()

            prev = ss_p1init
            for k in range(NBLK):
                par = k % 2
                T0 = k * BLK
                ssb = prev

                if k + 1 < NBLK:
                    @blk.sync
                    def _(sy, k=k, ssb=ssb):
                        waits(sy, ssb, ["TE"])
                        for j in range(CPB):
                            I(LX[(k + 1) % 2], sy.dma_start(out=xblk[(k + 1) % 2][:, j, :],
                                               in_=xb[(k + 1) * BLK + j * 128:
                                                      (k + 1) * BLK + (j + 1) * 128, :]), 16)
                    xld_ck[k + 1] = LX[(k + 1) % 2].n

                if k == 0:
                    emit_F1(0, full_ssb=ssb)
                    emit_F2(0, ssb)

                @blk.tensor
                def _(t, par=par, k=k, ssb=ssb):
                    waits(t, Fsn[k]["hx"], ["SE"])
                    waits(t, ssb, ["VE"])
                    waits(t, ss_wattn, ["GC"])
                    for h in range(2):
                        for kk in range(NKD):
                            mi = t.matmul(P[2][:, h * 256:h * 256 + BLK],
                                          wq_s[:, kk, h * 128:(h + 1) * 128], hx[par][:, kk, :],
                                          start=(kk == 0), stop=(kk == NKD - 1))
                            if kk == NKD - 1:
                                I(TE, mi)
                        for kk in range(NKD):
                            mi = t.matmul(P[3][:, h * 256:h * 256 + BLK],
                                          wk_s[:, kk, h * 128:(h + 1) * 128], hx[par][:, kk, :],
                                          start=(kk == 0), stop=(kk == NKD - 1))
                            if kk == NKD - 1:
                                I(TE, mi)
                ss_qk = snap()

                @blk.vector
                def _(v, ss_qk=ss_qk):
                    waits(v, ss_qk, ["TE"])
                    for h in range(2):
                        I(VE, v.tensor_copy(qb[:, h, :], P[2][:, h * 256:h * 256 + BLK]))
                        I(VE, v.tensor_copy(kb[:, h, :], P[3][:, h * 256:h * 256 + BLK]))
                ss_qke = snap()

                prev_vog = ss_qke
                for j in range(CPB):
                    @blk.tensor
                    def _(t, par=par, j=j, pv=prev_vog):
                        waits(t, pv, ["VE", "SE"])
                        waits(t, ss_wattn, ["GC"])
                        for kk in range(NKD):
                            mi = t.matmul(P[4][:, 0:512], hx[par][:, kk, j * 128:(j + 1) * 128],
                                          wv_s[:, kk, :], start=(kk == 0), stop=(kk == NKD - 1))
                            if kk == NKD - 1:
                                I(TE, mi)
                            mi = t.matmul(P[5][:, 0:512], hx[par][:, kk, j * 128:(j + 1) * 128],
                                          wo_s[:, kk, :], start=(kk == 0), stop=(kk == NKD - 1))
                            if kk == NKD - 1:
                                I(TE, mi)
                            t.matmul(P[6][0:128, 0:4], hx[par][:, kk, j * 128:(j + 1) * 128],
                                     wif_s[:, kk, :], start=(kk == 0), stop=False)
                        I(TE, t.matmul(P[6][0:128, 0:4], onesb[0:1, 0:128], rowcb[0:1, :],
                                       start=False, stop=True))
                    ss_v1 = snap()

                    @blk.vector
                    def _(v, j=j, ss_v1=ss_v1):
                        waits(v, ss_v1, ["TE"])
                        for h in range(2):
                            I(VE, v.tensor_copy(vb[:, j, h, 0:256], P[4][:, h * 256:(h + 1) * 256]))
                    ss_v2 = snap()

                    @blk.scalar
                    def _(s, j=j, ss_v1=ss_v1):
                        waits(s, ss_v1, ["TE"])
                        I(SE, s.activation(sgb[:, j, 0, :], P[5][:, 0:256], AF.Sigmoid,
                                           bias=0.0, scale=1.0))
                        I(SE, s.activation(sgb[:, j, 1, :], P[5][:, 256:512], AF.Sigmoid,
                                           bias=0.0, scale=1.0))
                        I(SE, s.activation(trin[:, j:j + 97:32], P[6][0:128, 0:4],
                                           AF.Tanh, bias=0.0, scale=1.0 / 15.0))
                    prev_vog = snap()
                ss_vog = prev_vog

                @blk.tensor
                def _(t, ss_vog=ss_vog):
                    waits(t, ss_vog, ["SE", "VE"])
                    I(TE, t.transpose(P[0][0:128, 0:128], trin[:, :], ident_fs[:, :]))
                ss_gt = snap()

                @blk.vector
                def _(v, ss_gt=ss_gt):
                    waits(v, ss_gt, ["TE"])
                    I(VE, v.tensor_copy(gw3[0:2, TI0, :], P[0][0:2, 0:128]))
                    I(VE, v.tensor_copy(gw3[0:2, TF0, :], P[0][32:34, 0:128]))
                    I(VE, v.tensor_copy(gw3[0:2, TI1, :], P[0][64:66, 0:128]))
                    I(VE, v.tensor_copy(gw3[0:2, TF1, :], P[0][96:98, 0:128]))
                ss_gr = snap()

                @blk.scalar
                def _(s, ss_gr=ss_gr):
                    waits(s, ss_gr, ["VE"])
                    for h in range(2):
                        I(SE, s.activation(gw3[0:2, SC0 + h, :], gw3[0:2, TF0 + 2 * h, :],
                                           AF.Exp, bias=0.0, scale=-15.0))
                        s.wait_ge(SE.h, SE.n)
                        I(SE, s.activation(gw3[0:2, SC0 + h, :], gw3[0:2, SC0 + h, :],
                                           AF.Ln, bias=1.0, scale=1.0))
                        s.wait_ge(SE.h, SE.n)
                ss_nb = snap()

                @blk.vector
                def _(v, ss_nb=ss_nb):
                    waits(v, ss_nb, ["SE"])
                    for h in range(2):
                        I(VE, v.tensor_scalar(gw3[0:2, FL0 + h, :], gw3[0:2, SC0 + h, :],
                                              -1.0, None, OP.mult))
                        I(VE, v.tensor_scalar(gw3[0:2, I50 + h, :], gw3[0:2, TI0 + 2 * h, :],
                                              15.0, None, OP.mult))
                    v.wait_ge(VE.h, VE.n)
                    for h in range(2):
                        I(VE, v.tensor_tensor_scan(gw3[0:2, BL0 + h, :], gw3[0:2, FL0 + h, :],
                                                   gw3[0:2, FL0 + h, :], 0.0, OP.add, OP.bypass))
                        I(VE, v.tensor_tensor_scan(gw3[0:2, ML0 + h, :], gw3[0:2, FL0 + h, :],
                                                   gw3[0:2, I50 + h, :], NEG, OP.add, OP.max))
                    v.wait_ge(VE.h, VE.n)
                    for h in range(2):
                        I(VE, v.tensor_copy(cin[0:2, h, 0:1], gw3[0:2, BL0 + h, 127:128]))
                        I(VE, v.tensor_copy(cin[0:2, h, 1:2], gw3[0:2, ML0 + h, 127:128]))
                ss_ce = snap()

                @blk.tensor
                def _(t, ss_ce=ss_ce):
                    waits(t, ss_ce, ["VE"])
                    I(TE, t.matmul(P[1][0:1, 0:8], ident_fs[0:2, 1:2],
                                   cin[0:2, :, :].rearrange("p a b -> p (a b)"),
                                   start=True, stop=True))
                ss_cx = snap()

                @blk.vector
                def _(v, ss_cx=ss_cx):
                    waits(v, ss_cx, ["TE"])
                    I(VE, v.tensor_copy(r1[0:1, :, :],
                                        P[1][0:1, 0:8].rearrange("p (a b) -> p a b", a=2)))
                    v.wait_ge(VE.h, VE.n)
                    for h in range(2):
                        be0 = cin[0:1, h, 0:1]
                        me0 = cin[0:1, h, 1:2]
                        be1 = r1[0:1, h, 0:1]
                        me1 = r1[0:1, h, 1:2]
                        Bc = carry[0:1, h, 0:1]
                        mc = carry[0:1, h, 1:2]
                        W = scr0[0:1, h, :]
                        I(VE, v.tensor_copy(W[:, 0:1], Bc))
                        I(VE, v.tensor_copy(W[:, 2:3], mc))
                        v.wait_ge(VE.h, VE.n)
                        I(VE, v.tensor_scalar(W[:, 1:2], be0, W[:, 0:1], None, OP.add))
                        I(VE, v.tensor_scalar(W[:, 3:4], be0, W[:, 2:3], None, OP.add))
                        v.wait_ge(VE.h, VE.n)
                        I(VE, v.tensor_tensor(W[:, 3:4], W[:, 3:4], me0, OP.max))
                        v.wait_ge(VE.h, VE.n)
                        I(VE, v.tensor_scalar(W[:, 6:7], be1, W[:, 1:2], None, OP.add))
                        I(VE, v.tensor_scalar(W[:, 7:8], be1, W[:, 3:4], None, OP.add))
                        v.wait_ge(VE.h, VE.n)
                        I(VE, v.tensor_tensor(W[:, 7:8], W[:, 7:8], me1, OP.max))
                        I(VE, v.tensor_tensor(W[:, 4:5], W[:, 2:3], W[:, 0:1], OP.subtract))
                        I(VE, v.tensor_tensor(W[:, 5:6], W[:, 3:4], W[:, 1:2], OP.subtract))
                        v.wait_ge(VE.h, VE.n)
                        I(VE, v.tensor_scalar(W[:, 4:6], W[:, 4:6], LNSC, None, OP.add))
                        v.wait_ge(VE.h, VE.n)
                        I(VE, v.tensor_copy(carry[0:1, h, 0:1], W[:, 6:7]))
                        I(VE, v.tensor_copy(carry[0:1, h, 1:2], W[:, 7:8]))
                ss_cw = snap()

                @blk.tensor
                def _(t, ss_cw=ss_cw):
                    waits(t, ss_cw, ["VE"])
                    for h in range(2):
                        t.matmul(P[1][0:2, 4 * h + 0:4 * h + 1], scr0[0:1, h, 0:2],
                                 onesf[0:1, 0:1], start=True, stop=True)
                        t.matmul(P[1][0:2, 4 * h + 1:4 * h + 2], scr0[0:1, h, 2:4],
                                 onesf[0:1, 0:1], start=True, stop=True)
                        mi = t.matmul(P[1][0:2, 4 * h + 2:4 * h + 3], scr0[0:1, h, 4:6],
                                      onesf[0:1, 0:1], start=True, stop=True)
                        if h == 1:
                            I(TE, mi)
                ss_cb = snap()

                @blk.vector
                def _(v, ss_cb=ss_cb):
                    waits(v, ss_cb, ["TE"])
                    I(VE, v.tensor_copy(colv[0:2, :, :],
                                        P[1][0:2, 0:8].rearrange("p (a b) -> p a b", a=2)))
                ss_colv = snap()

                @blk.vector
                def _(v, ss_colv=ss_colv):
                    waits(v, ss_colv, ["VE"])
                    for h in range(2):
                        I(VE, v.tensor_scalar(gw3[0:2, BG0 + h, :], gw3[0:2, BL0 + h, :],
                                              colv[0:2, h, 0:1], None, OP.add))
                        I(VE, v.tensor_scalar(gw3[0:2, SC0 + h, :], gw3[0:2, BL0 + h, :],
                                              colv[0:2, h, 1:2], None, OP.add))
                        v.wait_ge(VE.h, VE.n)
                        I(VE, v.tensor_tensor(gw3[0:2, MM0 + h, :], gw3[0:2, SC0 + h, :],
                                              gw3[0:2, ML0 + h, :], OP.max))
                        v.wait_ge(VE.h, VE.n)
                        I(VE, v.tensor_tensor(gw3[0:2, UU0 + h, :], gw3[0:2, I50 + h, :],
                                              gw3[0:2, BG0 + h, :], OP.subtract))
                        I(VE, v.tensor_tensor(gw3[0:2, GG0 + h, :], gw3[0:2, MM0 + h, :],
                                              gw3[0:2, BG0 + h, :], OP.subtract))
                        v.wait_ge(VE.h, VE.n)
                        I(VE, v.reduce_max(colv[0:2, h, 3:4], gw3[0:2, UU0 + h, :], axis=AX,
                                           negate=True))
                        v.wait_ge(VE.h, VE.n)
                ss_rows = snap()

                @blk.scalar
                def _(s, ss_rows=ss_rows):
                    waits(s, ss_rows, ["VE"])
                    for h in range(2):
                        q0 = 6 * h
                        I(SE, s.activation(gq[0:2, q0 + EXPU, :], gw3[0:2, UU0 + h, :],
                                           AF.Exp, bias=colv[0:2, h, 3:4], scale=1.0))
                        I(SE, s.activation(gq[0:2, q0 + S2Q, :], gw3[0:2, GG0 + h, :],
                                           AF.Exp, bias=colv[0:2, h, 2:3], scale=-1.0))
                        I(SE, s.activation(gq[0:2, q0 + EMQ, :], gw3[0:2, MM0 + h, :],
                                           AF.Exp, bias=0.0, scale=-1.0))
                ss_e1 = snap()

                @blk.vector
                def _(v, ss_e1=ss_e1):
                    waits(v, ss_e1, ["SE"])
                    for h in range(2):
                        I(VE, v.tensor_scalar(nstat[0:2, 4 + h:5 + h], colv[0:2, h, 3:4],
                                              -1.0, None, OP.mult))
                        I(VE, v.tensor_scalar(nstat[0:2, 6 + h:7 + h], gw3[0:2, GG0 + h, 127:128],
                                              -1.0, None, OP.mult))
                        I(VE, v.tensor_scalar(gw3[0:2, LR0 + h, 0:1], gw3[0:2, GG0 + h, 127:128],
                                              colv[0:2, h, 1:2], colv[0:2, h, 0:1],
                                              OP.subtract, OP.add))
                        v.wait_ge(VE.h, VE.n)
                ss_sm = snap()

                @blk.scalar
                def _(s, ss_sm=ss_sm):
                    waits(s, ss_sm, ["VE"])
                    for h in range(2):
                        q0 = 6 * h
                        I(SE, s.activation(gq[0:2, q0 + S1Q, :], gw3[0:2, GG0 + h, :],
                                           AF.Exp, bias=nstat[0:2, 4 + h:5 + h], scale=-1.0))
                        I(SE, s.activation(gq[0:2, q0 + AAQ, :], gw3[0:2, UU0 + h, :],
                                           AF.Exp, bias=nstat[0:2, 6 + h:7 + h], scale=1.0))
                        I(SE, s.activation(gw3[0:2, LR0 + h, 0:1], gw3[0:2, LR0 + h, 0:1],
                                           AF.Exp, bias=0.0, scale=-1.0))
                        s.wait_ge(SE.h, SE.n)
                ss_e2 = snap()

                @blk.vector
                def _(v, ss_e2=ss_e2):
                    waits(v, ss_e2, ["SE"])
                    for h in range(2):
                        q0 = 6 * h
                        I(VE, v.tensor_scalar(gq[0:2, q0 + LAMQ, :], ones2[0:2, :],
                                              gw3[0:2, LR0 + h, 0:1], None, OP.mult))
                ss_lam = snap()

                te1 = TE.n
                ve1 = VE.n

                @blk.tensor
                def _(t, ss_lam=ss_lam, ve1=ve1):
                    waits(t, ss_lam, ["VE", "SE"])
                    for qi in range(12):
                        if qi >= 2:
                            t.wait_ge(VE.h, ve1 + qi - 1)
                        I(TE, t.transpose(P[qi % 2][0:128, 0:2], gq[0:2, qi, :],
                                          ident_fs[0:2, 0:2]))

                @blk.vector
                def _(v, te1=te1):
                    for qi in range(12):
                        v.wait_ge(TE.h, te1 + qi + 1)
                        I(VE, v.tensor_copy(col_s[:, qi, :], P[qi % 2][0:128, 0:2]))
                assert VE.n == ve1 + 12
                ss_col = snap()

                if debug and k == 0:
                    @blk.sync
                    def _(sy, ss_col=ss_col):
                        waits(sy, ss_col, ["VE"])
                        I(DB, sy.dma_start(out=dbg["dbg_col"][:, :, :], in_=col_s[:, :, :]), 16)
                        I(DB, sy.dma_start(out=dbg["dbg_gw3"][:, :, :], in_=gw3[0:2, :, :]), 16)
                        I(DB, sy.dma_start(out=dbg["dbg_gq"][:, :, :], in_=gq[0:2, :, :]), 16)
                    _ssdbg = snap()

                    @blk.vector
                    def _(v, _ssdbg=_ssdbg):
                        waits(v, _ssdbg, ["DB"])
                        I(VE, v.engine_nop())

                if k + 1 < NBLK:
                    emit_F1(k + 1)

                ss_mix = ss_col
                prevu = None
                for c in range(CPB):
                    for h in range(2):
                        q0 = 6 * h

                        @blk.tensor
                        def _(t, c=c, h=h, ssm=ss_mix, prevu=prevu):
                            if prevu is None:
                                waits(t, ssm, ["VE", "SE"])
                            else:
                                waits(t, prevu[0], ["VE"])
                                waits(t, prevu[1], ["VE"])
                            I(TE, t.matmul(P[2][0:128, 0:128], kb[:, h, c * 128:(c + 1) * 128],
                                           qb[:, h, c * 128:(c + 1) * 128],
                                           start=True, stop=True))
                            I(TE, t.transpose(PBt[0:128, 0:128],
                                              kb[:, h, c * 128:(c + 1) * 128], ident_fs[:, :]))
                        ss_m1 = snap()

                        @blk.vector
                        def _(v, c=c, q0=q0, ss_m1=ss_m1):
                            waits(v, ss_m1, ["TE"])
                            I(VE, v.scalar_tensor_tensor(pp_s[:, :], P[2][0:128, 0:128],
                                                         col_s[:, q0 + EXPU, c:c + 1],
                                                         masks[:, :], OP.mult, OP.mult))
                        ss_pp = snap()

                        @blk.vector
                        def _(v, c=c, q0=q0, ss_m1=ss_m1):
                            waits(v, ss_m1, ["TE"])
                            I(VE, v.tensor_scalar(ak_s[:, :], PBt[0:128, 0:128],
                                                  col_s[:, q0 + AAQ, c:c + 1], None, OP.mult))
                        ss_ak = snap()
                        prevu = (ss_pp, ss_ak)

                        @blk.tensor
                        def _(t, c=c, h=h, ss_pp=ss_pp, ss_ak=ss_ak):
                            waits(t, ss_pp, ["VE"])
                            waits(t, ss_ak, ["VE"])
                            I(TE, t.matmul(P[3][0:128, 0:257], pp_s[:, :], vb[:, c, h, :],
                                           start=True, stop=True))
                            I(TE, t.matmul(P[4][0:128, 0:257],
                                           qb[:, h, c * 128:(c + 1) * 128],
                                           state[:, h, :], start=True, stop=True))
                            I(TE, t.matmul(P[5][0:128, 0:257], ak_s[:, :], vb[:, c, h, :],
                                           start=True, stop=True))
                        ss_m2 = snap()

                        @blk.vector
                        def _(v, c=c, h=h, q0=q0, ss_m2=ss_m2):
                            waits(v, ss_m2, ["TE"])
                            I(VE, v.tensor_scalar(tmp1[:, :], P[3][0:128, 0:257],
                                                  col_s[:, q0 + S1Q, c:c + 1], None, OP.mult))
                            v.wait_ge(VE.h, VE.n)
                            I(VE, v.scalar_tensor_tensor(hq_s[:, :], P[4][0:128, 0:257],
                                                         col_s[:, q0 + S2Q, c:c + 1],
                                                         tmp1[:, :], OP.mult, OP.add))
                            I(VE, v.scalar_tensor_tensor(state[:, h, :], state[:, h, :],
                                                         col_s[:, q0 + LAMQ, c:c + 1],
                                                         P[5][0:128, 0:257], OP.mult, OP.add))
                            v.wait_ge(VE.h, VE.n)
                            I(VE, v.tensor_scalar(nstat[:, 0:1], hq_s[:, 256:257],
                                                  -1.0, None, OP.mult))
                            v.wait_ge(VE.h, VE.n)
                            I(VE, v.tensor_max(nstat[:, 0:1], nstat[:, 0:1],
                                               hq_s[:, 256:257]))
                            v.wait_ge(VE.h, VE.n)
                            I(VE, v.tensor_scalar(nstat[:, 0:1], nstat[:, 0:1],
                                                  col_s[:, q0 + EMQ, c:c + 1], EPSD,
                                                  OP.max, OP.add))
                            v.wait_ge(VE.h, VE.n)
                            I(VE, v.reciprocal(nstat[:, 1:2], nstat[:, 0:1]))
                            v.wait_ge(VE.h, VE.n)
                            I(VE, v.tensor_scalar(hNb[:, c, h, :], hq_s[:, 0:256],
                                                  nstat[:, 1:2], None, OP.mult))
                            v.wait_ge(VE.h, VE.n)
                        ss_m3 = snap()

                        if debug and k == 1 and c == 0 and h == 0:
                            @blk.sync
                            def _(sy, ss_m3=ss_m3):
                                waits(sy, ss_m3, ["VE"])
                                I(DB, sy.dma_start(out=dbg["dbg_hq"][:, :], in_=hq_s[:, :]), 16)
                            _ssdbg = snap()

                            @blk.vector
                            def _(v, _ssdbg=_ssdbg):
                                waits(v, _ssdbg, ["DB"])
                                I(VE, v.engine_nop())

                        ss_mix = snap()

                if k + 1 < NBLK:
                    emit_F2(k + 1, ss_col)

                @blk.vector
                def _(v, ss_mix=ss_mix):
                    waits(v, ss_mix, ["VE", "SE"])
                    hv = hNb[:, :, :, :].rearrange("p a b c -> p (a b c)")
                    I(VE, v.tensor_mul(nsqb[:, :], hv, hv))
                    v.wait_ge(VE.h, VE.n)
                    I(VE, v.reduce_sum(nst2[:, 0:4],
                                       nsqb[:, :].rearrange("p (a c) -> p a c", a=4), axis=AX))
                    I(VE, v.reduce_sum(nst2[:, 4:8],
                                       hNb[:, :, :, :].rearrange("p a b c -> p (a b) c"), axis=AX))
                    v.wait_ge(VE.h, VE.n)
                    I(VE, v.tensor_scalar(nst2[:, 8:12], nst2[:, 4:8], 1.0 / DV, None, OP.mult))
                    I(VE, v.tensor_scalar(nst2[:, 12:16], nst2[:, 0:4], 1.0 / DV, None, OP.mult))
                    v.wait_ge(VE.h, VE.n)
                    I(VE, v.tensor_mul(nst2[:, 0:4], nst2[:, 8:12], nst2[:, 8:12]))
                    v.wait_ge(VE.h, VE.n)
                    I(VE, v.tensor_sub(nst2[:, 0:4], nst2[:, 0:4], nst2[:, 12:16]))
                    v.wait_ge(VE.h, VE.n)
                ss_nb1 = snap()

                @blk.scalar
                def _(s, ss_nb1=ss_nb1):
                    waits(s, ss_nb1, ["VE"])
                    I(SE, s.activation(nst2[:, 0:4], nst2[:, 0:4], AF.Ln,
                                       bias=epsc[:, 1:2], scale=-1.0))
                    s.wait_ge(SE.h, SE.n)
                    I(SE, s.activation(nst2[:, 0:4], nst2[:, 0:4], AF.Exp,
                                       bias=0.0, scale=-0.5))
                    s.wait_ge(SE.h, SE.n)
                ss_nb2 = snap()

                @blk.vector
                def _(v, ss_nb2=ss_nb2):
                    waits(v, ss_nb2, ["SE"])
                    for c in range(CPB):
                        for h in range(2):
                            cl = c * 2 + h
                            I(VE, v.tensor_scalar(hNb[:, c, h, :], hNb[:, c, h, :],
                                                  nst2[:, 8 + cl:9 + cl], nst2[:, cl:cl + 1],
                                                  OP.subtract, OP.mult))
                    v.wait_ge(VE.h, VE.n)
                    for c in range(CPB):
                        hv2 = hNb[:, c, :, :].rearrange("p b c -> p (b c)")
                        I(VE, v.tensor_mul(hv2, hv2, gamr[:, :]))
                        v.wait_ge(VE.h, VE.n)
                        I(VE, v.tensor_mul(hv2, hv2,
                                           sgb[:, c, :, :].rearrange("p b c -> p (b c)")))
                        v.wait_ge(VE.h, VE.n)
                ss_nb3 = snap()

                te_tr = TE.n
                ve_tr = VE.n

                @blk.tensor
                def _(t, ss_nb3=ss_nb3, ve_tr=ve_tr):
                    waits(t, ss_nb3, ["VE"])
                    i = 0
                    for c in range(CPB):
                        for h in range(2):
                            for dvt in range(2):
                                if i >= 2:
                                    t.wait_ge(VE.h, ve_tr + i - 1)
                                I(TE, t.transpose(P[3 + i % 2][0:128, 0:128],
                                                  hNb[:, c, h, dvt * 128:(dvt + 1) * 128],
                                                  ident_fs[:, :]))
                                i += 1

                @blk.vector
                def _(v, te_tr=te_tr, ssb=ssb):
                    waits(v, ssb, ["EX"])
                    i = 0
                    for c in range(CPB):
                        for h in range(2):
                            for dvt in range(2):
                                v.wait_ge(TE.h, te_tr + i + 1)
                                I(VE, v.tensor_copy(exb[:, h * 2 + dvt, c * 128:(c + 1) * 128],
                                                    P[3 + i % 2][0:128, 0:128]))
                                i += 1
                ss_mix = snap()

                if debug and k == 0:
                    @blk.sync
                    def _(sy, ss_mix=ss_mix):
                        waits(sy, ss_mix, ["VE"])
                        I(DB, sy.dma_start(out=dbg["dbg_state"][:, :], in_=state[:, 0, :]), 16)
                    _ssdbg2 = snap()

                    @blk.vector
                    def _(v, _ssdbg2=_ssdbg2):
                        waits(v, _ssdbg2, ["DB"])
                        I(VE, v.engine_nop())

                @blk.sync
                def _(sy, T0=T0, ss_mix=ss_mix):
                    waits(sy, ss_mix, ["VE"])
                    for vc in range(4):
                        I(EX, sy.dma_start(out=hout_in.ap()[vc * 128:(vc + 1) * 128, T0:T0 + BLK],
                                           in_=exb[:, vc, :]), 16)
                prev = snap()

            ss_p1 = prev

        @blk.gpsimd
        def _(gp):
            gp.wait_ge(EX.h, ss_p1["EX"])
            I(CC, gp.collective_compute("AllGather", OP.bypass,
                                        replica_groups=[[0, 1, 2, 3], [4, 5, 6, 7]],
                                        ins=[hout_in.ap().opt()], outs=[hout_out.ap().opt()]))
        ss_hag = snap()

        x2t = [Et(nc.sbuf_tensor(f"x2t{i}", [128, D], F32)) for i in range(2)]
        outs2 = [Et(nc.sbuf_tensor(f"outs{i}", [128, 1024], F32)) for i in range(2)]
        xsc = Et(nc.sbuf_tensor("xsc", [128, D], F32))
        sts2 = Et(nc.sbuf_tensor("sts2", [128, 10], F32))

        with ExitStack() as p2a:
            E2 = p2a.enter_context
            woutb = E2(nc.sbuf_tensor("woutb", [128, NKD, D], BF16))
            houtmy = E2(nc.sbuf_tensor("houtmy", [128, NKD, TOK], BF16))

            @blk.sync
            def _(sy):
                waits(sy, ss_wfag, ["CC"])
                wflat = wffn_bf.ap()[WOUTOFF:WOUTOFF + 128 * NKD * D]
                wpk = wflat.rearrange("(p k w) -> p k w", p=128, k=NKD)
                for i in range(4):
                    I(LH, sy.dma_start(out=woutb[:, 4 * i:4 * (i + 1), :],
                                       in_=wpk[:, 4 * i:4 * (i + 1), :]), 16)
            ss_wout = snap()

            @blk.sync
            def _(sy, ss_hag=ss_hag, ss_wout=ss_wout):
                waits(sy, ss_hag, ["CC"])
                waits(sy, ss_wout, ["LH"])
                pid = nc.partition_id(engines=[SP_ENG])
                for vc in range(NKD):
                    I(LH, sy.dma_start(out=houtmy[:, vc, :],
                                       in_=hout_out.ap()[vc * 128:(vc + 1) * 128,
                                                         bass.ts(pid % 4, TOK)]), 16)
            ss_hmy = snap()

            @blk.vector
            def _(v, ss_wout=ss_wout, ss_p1=ss_p1):
                waits(v, ss_wout, ["LH"])
                waits(v, ss_p1, ["VE"])
                for kk in range(NKD):
                    I(VE, v.tensor_mul(woutb[:, kk, :], woutb[:, kk, :], garep[:, :]))
            ss_wfold = snap()

            prev_w = ss_wfold
            x2st_ck = {}
            for tt in range(NTT):
                @blk.sync
                def _(sy, tt=tt, pw=prev_w):
                    if tt >= 2:
                        waits(sy, pw, ["VE"])
                    I(LT[tt % 2], sy.dma_start(out=x2t[tt % 2][:, :],
                                               in_=xresd[tt * 128:(tt + 1) * 128, :]), 16)
                lt_ck_w = LT[tt % 2].n
                ss_xl = snap()

                @blk.tensor
                def _(t, tt=tt, pw=prev_w, ss_hmy=ss_hmy, ss_wfold=ss_wfold):
                    waits(t, pw, ["VE"])
                    if tt == 0:
                        waits(t, ss_hmy, ["LH"])
                        waits(t, ss_wfold, ["VE"])
                    for dc in range(4):
                        for vc in range(NKD):
                            mi = t.matmul(P[dc][:, :], houtmy[:, vc, tt * 128:(tt + 1) * 128],
                                          woutb[:, vc, dc * 512:(dc + 1) * 512],
                                          start=(vc == 0), stop=(vc == NKD - 1))
                            if vc == NKD - 1:
                                I(TE, mi)
                ss_wmm = snap()

                @blk.vector
                def _(v, tt=tt, ss_wmm=ss_wmm, ss_xl=ss_xl):
                    waits(v, ss_wmm, ["TE"])
                    v.wait_ge(LT[tt % 2].h, ss_xl["LT%d" % (tt % 2)])
                    if tt >= 2:
                        v.wait_ge(SX[tt % 2].h, x2st_ck[tt - 2])
                    for dc in range(4):
                        I(VE, v.tensor_add(x2t[tt % 2][:, dc * 512:(dc + 1) * 512],
                                           x2t[tt % 2][:, dc * 512:(dc + 1) * 512],
                                           P[dc][:, :]))
                        v.wait_ge(VE.h, VE.n)
                ss_we = snap()

                @blk.sync
                def _(sy, tt=tt, ss_we=ss_we):
                    waits(sy, ss_we, ["VE"])
                    I(SX[tt % 2], sy.dma_start(out=x2d.ap()[tt * 128:(tt + 1) * 128, :],
                                               in_=x2t[tt % 2][:, :]), 16)
                x2st_ck[tt] = SX[tt % 2].n
                prev_w = snap()
            ss_x2 = prev_w

            if debug:
                @blk.sync
                def _(sy, ss_x2=ss_x2):
                    waits(sy, ss_x2, ["VE"])
                    I(DB, sy.dma_start(out=dbg["dbg_x2"][:, :], in_=x2t[0][:, :]), 16)
                _ssdbg = snap()

                @blk.vector
                def _(v, _ssdbg=_ssdbg):
                    waits(v, _ssdbg, ["DB"])
                    I(VE, v.engine_nop())

        with ExitStack() as p2b:
            E3 = p2b.enter_context
            h2 = E3(nc.sbuf_tensor("h2", [128, NKD, THALF], BF16))
            hmc = E3(nc.sbuf_tensor("hmc", [128, NKH, THALF], BF16))
            w1b = [E3(nc.sbuf_tensor(f"w1b{i}", [128, NKD, 128], BF16)) for i in range(2)]
            w2b = [E3(nc.sbuf_tensor(f"w2b{i}", [128, 2048], BF16)) for i in range(2)]
            combs = E3(nc.sbuf_tensor("combs", [128, TTH, E_], F32))
            combrow = E3(nc.sbuf_tensor("combrow", [1, E_, THALF], BF16))
            comb8 = E3(nc.sbuf_tensor("comb8", [E_, THALF], BF16))
            combrep = E3(nc.sbuf_tensor("combrep", [128, THALF], BF16))
            P7 = E3(nc.psum_tensor("P7", [128, 512], F32))
            PF = P + [P7]

            prev_half = snap()
            for hh in range(2):
                prev_ln = prev_half
                for tl in range(TTH):
                    tt = hh * TTH + tl

                    @blk.sync
                    def _(sy, tt=tt, tl=tl, pl=prev_ln):
                        waits(sy, pl, ["VE", "TE", "SE", "SX0", "SX1", "SO0", "SO1"])
                        I(LT[tl % 2], sy.dma_start(out=x2t[tl % 2][:, :],
                                                   in_=x2d.ap()[tt * 128:(tt + 1) * 128, :]), 16)
                    ss_xld = snap()

                    @blk.scalar
                    def _(s, tl=tl, pl=prev_ln, ss_xld=ss_xld):
                        waits(s, pl, ["VE", "TE", "SE", "GC", "CC"])
                        s.wait_ge(LT[tl % 2].h, ss_xld["LT%d" % (tl % 2)])
                        I(SE, s.activation(xsc[:, :], x2t[tl % 2][:, :], AF.Square,
                                           accum_out=sts2[:, 0:1]))
                        s.wait_ge(SE.h, SE.n)
                        I(SE, s.activation(xsc[:, :], x2t[tl % 2][:, :], AF.Copy,
                                           accum_out=sts2[:, 1:2]))
                        s.wait_ge(SE.h, SE.n)
                    ss_a = snap()

                    @blk.vector
                    def _(v, ss_a=ss_a):
                        waits(v, ss_a, ["SE"])
                        I(VE, v.tensor_scalar(sts2[:, 4:5], sts2[:, 1:2], 1.0 / D, None, OP.mult))
                        I(VE, v.tensor_scalar(sts2[:, 8:9], sts2[:, 0:1], 1.0 / D, None, OP.mult))
                        v.wait_ge(VE.h, VE.n)
                        I(VE, v.scalar_tensor_tensor(sts2[:, 5:6], sts2[:, 4:5], sts2[:, 4:5],
                                                     sts2[:, 8:9], OP.mult, OP.subtract))
                        v.wait_ge(VE.h, VE.n)
                    ss_b = snap()

                    @blk.scalar
                    def _(s, ss_b=ss_b):
                        waits(s, ss_b, ["VE"])
                        I(SE, s.activation(sts2[:, 5:6], sts2[:, 5:6], AF.Ln,
                                           bias=epsc[:, 0:1], scale=-1.0))
                        s.wait_ge(SE.h, SE.n)
                        I(SE, s.activation(sts2[:, 5:6], sts2[:, 5:6], AF.Exp,
                                           bias=0.0, scale=-0.5))
                        s.wait_ge(SE.h, SE.n)
                    ss_c = snap()

                    @blk.vector
                    def _(v, tl=tl, ss_c=ss_c):
                        waits(v, ss_c, ["SE"])
                        I(VE, v.tensor_scalar(xsc[:, :], x2t[tl % 2][:, :], sts2[:, 4:5],
                                              sts2[:, 5:6], OP.subtract, OP.mult))
                    ss_d = snap()

                    te2 = TE.n
                    se2 = SE.n

                    @blk.tensor
                    def _(t, ss_d=ss_d, se2=se2):
                        waits(t, ss_d, ["VE", "SE"])
                        for kk in range(NKD):
                            if kk >= 2:
                                t.wait_ge(SE.h, se2 + kk - 1)
                            I(TE, t.transpose(P[kk % 2][0:128, 0:128],
                                              xsc[:, kk * 128:(kk + 1) * 128], ident_fs[:, :]))

                    @blk.scalar
                    def _(s, tl=tl, te2=te2):
                        for kk in range(NKD):
                            s.wait_ge(TE.h, te2 + kk + 1)
                            I(SE, s.activation(h2[:, kk, tl * 128:(tl + 1) * 128],
                                               P[kk % 2][0:128, 0:128], AF.Identity,
                                               bias=modcol[:, 2, kk:kk + 1],
                                               scale=modcol[:, 3, kk:kk + 1]))
                    prev_ln = snap()
                ss_h2 = prev_ln

                prev_cb = ss_h2
                for tl in range(TTH):
                    @blk.tensor
                    def _(t, tl=tl, pc=prev_cb, ss_h2=ss_h2):
                        waits(t, ss_h2, ["SE"])
                        waits(t, pc, ["VE"])
                        for kk in range(NKD):
                            t.matmul(P[2][0:128, 0:E_], h2[:, kk, tl * 128:(tl + 1) * 128],
                                     gw_s[:, kk, :], start=(kk == 0), stop=False)
                        I(TE, t.matmul(P[2][0:128, 0:E_], onesf[0:1, 0:128], gbs[0:1, 0:E_],
                                       start=False, stop=True))
                    ss_sc = snap()

                    @blk.vector
                    def _(v, tl=tl, ss_sc=ss_sc):
                        waits(v, ss_sc, ["TE"])
                        I(VE, v.tensor_copy(combs[:, tl, :], P[2][0:128, 0:E_]))
                        v.wait_ge(VE.h, VE.n)
                        I(VE, v.reduce_max(sts2[:, 0:1], combs[:, tl, :], axis=AX, negate=True))
                        v.wait_ge(VE.h, VE.n)
                        I(VE, v.tensor_scalar(sts2[:, 1:2], sts2[:, 0:1], -1.0, None, OP.mult))
                        v.wait_ge(VE.h, VE.n)
                        I(VE, v.tensor_scalar(xsc[:, 0:E_], combs[:, tl, :], sts2[:, 1:2],
                                              None, OP.is_ge))
                        v.wait_ge(VE.h, VE.n)
                        I(VE, v.reduce_sum(sts2[:, 2:3], xsc[:, 0:E_], axis=AX))
                        I(VE, v.scalar_tensor_tensor(xsc[:, E_:2 * E_], xsc[:, 0:E_], NEG,
                                                     combs[:, tl, :], OP.mult, OP.add))
                        v.wait_ge(VE.h, VE.n)
                        I(VE, v.reduce_max(sts2[:, 3:4], xsc[:, E_:2 * E_], axis=AX))
                        I(VE, v.tensor_scalar(sts2[:, 2:3], sts2[:, 2:3], 2.0, None, OP.is_ge))
                        v.wait_ge(VE.h, VE.n)
                        I(VE, v.tensor_tensor(sts2[:, 4:5], sts2[:, 1:2], sts2[:, 3:4],
                                              OP.subtract))
                        v.wait_ge(VE.h, VE.n)
                        I(VE, v.tensor_mul(sts2[:, 4:5], sts2[:, 4:5], sts2[:, 2:3]))
                        v.wait_ge(VE.h, VE.n)
                        I(VE, v.tensor_add(sts2[:, 4:5], sts2[:, 4:5], sts2[:, 3:4]))
                        v.wait_ge(VE.h, VE.n)
                        I(VE, v.tensor_scalar(xsc[:, 0:E_], combs[:, tl, :], sts2[:, 4:5],
                                              None, OP.is_ge))
                        v.wait_ge(VE.h, VE.n)
                    ss_c1 = snap()

                    @blk.scalar
                    def _(s, tl=tl, ss_c1=ss_c1):
                        waits(s, ss_c1, ["VE"])
                        I(SE, s.activation(combs[:, tl, :], combs[:, tl, :], AF.Exp,
                                           bias=sts2[:, 0:1], scale=1.0))
                        s.wait_ge(SE.h, SE.n)
                    ss_c2 = snap()

                    @blk.vector
                    def _(v, tl=tl, ss_c2=ss_c2):
                        waits(v, ss_c2, ["SE"])
                        I(VE, v.tensor_mul(combs[:, tl, :], combs[:, tl, :], xsc[:, 0:E_]))
                        v.wait_ge(VE.h, VE.n)
                        I(VE, v.reduce_sum(sts2[:, 5:6], combs[:, tl, :], axis=AX))
                        v.wait_ge(VE.h, VE.n)
                        I(VE, v.reciprocal(sts2[:, 6:7], sts2[:, 5:6]))
                        v.wait_ge(VE.h, VE.n)
                        I(VE, v.tensor_scalar(combs[:, tl, :], combs[:, tl, :], sts2[:, 6:7],
                                              None, OP.mult))
                    ss_c3 = snap()

                    if debug and hh == 0 and tl == 0:
                        @blk.sync
                        def _(sy, ss_c3=ss_c3):
                            waits(sy, ss_c3, ["VE"])
                            I(DB, sy.dma_start(out=dbg["dbg_comb"][:, :], in_=combs[:, 0, :]), 16)
                        _ssdbg = snap()

                        @blk.vector
                        def _(v, _ssdbg=_ssdbg):
                            waits(v, _ssdbg, ["DB"])
                            I(VE, v.engine_nop())

                    for grp in range(2):
                        @blk.tensor
                        def _(t, tl=tl, grp=grp, sw=snap()):
                            waits(t, sw, ["VE"])
                            for ei in range(4):
                                e = grp * 4 + ei
                                mi = t.transpose(P[3][0:1, ei * 128:(ei + 1) * 128],
                                                 combs[:, tl, e:e + 1], ident_fs[:, :])
                                if ei == 3:
                                    I(TE, mi)
                            if grp == 0:
                                I(TE, t.transpose(P[4][0:E_, 0:128], combs[:, tl, :],
                                                  ident_fs[:, :]))
                        ss_ct = snap()

                        @blk.vector
                        def _(v, tl=tl, grp=grp, ss_ct=ss_ct):
                            waits(v, ss_ct, ["TE"])
                            for ei in range(4):
                                e = grp * 4 + ei
                                I(VE, v.tensor_copy(combrow[0:1, e, tl * 128:(tl + 1) * 128],
                                                    P[3][0:1, ei * 128:(ei + 1) * 128]))
                            if grp == 0:
                                I(VE, v.tensor_copy(comb8[:, tl * 128:(tl + 1) * 128],
                                                    P[4][0:E_, 0:128]))
                    prev_cb = snap()
                ss_comb = prev_cb

                prev_hm = ss_comb
                mkL = []
                sil_ck = []
                sil_se_ck = []
                for e in range(E_):
                    @blk.tensor
                    def _(t, e=e, pm=prev_hm):
                        waits(t, pm, ["VE", "SE"])
                        I(TE, t.matmul(P[6][:, 0:THALF], onesb[0:1, 0:128],
                                       combrow[0:1, e, :], start=True, stop=True))
                    ss_cr = snap()

                    @blk.vector
                    def _(v, e=e, ss_cr=ss_cr):
                        waits(v, ss_cr, ["TE"])
                        if e > 0:
                            v.wait_ge(TE.h, mkL[-1])
                        I(VE, v.tensor_copy(combrep[:, :], P[6][:, 0:THALF]))
                    ss_cre = snap()

                    for ht in range(NHT):
                        L = e * NHT + ht
                        PH = [P[5], P7][L % 2]

                        @blk.sync
                        def _(sy, e=e, ht=ht, L=L, ss_comb=ss_comb):
                            if L < 2:
                                waits(sy, ss_comb, ["TE", "VE"])
                            else:
                                sy.wait_ge(TE.h, mkL[L - 2])
                            off = L * W1BLK
                            I(W1S[L % 2], sy.dma_start(out=w1b[L % 2][:, :, :],
                                              in_=wffn_bf.ap()[off:off + W1BLK]
                                              .rearrange("(p k h) -> p k h", p=128, k=NKD)), 16)
                        ss_w1 = snap()

                        @blk.tensor
                        def _(t, ht=ht, L=L, PH=PH, ss_w1=ss_w1, ss_cre=ss_cre):
                            t.wait_ge(W1S[L % 2].h, ss_w1["W1%d" % (L % 2)])
                            if ht == 0:
                                waits(t, ss_cre, ["VE"])
                            if L >= 2:
                                t.wait_ge(SE.h, sil_se_ck[L - 2])
                            for kk in range(NKD):
                                mi = t.matmul(PH[:, 0:THALF], w1b[L % 2][:, kk, :],
                                              h2[:, kk, :], start=(kk == 0),
                                              stop=(kk == NKD - 1))
                                if kk == NKD - 1:
                                    I(TE, mi)
                        mkL.append(TE.n)
                        ss_h1 = snap()

                        @blk.scalar
                        def _(s, e=e, ht=ht, L=L, PH=PH, ss_h1=ss_h1):
                            waits(s, ss_h1, ["TE"])
                            if L >= 1:
                                s.wait_ge(VE.h, sil_ck[L - 1])
                            I(SE, s.activation(xsc[:, 0:THALF], PH[:, 0:THALF], AF.Silu,
                                               bias=b1col[:, e, ht:ht + 1], scale=1.0))
                        sil_se_ck.append(SE.n)
                        ss_h2e = snap()

                        @blk.vector
                        def _(v, e=e, ht=ht, ss_h2e=ss_h2e):
                            waits(v, ss_h2e, ["SE"])
                            I(VE, v.tensor_mul(hmc[:, e * NHT + ht, :], xsc[:, 0:THALF],
                                               combrep[:, :]))
                        sil_ck.append(VE.n)
                        prev_hm = snap()
                ss_hmid = prev_hm

                for dp in range(2):
                    swd = snap()
                    te3 = TE.n

                    w2_base = {0: W20.n, 1: W21.n}

                    @blk.sync
                    def _(sy, dp=dp, swd=swd, te3=te3):
                        waits(sy, swd, ["VE", "TE"])
                        for kk in range(NKH):
                            if kk >= 2:
                                sy.wait_ge(TE.h, te3 + kk - 1)
                            off = W1TOT + kk * W2BLK
                            I(W2S[kk % 2], sy.dma_start(out=w2b[kk % 2][:, :],
                                              in_=wffn_bf.ap()[off:off + W2BLK]
                                              .rearrange("(p w) -> p w", p=128)), 16)

                    @blk.tensor
                    def _(t, dp=dp, w2_base=w2_base, swd=swd):
                        waits(t, swd, ["VE"])
                        for tl in range(TTH):
                            for dc in range(2):
                                t.matmul(PF[tl * 2 + dc][:, :], comb8[:, tl * 128:(tl + 1) * 128],
                                         b2p[:, dp * 1024 + dc * 512:dp * 1024 + (dc + 1) * 512],
                                         start=True, stop=False)
                        for kk in range(NKH):
                            t.wait_ge(W2S[kk % 2].h, w2_base[kk % 2] + 16 * (kk // 2 + 1))
                            for tl in range(TTH):
                                for dc in range(2):
                                    mi = t.matmul(PF[tl * 2 + dc][:, :],
                                                  hmc[:, kk, tl * 128:(tl + 1) * 128],
                                                  w2b[kk % 2][:, dp * 1024 + dc * 512:
                                                              dp * 1024 + (dc + 1) * 512],
                                                  start=False, stop=(kk == NKH - 1))
                                    if tl == TTH - 1 and dc == 1:
                                        I(TE, mi)
                    ss_y2 = snap()

                    st_ck = {}
                    for tl in range(TTH):
                        @blk.sync
                        def _(sy, dp=dp, hh=hh, tl=tl, ss_y2=ss_y2, st_ck=dict(st_ck)):
                            waits(sy, ss_y2, ["TE"])
                            if tl >= 2:
                                sy.wait_ge(VE.h, st_ck[(tl - 2, "ve")])
                            tt = hh * TTH + tl
                            I(LT[tl % 2], sy.dma_start(out=x2t[tl % 2][:, 0:1024],
                                               in_=x2d.ap()[tt * 128:(tt + 1) * 128,
                                                            dp * 1024:(dp + 1) * 1024]), 16)
                        ss_xh2 = snap()

                        @blk.vector
                        def _(v, dp=dp, tl=tl, ss_y2=ss_y2, ss_xh2=ss_xh2,
                              st_ck=dict(st_ck)):
                            waits(v, ss_y2, ["TE", "SO0", "SO1"])
                            v.wait_ge(LT[tl % 2].h, ss_xh2["LT%d" % (tl % 2)])
                            if tl >= 2:
                                v.wait_ge(SO[tl % 2].h, st_ck[(tl - 2, "st")])
                            ob = outs2[tl % 2]
                            for dc in range(2):
                                I(VE, v.tensor_mul(ob[:, dc * 512:(dc + 1) * 512],
                                                   PF[tl * 2 + dc][:, :],
                                                   gbrep[:, dp * 1024 + dc * 512:
                                                          dp * 1024 + (dc + 1) * 512]))
                                v.wait_ge(VE.h, VE.n)
                                I(VE, v.tensor_add(ob[:, dc * 512:(dc + 1) * 512],
                                                   ob[:, dc * 512:(dc + 1) * 512],
                                                   x2t[tl % 2][:, dc * 512:(dc + 1) * 512]))
                                v.wait_ge(VE.h, VE.n)
                        ss_ev = snap()
                        st_ck[(tl, "ve")] = VE.n

                        @blk.sync
                        def _(sy, dp=dp, hh=hh, tl=tl, ss_ev=ss_ev):
                            waits(sy, ss_ev, ["VE"])
                            tt = hh * TTH + tl
                            I(SO[tl % 2], sy.dma_start(out=out[tt * 128:(tt + 1) * 128,
                                                       dp * 1024:(dp + 1) * 1024],
                                               in_=outs2[tl % 2][:, :]), 16)
                        st_ck[(tl, "st")] = SO[tl % 2].n
                    prev_half = snap()

        @blk.sync
        def _(sy):
            for sm in [SO0, SO1, SX0, SX1, EX, CC, DB]:
                if sm.n > 0:
                    sy.wait_ge(sm.h, sm.n)

    return nc


def _prep(inputs, cfg):
    S, TOK = cfg.S, cfg.TOK
    x = np.asarray(inputs["L0_attn_x"], np.float32)
    Wq = np.asarray(inputs["L0_attn_Wq"], np.float32)
    Wk = np.asarray(inputs["L0_attn_Wk"], np.float32)
    Wv = np.asarray(inputs["L0_attn_Wv"], np.float32)
    Wog = np.asarray(inputs["L0_attn_Wog"], np.float32)
    Wig = np.asarray(inputs["L0_attn_Wig"], np.float32)
    big = np.asarray(inputs["L0_attn_big"], np.float32)
    Wfg = np.asarray(inputs["L0_attn_Wfg"], np.float32)
    bfg = np.asarray(inputs["L0_attn_bfg"], np.float32)
    gamma = np.asarray(inputs["L0_attn_gamma"], np.float32)
    Wout = np.asarray(inputs["L0_attn_Wout"], np.float32)
    gate_w = np.asarray(inputs["L0_ffn_gate_w"], np.float32)
    gate_b = np.asarray(inputs["L0_ffn_gate_b"], np.float32)
    bias = np.asarray(inputs["L0_ffn_bias"], np.float32)
    w1 = np.asarray(inputs["L0_ffn_w1"], np.float32)
    b1 = np.asarray(inputs["L0_ffn_b1"], np.float32)
    w2 = np.asarray(inputs["L0_ffn_w2"], np.float32)
    b2 = np.asarray(inputs["L0_ffn_b2"], np.float32)
    cc = np.asarray(inputs["L0_c"], np.float32)
    wmod = np.asarray(inputs["L0_wmod"], np.float32)

    C = np.ascontiguousarray

    def PK(a):
        a = np.asarray(a)
        w = a.shape[1]
        return C(a.reshape(16, 128, w).transpose(1, 0, 2).reshape(128, 16 * w))
    WqT, WkT, WvT, WogT = C(Wq.T), C(Wk.T), C(Wv.T), C(Wog.T)
    WigT, WfgT = C(Wig.T), C(Wfg.T)
    WoutT = C(Wout.T)
    gwT = C(gate_w.T)
    w1T = w1.transpose(0, 2, 1)
    w1flat = C(w1T.reshape(E_, NKD, 128, NHT, 128).transpose(0, 3, 2, 1, 4)).reshape(-1)
    w2flat = C(w2.transpose(0, 2, 1)).reshape(-1)
    wffn_full = np.concatenate([w1flat, w2flat, PK(WoutT).reshape(-1)]).reshape(8, -1)

    in_maps = []
    for c in range(8):
        b, g = c // 4, c % 4
        hg = 2 * g
        m = {
            "xb": C(x[b, :S]),
            "xresd": C(x[b, g * TOK:(g + 1) * TOK]),
            "wqT": PK(WqT[:, hg * 128:(hg + 2) * 128]),
            "wkT": PK(WkT[:, hg * 128:(hg + 2) * 128]),
            "wvT": PK(WvT[:, hg * 256:(hg + 2) * 256]),
            "wogT": PK(WogT[:, hg * 256:(hg + 2) * 256]),
            "wifT": PK(np.stack([WigT[:, hg], WfgT[:, hg],
                                WigT[:, hg + 1], WfgT[:, hg + 1]], axis=1)),
            "bifr": np.array([[big[hg], bfg[hg], big[hg + 1], bfg[hg + 1]]], np.float32),
            "gammag": C(gamma[hg:hg + 2].reshape(1, 512)),
            "wmod_sl": C(np.concatenate([wmod[:, (g + 4 * i) * 512:(g + 4 * i + 1) * 512]
                                         for i in range(6)], axis=1)),
            "gwTd": gwT,
            "gb2d": gate_b.reshape(1, E_),
            "gbiasd": bias.reshape(1, E_),
            "b1e": b1,
            "b2e": b2,
            "cvecd": cc[b].reshape(1, D),
            "wffn": wffn_full[c],
        }
        in_maps.append(m)
    return in_maps


def kernel(**inputs):
    cfg = CFG(int(np.asarray(inputs["L0_attn_x"]).shape[1]))
    in_maps = _prep(inputs, cfg)
    nc = build_program(cfg, debug=False)
    res = run_bass_kernel_spmd(nc, in_maps, core_ids=list(range(8)))
    B_, S_ = np.asarray(inputs["L0_attn_x"]).shape[0], cfg.S
    outa = np.empty((B_, S_, D), np.float32)
    for c in range(8):
        b, g = c // 4, c % 4
        outa[b, g * cfg.TOK:(g + 1) * cfg.TOK] = res.results[c]["out"]
    return outa


# version 23
# speedup vs baseline: 1.0115x; 1.0115x over previous
import numpy as np
import ml_dtypes
from contextlib import ExitStack

import concourse.bass as bass
import concourse.mybir as mybir
from concourse.bass_utils import run_bass_kernel_spmd

F32 = mybir.dt.float32
BF16 = mybir.dt.bfloat16
AF = mybir.ActivationFunctionType
OP = mybir.AluOpType
AX = mybir.AxisListType.X
SP_ENG = mybir.EngineType.SP

D = 2048
DQ, DV = 128, 256
E_, H_ = 8, 1408
NHT = H_ // 128
NKD = D // 128
NKH = E_ * NHT
QKSCALE = float(DQ) ** -0.5
LNSC = float(np.log(QKSCALE))
EPS1, EPS2, EPSD = 1e-5, 1e-6, 1e-6
NEG = -1e30

W1BLK = NKD * 128 * 128
W2BLK = 128 * D
W1TOT = E_ * NHT * W1BLK
W2TOT = E_ * NHT * W2BLK
WOUTOFF = W1TOT + W2TOT
WFTOT = WOUTOFF + 128 * 16 * 2048
WFSL = WFTOT // 8

(TI0, TF0, TI1, TF1, FL0, FL1, I50, I51, BL0, BL1, ML0, ML1,
 BG0, BG1, SC0, SC1, UU0, UU1, GG0, GG1, MM0, MM1, LR0, LR1) = range(24)
NSLOT = 24
EXPU, S1Q, S2Q, AAQ, LAMQ, EMQ = range(6)


class CFG:
    def __init__(self, S):
        self.S = S
        self.TOK = S // 4
        self.BLK = 256
        self.NBLK = S // self.BLK
        self.CPB = 2
        self.NTT = self.TOK // 128
        self.TTH = self.NTT // 2
        self.THALF = self.TOK // 2


def build_program(cfg, debug=False):
    S, TOK, BLK, NBLK, CPB, NTT, TTH, THALF = (cfg.S, cfg.TOK, cfg.BLK, cfg.NBLK,
                                               cfg.CPB, cfg.NTT, cfg.TTH, cfg.THALF)
    nc = bass.Bass("TRN2", target_bir_lowering=False)

    def din(name, shape, dt=F32):
        return nc.dram_tensor(name, shape, dt, kind="ExternalInput").ap()

    xb = din("xb", [S, D])
    xresd = din("xresd", [TOK, D])
    wqT = din("wqT", [128, NKD * 256])
    wkT = din("wkT", [128, NKD * 256])
    wvT = din("wvT", [128, NKD * 512])
    wogT = din("wogT", [128, NKD * 512])
    wifT = din("wifT", [128, NKD * 4])
    bifr = din("bifr", [1, 4])
    gammag = din("gammag", [1, 512])
    wmod_sl = din("wmod_sl", [D, 6 * 512])
    gwTd = din("gwTd", [D, E_])
    gb2d = din("gb2d", [1, E_])
    gbiasd = din("gbiasd", [1, E_])
    b1e = din("b1e", [E_, H_])
    b2e = din("b2e", [E_, D])
    cvecd = din("cvecd", [1, D])
    wffn = din("wffn", [WFSL])

    out = nc.dram_tensor("out", [TOK, D], F32, kind="ExternalOutput").ap()
    dbg = {}
    if debug:
        for nm, shp, dt in [("dbg_xhat", [128, D], F32),
                            ("dbg_hq", [128, 257], F32),
                            ("dbg_x2", [128, D], F32),
                            ("dbg_col", [128, 12, 2], F32),
                            ("dbg_gw3", [2, NSLOT, 128], F32),
                            ("dbg_state", [128, 257], F32),
                            ("dbg_gq", [2, 12, 128], F32),
                            ("dbg_comb", [128, E_], F32)]:
            dbg[nm] = nc.dram_tensor(nm, shp, dt, kind="ExternalOutput").ap()

    mod_in = nc.dram_tensor("mod_in", [1, 6 * 512], F32, kind="Internal")
    mod_out = nc.dram_tensor("mod_out", [4, 6 * 512], F32, kind="Internal")
    hout_in = nc.dram_tensor("hout_in", [512, S], BF16, kind="Internal")
    hout_out = nc.dram_tensor("hout_out", [2048, S], BF16, kind="Internal")
    x2d = nc.dram_tensor("x2d", [cfg.TOK, D], F32, kind="Internal")
    wffn_my = nc.dram_tensor("wffn_my", [WFSL], BF16, kind="Internal")
    wffn_bf = nc.dram_tensor("wffn_bf", [WFTOT], BF16, kind="Internal")

    ident_f = nc.inline_tensor(np.eye(128, dtype=np.float32), "ident_f")
    ident_b = nc.inline_tensor(np.eye(128, dtype=np.float32).astype(ml_dtypes.bfloat16), "ident_b")
    masksc_t = nc.inline_tensor(np.triu(np.full((128, 128), QKSCALE, np.float32)), "masksc")

    with ExitStack() as ctx:
        Et = ctx.enter_context

        class Sem:
            def __init__(self, name):
                self.h = Et(nc.semaphore(name))
                self.n = 0

        _sem_names = ["LC", "LCP", "LX0", "LX1", "EX", "LH", "LT0", "LT1",
                      "SX0", "SX1", "SO0", "SO1", "GC", "GM0", "GM1",
                      "W10", "W11", "W20", "W21", "TE", "VE", "SE", "CC", "DB"]
        SEMS = {k: Sem(k) for k in _sem_names}
        (LC, LCP, LX0, LX1, EX, LH, LT0, LT1, SX0, SX1, SO0, SO1, GC, GM0,
         GM1, W10, W11, W20, W21, TE, VE, SE, CC, DB) = (SEMS[k] for k in _sem_names)
        LX = [LX0, LX1]
        LT = [LT0, LT1]
        SX = [SX0, SX1]
        SO = [SO0, SO1]
        GM = [GM0, GM1]
        W1S = [W10, W11]
        W2S = [W20, W21]

        def snap():
            return {k: s.n for k, s in SEMS.items()}

        def waits(eng, ss, only):
            for k in only:
                if ss[k] > 0:
                    eng.wait_ge(SEMS[k].h, ss[k])

        def I(sem, instr, k=1):
            instr.then_inc(sem.h, k)
            sem.n += k
            return instr

        onesf = Et(nc.sbuf_tensor("onesf", [1, 512], F32))
        onesb = Et(nc.sbuf_tensor("onesb", [1, 128], BF16))
        ones2 = Et(nc.sbuf_tensor("ones2", [2, 128], F32))
        epsc = Et(nc.sbuf_tensor("epsc", [128, 2], F32))
        ident_fs = Et(nc.sbuf_tensor("ident_fs", [128, 128], F32))
        ident_bs = Et(nc.sbuf_tensor("ident_bs", [128, 128], BF16))
        modcol = Et(nc.sbuf_tensor("modcol", [128, 4, NKD], F32))
        garep = Et(nc.sbuf_tensor("garep", [128, D], F32))
        gbrep = Et(nc.sbuf_tensor("gbrep", [128, D], F32))
        b1col = Et(nc.sbuf_tensor("b1col", [128, E_, NHT], F32))
        b2p = Et(nc.sbuf_tensor("b2p", [E_, D], BF16))
        gbs = Et(nc.sbuf_tensor("gbs", [1, 2 * E_], F32))
        gw_s = Et(nc.sbuf_tensor("gw_s", [128, NKD, E_], BF16))
        garow = Et(nc.sbuf_tensor("garow", [1, 512], F32))
        gascr = Et(nc.sbuf_tensor("gascr", [1, 512], F32))
        garo2 = Et(nc.sbuf_tensor("garo2", [1, 512], F32))
        gamrow = Et(nc.sbuf_tensor("gamrow", [1, 512], F32))
        rowcb = Et(nc.sbuf_tensor("rowcb", [1, 4], BF16))

        P = [Et(nc.psum_tensor(f"P{i}", [128, 512], F32)) for i in range(7)]

        blk = Et(nc.Block())

        @blk.vector
        def _(v):
            I(VE, v.memset(onesf[:, :], 1.0))
            I(VE, v.memset(onesb[:, :], 1.0))
            I(VE, v.memset(ones2[:, :], 1.0))
            I(VE, v.memset(epsc[:, 0:1], EPS1))
            I(VE, v.memset(epsc[:, 1:2], EPS2))

        rowc = Et(nc.sbuf_tensor("rowc", [1, 4], F32))

        @blk.sync
        def _(sy):
            I(LC, sy.dma_start(out=ident_fs[:, :], in_=ident_f.ap()), 16)
            I(LC, sy.dma_start(out=ident_bs[:, :], in_=ident_b.ap()), 16)
            I(LC, sy.dma_start(out=gbs[0:1, 0:E_], in_=gb2d[:, :]), 16)
            I(LC, sy.dma_start(out=gbs[0:1, E_:2 * E_], in_=gbiasd[:, :]), 16)
            I(LC, sy.dma_start(out=gamrow[0:1, :], in_=gammag[:, :]), 16)
            I(LC, sy.dma_start(out=rowc[:, :], in_=bifr[:, :]), 16)

        cvcol = Et(nc.sbuf_tensor("cvcol", [128, NKD], F32))

        @blk.sync
        def _(sy):
            I(LC, sy.dma_start(out=cvcol[:, :],
                               in_=cvecd[0, :].rearrange("(k p) -> p k", p=128),
                               allow_slow_non_contiguous=True), 16)

        @blk.sync
        def _(sy):
            for e in range(E_):
                I(LC, sy.dma_start(out=b1col[:, e, :],
                                   in_=b1e[e, :].rearrange("(h p) -> p h", p=128),
                                   allow_slow_non_contiguous=True), 16)
        ss_consts = snap()

        @blk.vector
        def _(v):
            waits(v, ss_consts, ["LC"])
            I(VE, v.tensor_copy(rowcb[:, :], rowc[:, :]))


        siluc = Et(nc.sbuf_tensor("siluc", [128, NKD], F32))
        modmy = Et(nc.sbuf_tensor("modmy", [1, 2, 512], F32))

        @blk.scalar
        def _(s):
            waits(s, ss_consts, ["LC"])
            I(SE, s.activation(siluc[:, :], cvcol[:, :], AF.Exp, bias=0.0, scale=-1.0))
        ss1 = snap()

        @blk.vector
        def _(v):
            waits(v, ss1, ["SE", "LC"])
            I(VE, v.tensor_scalar(siluc[:, :], siluc[:, :], 1.0, None, OP.add))
            v.wait_ge(VE.h, VE.n)
            I(VE, v.reciprocal(siluc[:, :], siluc[:, :]))
            v.wait_ge(VE.h, VE.n)
            I(VE, v.tensor_mul(siluc[:, :], siluc[:, :], cvcol[:, :]))
        ss_silu = snap()

        with nc.sbuf_tensor("modw0", [128, 6 * 512], F32) as modw0, \
             nc.sbuf_tensor("modw1", [128, 6 * 512], F32) as modw1:
            modw = [modw0, modw1]
            te0m = TE.n

            @blk.sync
            def _(sy):
                for k in range(NKD):
                    if k >= 2:
                        sy.wait_ge(TE.h, te0m + (k - 1))
                    I(GM[k % 2], sy.dma_start(out=modw[k % 2][:, :],
                                              in_=wmod_sl[k * 128:(k + 1) * 128, :]), 16)

            @blk.tensor
            def _(t):
                waits(t, ss_silu, ["VE"])
                for k in range(NKD):
                    t.wait_ge(GM[k % 2].h, 16 * (k // 2 + 1))
                    for cc in range(6):
                        mi = t.matmul(P[cc][0:1, :], siluc[:, k:k + 1],
                                      modw[k % 2][:, cc * 512:(cc + 1) * 512],
                                      start=(k == 0), stop=(k == NKD - 1))
                        if cc == 5:
                            I(TE, mi)
            ss_modmm = snap()

            for cc in range(6):
                @blk.vector
                def _(v, cc=cc, sw=snap()):
                    waits(v, sw, ["TE"])
                    if cc >= 2:
                        v.wait_ge(LX[cc % 2].h, mod_st_ck[cc - 2])
                    I(VE, v.tensor_copy(modmy[0:1, cc % 2, :], P[cc][0:1, :]))
                sse = snap()

                @blk.sync
                def _(sy, cc=cc, sse=sse):
                    waits(sy, sse, ["VE"])
                    I(LX[cc % 2], sy.dma_start(out=mod_in.ap()[0, cc * 512:(cc + 1) * 512][None, :],
                                               in_=modmy[0:1, cc % 2, :]), 16)
                if cc == 0:
                    mod_st_ck = {}
                mod_st_ck[cc] = LX[cc % 2].n
            ss_modst = snap()

            @blk.gpsimd
            def _(gp):
                gp.wait_ge(LX0.h, ss_modst["LX0"])
                gp.wait_ge(LX1.h, ss_modst["LX1"])
                I(CC, gp.collective_compute("AllGather", OP.bypass,
                                            replica_groups=[[0, 1, 2, 3], [4, 5, 6, 7]],
                                            ins=[mod_in.ap().opt()], outs=[mod_out.ap().opt()]))
            ss_modag = snap()

            @blk.sync
            def _(sy):
                waits(sy, ss_modag, ["CC"])
                for slot, q in [(0, 0), (1, 1), (2, 3), (3, 4)]:
                    for jj in range(4):
                        I(LC, sy.dma_start(out=modcol[:, slot, jj * 4:(jj + 1) * 4],
                                           in_=mod_out.ap()[jj, q * 512:(q + 1) * 512]
                                           .rearrange("(k p) -> p k", p=128),
                                           allow_slow_non_contiguous=True), 16)
            prev_mq = snap()

            @blk.vector
            def _(v, pm=prev_mq):
                waits(v, pm, ["VE", "LC"])
                I(VE, v.tensor_scalar(modcol[:, 1, :], modcol[:, 1, :], 1.0, None, OP.add))
                I(VE, v.tensor_scalar(modcol[:, 3, :], modcol[:, 3, :], 1.0, None, OP.add))
            ss_ca1 = snap()

            prev_g = ss_ca1
            for qi, q in [(0, 2), (1, 5)]:
                for jj in range(4):
                    @blk.sync
                    def _(sy, q=q, jj=jj, pg=prev_g):
                        waits(sy, pg, ["VE"])
                        I(LC, sy.dma_start(out=garow[0:1, :],
                                           in_=mod_out.ap()[jj, q * 512:(q + 1) * 512][None, :]), 16)
                    ssg1 = snap()

                    @blk.vector
                    def _(v, ssg1=ssg1):
                        waits(v, ssg1, ["LC"])
                        I(VE, v.tensor_scalar(gascr[0:1, :], garow[0:1, :], 0.0, None, OP.is_ge))
                        v.wait_ge(VE.h, VE.n)
                        I(VE, v.tensor_scalar(gascr[0:1, :], gascr[0:1, :], 2.0, None, OP.mult))
                        v.wait_ge(VE.h, VE.n)
                        I(VE, v.tensor_scalar(gascr[0:1, :], gascr[0:1, :], -1.0, None, OP.add))
                        I(VE, v.tensor_scalar(garo2[0:1, :], garow[0:1, :], -1.0, None, OP.mult))
                        v.wait_ge(VE.h, VE.n)
                        I(VE, v.tensor_max(garow[0:1, :], garow[0:1, :], garo2[0:1, :]))
                        v.wait_ge(VE.h, VE.n)
                        I(VE, v.tensor_scalar(garow[0:1, :], garow[0:1, :], 0.25, None, OP.max))
                        v.wait_ge(VE.h, VE.n)
                        I(VE, v.tensor_mul(garow[0:1, :], garow[0:1, :], gascr[0:1, :]))
                    ssg2 = snap()

                    @blk.tensor
                    def _(t, ssg2=ssg2):
                        waits(t, ssg2, ["VE"])
                        I(TE, t.matmul(P[0][:, :], onesf[0:1, 0:128], garow[0:1, :],
                                       start=True, stop=True))
                    ssg3 = snap()

                    @blk.vector
                    def _(v, qi=qi, jj=jj, ssg3=ssg3):
                        waits(v, ssg3, ["TE"])
                        if qi == 0:
                            I(VE, v.tensor_copy(garep[:, jj * 512:(jj + 1) * 512], P[0][:, :]))
                        else:
                            I(VE, v.tensor_copy(gbrep[:, jj * 512:(jj + 1) * 512], P[0][:, :]))
                    prev_g = snap()
            ss_garep = prev_g

            @blk.tensor
            def _(t):
                waits(t, ss_garep, ["VE"])
                I(TE, t.matmul(P[1][:, :], onesf[0:1, 0:128], gamrow[0:1, :],
                               start=True, stop=True))
            ss_rep2 = snap()

        gamr = Et(nc.sbuf_tensor("gamr", [128, 512], F32))

        @blk.vector
        def _(v):
            waits(v, ss_rep2, ["TE"])
            I(VE, v.tensor_copy(gamr[:, :], P[1][:, :]))
            I(VE, v.tensor_add(gbs[0:1, 0:E_], gbs[0:1, 0:E_], gbs[0:1, E_:2 * E_]))
            v.wait_ge(VE.h, VE.n)
        ss_prol = snap()

        with ExitStack() as p1:
            E1 = p1.enter_context
            b2f = E1(nc.sbuf_tensor("b2f", [E_, D], F32))

            @blk.sync
            def _(sy, sw=snap()):
                waits(sy, sw, ["LC"])
                I(LC, sy.dma_start(out=b2f[:, :], in_=b2e[:, :]), 16)
            ss_b2l = snap()

            @blk.vector
            def _(v, ss_b2l=ss_b2l):
                waits(v, ss_b2l, ["LC"])
                waits(v, ss_prol, ["VE"])
                I(VE, v.tensor_mul(b2p[:, :], b2f[:, :], gbrep[0:E_, :]))

            xblk = [E1(nc.sbuf_tensor(f"xblk{i}", [128, CPB, D], F32)) for i in range(2)]
            hx = [E1(nc.sbuf_tensor(f"hx{i}", [128, NKD, BLK], BF16)) for i in range(2)]
            masks = E1(nc.sbuf_tensor("masks", [128, 128], F32))
            wq_s = E1(nc.sbuf_tensor("wq_s", [128, NKD, 256], BF16))
            wk_s = E1(nc.sbuf_tensor("wk_s", [128, NKD, 256], BF16))
            wv_s = E1(nc.sbuf_tensor("wv_s", [128, NKD, 512], BF16))
            wo_s = E1(nc.sbuf_tensor("wo_s", [128, NKD, 512], BF16))
            wif_s = E1(nc.sbuf_tensor("wif_s", [128, NKD, 4], BF16))
            qb = E1(nc.sbuf_tensor("qb", [128, 2, BLK], F32))
            kb = E1(nc.sbuf_tensor("kb", [128, 2, BLK], F32))
            vb = E1(nc.sbuf_tensor("vb", [128, CPB, 2, 257], F32))
            sgb = E1(nc.sbuf_tensor("sgb", [128, CPB, 2, DV], F32))
            sqs = E1(nc.sbuf_tensor("sqs", [128, D], BF16))
            sts = E1(nc.sbuf_tensor("sts", [128, 10], F32))
            trin = E1(nc.sbuf_tensor("trin", [128, 128], F32))
            gw3 = E1(nc.sbuf_tensor("gw3", [2, NSLOT, 128], F32))
            gq = E1(nc.sbuf_tensor("gq", [2, 12, 128], F32))
            col_s = E1(nc.sbuf_tensor("col_s", [128, 12, CPB], F32))
            cin = E1(nc.sbuf_tensor("cin", [2, 2, 4], F32))
            r1 = E1(nc.sbuf_tensor("r1", [1, 2, 4], F32))
            scr0 = E1(nc.sbuf_tensor("scr0", [1, 2, 8], F32))
            colv = E1(nc.sbuf_tensor("colv", [2, 2, 4], F32))
            carry = E1(nc.sbuf_tensor("carry", [1, 2, 2], F32))
            state = E1(nc.sbuf_tensor("state", [128, 2, 257], F32))
            pp_s = E1(nc.sbuf_tensor("pp_s", [128, 128], F32))
            ak_s = E1(nc.sbuf_tensor("ak_s", [128, 128], F32))
            hq_s = E1(nc.sbuf_tensor("hq_s", [128, 257], F32))
            tmp1 = E1(nc.sbuf_tensor("tmp1", [128, 257], F32))
            hNb = E1(nc.sbuf_tensor("hNb", [128, CPB, 2, DV], F32))
            nsqb = E1(nc.sbuf_tensor("nsqb", [128, CPB * 2 * DV], F32))
            nstat = E1(nc.sbuf_tensor("nstat", [128, 8], F32))
            nst2 = E1(nc.sbuf_tensor("nst2", [128, 16], F32))
            exb = E1(nc.sbuf_tensor("exb", [128, 4, BLK], BF16))
            PBt = E1(nc.psum_tensor("PBt", [128, 256], F32))

            @blk.sync
            def _(sy):
                I(LCP, sy.dma_start(out=masks[:, :], in_=masksc_t.ap()), 16)

            @blk.gpsimd
            def _(gp):
                for dst, src in [(wq_s, wqT), (wk_s, wkT), (wv_s, wvT),
                                 (wo_s, wogT), (wif_s, wifT)]:
                    I(GC, gp.dma_start(out=dst[:, :, :],
                                       in_=src.rearrange("p (k w) -> p k w", k=NKD)), 16)
                I(GC, gp.dma_start(out=gw_s[:, :, :],
                                   in_=gwTd.rearrange("(k p) w -> p k w", p=128)), 16)
            ss_wattn = snap()

            @blk.gpsimd
            def _(gp):
                waits(gp, ss_wattn, ["GC"])
                q4 = WFSL // 4
                for i in range(4):
                    I(GC, gp.dma_start(out=wffn_my.ap()[i * q4:(i + 1) * q4],
                                       in_=wffn[i * q4:(i + 1) * q4]), 16)
            ss_wcast2 = snap()

            @blk.gpsimd
            def _(gp):
                gp.wait_ge(GC.h, ss_wcast2["GC"])
                I(CC, gp.collective_compute("AllGather", OP.bypass,
                                            replica_groups=[list(range(8))],
                                            ins=[wffn_my.ap().opt()], outs=[wffn_bf.ap().opt()]))
            ss_wfag = snap()

            @blk.vector
            def _(v):
                I(VE, v.memset(gw3[0:2, :, :], 0.0))
                I(VE, v.memset(gq[0:2, :, :], 0.0))
                I(VE, v.memset(trin[:, :], 0.0))
                I(VE, v.memset(carry[0:1, :, :], 0.0))
                I(VE, v.memset(state[:, :, :], 0.0))
                I(VE, v.memset(cin[0:2, :, :], 0.0))
                for j in range(CPB):
                    for h in range(2):
                        I(VE, v.memset(vb[:, j, h, 256:257], 1.0))

            xld_ck = {}

            @blk.sync
            def _(sy):
                for j in range(CPB):
                    I(LX0, sy.dma_start(out=xblk[0][:, j, :], in_=xb[j * 128:(j + 1) * 128, :]), 16)
            xld_ck[0] = LX0.n
            ss_p1init = snap()


            Fsn = {}

            def emit_F1(k, full_ssb=None):
                par = k % 2

                @blk.scalar
                def _(s, par=par, k=k, full_ssb=full_ssb):
                    if full_ssb is not None:
                        waits(s, full_ssb, ["LC", "LCP", "VE", "TE", "EX"])
                    s.wait_ge(LX[k % 2].h, xld_ck[k])
                    for j in range(CPB):
                        I(SE, s.activation(sqs[:, :], xblk[par][:, j, :], AF.Square,
                                           accum_out=sts[:, j:j + 1]))
                        s.wait_ge(SE.h, SE.n)
                ss_ac = snap()

                @blk.vector
                def _(v, par=par, k=k, ss_ac=ss_ac, full_ssb=full_ssb):
                    if full_ssb is not None:
                        waits(v, full_ssb, ["LC", "LCP", "SE", "TE"])
                    v.wait_ge(LX[k % 2].h, xld_ck[k])
                    for j in range(CPB):
                        I(VE, v.reduce_sum(sts[:, 2 + j:3 + j], xblk[par][:, j, :], axis=AX))
                    v.wait_ge(VE.h, VE.n)
                    waits(v, ss_ac, ["SE"])
                    I(VE, v.tensor_scalar(sts[:, 4:6], sts[:, 2:4], 1.0 / D, None, OP.mult))
                    I(VE, v.tensor_scalar(sts[:, 8:10], sts[:, 0:2], 1.0 / D, None, OP.mult))
                    v.wait_ge(VE.h, VE.n)
                    I(VE, v.tensor_mul(sts[:, 6:8], sts[:, 4:6], sts[:, 4:6]))
                    v.wait_ge(VE.h, VE.n)
                    I(VE, v.tensor_sub(sts[:, 6:8], sts[:, 6:8], sts[:, 8:10]))
                    v.wait_ge(VE.h, VE.n)
                ss_mv = snap()

                @blk.scalar
                def _(s, ss_mv=ss_mv):
                    waits(s, ss_mv, ["VE"])
                    I(SE, s.activation(sts[:, 6:8], sts[:, 6:8],
                                       AF.Ln, bias=epsc[:, 0:1], scale=-1.0))
                    s.wait_ge(SE.h, SE.n)
                    I(SE, s.activation(sts[:, 6:8], sts[:, 6:8],
                                       AF.Exp, bias=0.0, scale=-0.5))
                    s.wait_ge(SE.h, SE.n)
                ss_r = snap()

                @blk.vector
                def _(v, par=par, ss_r=ss_r):
                    waits(v, ss_r, ["SE"])
                    for j in range(CPB):
                        I(VE, v.tensor_scalar(xblk[par][:, j, :], xblk[par][:, j, :],
                                              sts[:, 4 + j:5 + j],
                                              sts[:, 6 + j:7 + j],
                                              OP.subtract, OP.mult))
                Fsn[k] = {"xh": snap()}

            def emit_F2(k, ss_colev):
                par = k % 2
                ss_xh = Fsn[k]["xh"]

                if debug and k == 0:
                    @blk.sync
                    def _(sy, ss_xh=ss_xh):
                        waits(sy, ss_xh, ["VE"])
                        I(DB, sy.dma_start(out=dbg["dbg_xhat"][:, :], in_=xblk[0][:, 0, :]), 16)
                    _ssdbg = snap()

                    @blk.vector
                    def _(v, _ssdbg=_ssdbg):
                        waits(v, _ssdbg, ["DB"])
                        I(VE, v.engine_nop())

                te0 = TE.n
                se0 = SE.n

                @blk.tensor
                def _(t, par=par, ss_xh=ss_xh, ss_colev=ss_colev, te0=te0, se0=se0):
                    waits(t, ss_xh, ["VE", "SE"])
                    waits(t, ss_colev, ["VE"])
                    i = 0
                    for j in range(CPB):
                        for kk in range(NKD):
                            if i >= 2:
                                t.wait_ge(SE.h, se0 + i - 1)
                            I(TE, t.transpose(P[i % 2][0:128, 0:128],
                                              xblk[par][:, j, kk * 128:(kk + 1) * 128],
                                              ident_fs[:, :]))
                            i += 1

                @blk.scalar
                def _(s, par=par, te0=te0):
                    i = 0
                    for j in range(CPB):
                        for kk in range(NKD):
                            s.wait_ge(TE.h, te0 + i + 1)
                            I(SE, s.activation(hx[par][:, kk, j * 128:(j + 1) * 128],
                                               P[i % 2][0:128, 0:128], AF.Identity,
                                               bias=modcol[:, 0, kk:kk + 1],
                                               scale=modcol[:, 1, kk:kk + 1]))
                            i += 1
                assert SE.n == se0 + CPB * NKD
                Fsn[k]["hx"] = snap()

            prev = ss_p1init
            for k in range(NBLK):
                par = k % 2
                T0 = k * BLK
                ssb = prev

                if k + 1 < NBLK:
                    @blk.sync
                    def _(sy, k=k, ssb=ssb):
                        waits(sy, ssb, ["TE"])
                        for j in range(CPB):
                            I(LX[(k + 1) % 2], sy.dma_start(out=xblk[(k + 1) % 2][:, j, :],
                                               in_=xb[(k + 1) * BLK + j * 128:
                                                      (k + 1) * BLK + (j + 1) * 128, :]), 16)
                    xld_ck[k + 1] = LX[(k + 1) % 2].n

                if k == 0:
                    emit_F1(0, full_ssb=ssb)
                    emit_F2(0, ssb)

                @blk.tensor
                def _(t, par=par, k=k, ssb=ssb):
                    waits(t, Fsn[k]["hx"], ["SE"])
                    waits(t, ssb, ["VE"])
                    waits(t, ss_wattn, ["GC"])
                    for h in range(2):
                        for kk in range(NKD):
                            mi = t.matmul(P[2][:, h * 256:h * 256 + BLK],
                                          wq_s[:, kk, h * 128:(h + 1) * 128], hx[par][:, kk, :],
                                          start=(kk == 0), stop=(kk == NKD - 1))
                            if kk == NKD - 1:
                                I(TE, mi)
                        for kk in range(NKD):
                            mi = t.matmul(P[3][:, h * 256:h * 256 + BLK],
                                          wk_s[:, kk, h * 128:(h + 1) * 128], hx[par][:, kk, :],
                                          start=(kk == 0), stop=(kk == NKD - 1))
                            if kk == NKD - 1:
                                I(TE, mi)
                ss_qk = snap()

                @blk.vector
                def _(v, ss_qk=ss_qk):
                    waits(v, ss_qk, ["TE"])
                    for h in range(2):
                        I(VE, v.tensor_copy(qb[:, h, :], P[2][:, h * 256:h * 256 + BLK]))
                        I(VE, v.tensor_copy(kb[:, h, :], P[3][:, h * 256:h * 256 + BLK]))
                ss_qke = snap()

                prev_vog = ss_qke
                for j in range(CPB):
                    @blk.tensor
                    def _(t, par=par, j=j, pv=prev_vog):
                        waits(t, pv, ["VE", "SE"])
                        waits(t, ss_wattn, ["GC"])
                        for kk in range(NKD):
                            mi = t.matmul(P[4][:, 0:512], hx[par][:, kk, j * 128:(j + 1) * 128],
                                          wv_s[:, kk, :], start=(kk == 0), stop=(kk == NKD - 1))
                            if kk == NKD - 1:
                                I(TE, mi)
                            mi = t.matmul(P[5][:, 0:512], hx[par][:, kk, j * 128:(j + 1) * 128],
                                          wo_s[:, kk, :], start=(kk == 0), stop=(kk == NKD - 1))
                            if kk == NKD - 1:
                                I(TE, mi)
                            t.matmul(P[6][0:128, 0:4], hx[par][:, kk, j * 128:(j + 1) * 128],
                                     wif_s[:, kk, :], start=(kk == 0), stop=False)
                        I(TE, t.matmul(P[6][0:128, 0:4], onesb[0:1, 0:128], rowcb[0:1, :],
                                       start=False, stop=True))
                    ss_v1 = snap()

                    @blk.vector
                    def _(v, j=j, ss_v1=ss_v1):
                        waits(v, ss_v1, ["TE"])
                        for h in range(2):
                            I(VE, v.tensor_copy(vb[:, j, h, 0:256], P[4][:, h * 256:(h + 1) * 256]))
                    ss_v2 = snap()

                    @blk.scalar
                    def _(s, j=j, ss_v1=ss_v1):
                        waits(s, ss_v1, ["TE"])
                        I(SE, s.activation(sgb[:, j, 0, :], P[5][:, 0:256], AF.Sigmoid,
                                           bias=0.0, scale=1.0))
                        I(SE, s.activation(sgb[:, j, 1, :], P[5][:, 256:512], AF.Sigmoid,
                                           bias=0.0, scale=1.0))
                        I(SE, s.activation(trin[:, j:j + 97:32], P[6][0:128, 0:4],
                                           AF.Tanh, bias=0.0, scale=1.0 / 15.0))
                    prev_vog = snap()
                ss_vog = prev_vog

                @blk.tensor
                def _(t, ss_vog=ss_vog):
                    waits(t, ss_vog, ["SE", "VE"])
                    I(TE, t.transpose(P[0][0:128, 0:128], trin[:, :], ident_fs[:, :]))
                ss_gt = snap()

                @blk.vector
                def _(v, ss_gt=ss_gt):
                    waits(v, ss_gt, ["TE"])
                    I(VE, v.tensor_copy(gw3[0:2, TI0, :], P[0][0:2, 0:128]))
                    I(VE, v.tensor_copy(gw3[0:2, TF0, :], P[0][32:34, 0:128]))
                    I(VE, v.tensor_copy(gw3[0:2, TI1, :], P[0][64:66, 0:128]))
                    I(VE, v.tensor_copy(gw3[0:2, TF1, :], P[0][96:98, 0:128]))
                ss_gr = snap()

                @blk.scalar
                def _(s, ss_gr=ss_gr):
                    waits(s, ss_gr, ["VE"])
                    for h in range(2):
                        I(SE, s.activation(gw3[0:2, SC0 + h, :], gw3[0:2, TF0 + 2 * h, :],
                                           AF.Exp, bias=0.0, scale=-15.0))
                        s.wait_ge(SE.h, SE.n)
                        I(SE, s.activation(gw3[0:2, SC0 + h, :], gw3[0:2, SC0 + h, :],
                                           AF.Ln, bias=1.0, scale=1.0))
                        s.wait_ge(SE.h, SE.n)
                ss_nb = snap()

                @blk.vector
                def _(v, ss_nb=ss_nb):
                    waits(v, ss_nb, ["SE"])
                    for h in range(2):
                        I(VE, v.tensor_scalar(gw3[0:2, FL0 + h, :], gw3[0:2, SC0 + h, :],
                                              -1.0, None, OP.mult))
                        I(VE, v.tensor_scalar(gw3[0:2, I50 + h, :], gw3[0:2, TI0 + 2 * h, :],
                                              15.0, None, OP.mult))
                    v.wait_ge(VE.h, VE.n)
                    for h in range(2):
                        I(VE, v.tensor_tensor_scan(gw3[0:2, BL0 + h, :], gw3[0:2, FL0 + h, :],
                                                   gw3[0:2, FL0 + h, :], 0.0, OP.add, OP.bypass))
                        I(VE, v.tensor_tensor_scan(gw3[0:2, ML0 + h, :], gw3[0:2, FL0 + h, :],
                                                   gw3[0:2, I50 + h, :], NEG, OP.add, OP.max))
                    v.wait_ge(VE.h, VE.n)
                    for h in range(2):
                        I(VE, v.tensor_copy(cin[0:2, h, 0:1], gw3[0:2, BL0 + h, 127:128]))
                        I(VE, v.tensor_copy(cin[0:2, h, 1:2], gw3[0:2, ML0 + h, 127:128]))
                ss_ce = snap()

                @blk.tensor
                def _(t, ss_ce=ss_ce):
                    waits(t, ss_ce, ["VE"])
                    I(TE, t.matmul(P[1][0:1, 0:8], ident_fs[0:2, 1:2],
                                   cin[0:2, :, :].rearrange("p a b -> p (a b)"),
                                   start=True, stop=True))
                ss_cx = snap()

                @blk.vector
                def _(v, ss_cx=ss_cx):
                    waits(v, ss_cx, ["TE"])
                    I(VE, v.tensor_copy(r1[0:1, :, :],
                                        P[1][0:1, 0:8].rearrange("p (a b) -> p a b", a=2)))
                    v.wait_ge(VE.h, VE.n)
                    for h in range(2):
                        be0 = cin[0:1, h, 0:1]
                        me0 = cin[0:1, h, 1:2]
                        be1 = r1[0:1, h, 0:1]
                        me1 = r1[0:1, h, 1:2]
                        Bc = carry[0:1, h, 0:1]
                        mc = carry[0:1, h, 1:2]
                        W = scr0[0:1, h, :]
                        I(VE, v.tensor_copy(W[:, 0:1], Bc))
                        I(VE, v.tensor_copy(W[:, 2:3], mc))
                        v.wait_ge(VE.h, VE.n)
                        I(VE, v.tensor_scalar(W[:, 1:2], be0, W[:, 0:1], None, OP.add))
                        I(VE, v.tensor_scalar(W[:, 3:4], be0, W[:, 2:3], None, OP.add))
                        v.wait_ge(VE.h, VE.n)
                        I(VE, v.tensor_tensor(W[:, 3:4], W[:, 3:4], me0, OP.max))
                        v.wait_ge(VE.h, VE.n)
                        I(VE, v.tensor_scalar(W[:, 6:7], be1, W[:, 1:2], None, OP.add))
                        I(VE, v.tensor_scalar(W[:, 7:8], be1, W[:, 3:4], None, OP.add))
                        v.wait_ge(VE.h, VE.n)
                        I(VE, v.tensor_tensor(W[:, 7:8], W[:, 7:8], me1, OP.max))
                        I(VE, v.tensor_tensor(W[:, 4:5], W[:, 2:3], W[:, 0:1], OP.subtract))
                        I(VE, v.tensor_tensor(W[:, 5:6], W[:, 3:4], W[:, 1:2], OP.subtract))
                        v.wait_ge(VE.h, VE.n)
                        I(VE, v.tensor_scalar(W[:, 4:6], W[:, 4:6], LNSC, None, OP.add))
                        v.wait_ge(VE.h, VE.n)
                        I(VE, v.tensor_copy(carry[0:1, h, 0:1], W[:, 6:7]))
                        I(VE, v.tensor_copy(carry[0:1, h, 1:2], W[:, 7:8]))
                ss_cw = snap()

                @blk.tensor
                def _(t, ss_cw=ss_cw):
                    waits(t, ss_cw, ["VE"])
                    for h in range(2):
                        t.matmul(P[1][0:2, 4 * h + 0:4 * h + 1], scr0[0:1, h, 0:2],
                                 onesf[0:1, 0:1], start=True, stop=True)
                        t.matmul(P[1][0:2, 4 * h + 1:4 * h + 2], scr0[0:1, h, 2:4],
                                 onesf[0:1, 0:1], start=True, stop=True)
                        mi = t.matmul(P[1][0:2, 4 * h + 2:4 * h + 3], scr0[0:1, h, 4:6],
                                      onesf[0:1, 0:1], start=True, stop=True)
                        if h == 1:
                            I(TE, mi)
                ss_cb = snap()

                @blk.vector
                def _(v, ss_cb=ss_cb):
                    waits(v, ss_cb, ["TE"])
                    I(VE, v.tensor_copy(colv[0:2, :, :],
                                        P[1][0:2, 0:8].rearrange("p (a b) -> p a b", a=2)))
                ss_colv = snap()

                @blk.vector
                def _(v, ss_colv=ss_colv):
                    waits(v, ss_colv, ["VE"])
                    for h in range(2):
                        I(VE, v.tensor_scalar(gw3[0:2, BG0 + h, :], gw3[0:2, BL0 + h, :],
                                              colv[0:2, h, 0:1], None, OP.add))
                        I(VE, v.tensor_scalar(gw3[0:2, SC0 + h, :], gw3[0:2, BL0 + h, :],
                                              colv[0:2, h, 1:2], None, OP.add))
                        v.wait_ge(VE.h, VE.n)
                        I(VE, v.tensor_tensor(gw3[0:2, MM0 + h, :], gw3[0:2, SC0 + h, :],
                                              gw3[0:2, ML0 + h, :], OP.max))
                        v.wait_ge(VE.h, VE.n)
                        I(VE, v.tensor_tensor(gw3[0:2, UU0 + h, :], gw3[0:2, I50 + h, :],
                                              gw3[0:2, BG0 + h, :], OP.subtract))
                        I(VE, v.tensor_tensor(gw3[0:2, GG0 + h, :], gw3[0:2, MM0 + h, :],
                                              gw3[0:2, BG0 + h, :], OP.subtract))
                        v.wait_ge(VE.h, VE.n)
                        I(VE, v.reduce_max(colv[0:2, h, 3:4], gw3[0:2, UU0 + h, :], axis=AX,
                                           negate=True))
                        v.wait_ge(VE.h, VE.n)
                ss_rows = snap()

                @blk.scalar
                def _(s, ss_rows=ss_rows):
                    waits(s, ss_rows, ["VE"])
                    for h in range(2):
                        q0 = 6 * h
                        I(SE, s.activation(gq[0:2, q0 + EXPU, :], gw3[0:2, UU0 + h, :],
                                           AF.Exp, bias=colv[0:2, h, 3:4], scale=1.0))
                        I(SE, s.activation(gq[0:2, q0 + S2Q, :], gw3[0:2, GG0 + h, :],
                                           AF.Exp, bias=colv[0:2, h, 2:3], scale=-1.0))
                        I(SE, s.activation(gq[0:2, q0 + EMQ, :], gw3[0:2, MM0 + h, :],
                                           AF.Exp, bias=0.0, scale=-1.0))
                ss_e1 = snap()

                @blk.vector
                def _(v, ss_e1=ss_e1):
                    waits(v, ss_e1, ["SE"])
                    for h in range(2):
                        I(VE, v.tensor_scalar(nstat[0:2, 4 + h:5 + h], colv[0:2, h, 3:4],
                                              -1.0, None, OP.mult))
                        I(VE, v.tensor_scalar(nstat[0:2, 6 + h:7 + h], gw3[0:2, GG0 + h, 127:128],
                                              -1.0, None, OP.mult))
                        I(VE, v.tensor_scalar(gw3[0:2, LR0 + h, 0:1], gw3[0:2, GG0 + h, 127:128],
                                              colv[0:2, h, 1:2], colv[0:2, h, 0:1],
                                              OP.subtract, OP.add))
                        v.wait_ge(VE.h, VE.n)
                ss_sm = snap()

                @blk.scalar
                def _(s, ss_sm=ss_sm):
                    waits(s, ss_sm, ["VE"])
                    for h in range(2):
                        q0 = 6 * h
                        I(SE, s.activation(gq[0:2, q0 + S1Q, :], gw3[0:2, GG0 + h, :],
                                           AF.Exp, bias=nstat[0:2, 4 + h:5 + h], scale=-1.0))
                        I(SE, s.activation(gq[0:2, q0 + AAQ, :], gw3[0:2, UU0 + h, :],
                                           AF.Exp, bias=nstat[0:2, 6 + h:7 + h], scale=1.0))
                        I(SE, s.activation(gw3[0:2, LR0 + h, 0:1], gw3[0:2, LR0 + h, 0:1],
                                           AF.Exp, bias=0.0, scale=-1.0))
                        s.wait_ge(SE.h, SE.n)
                ss_e2 = snap()

                @blk.vector
                def _(v, ss_e2=ss_e2):
                    waits(v, ss_e2, ["SE"])
                    for h in range(2):
                        q0 = 6 * h
                        I(VE, v.tensor_scalar(gq[0:2, q0 + LAMQ, :], ones2[0:2, :],
                                              gw3[0:2, LR0 + h, 0:1], None, OP.mult))
                ss_lam = snap()

                @blk.tensor
                def _(t, ss_lam=ss_lam):
                    waits(t, ss_lam, ["VE", "SE"])
                    for qi in range(12):
                        mi = t.transpose(P[0][0:128, 2 * qi:2 * qi + 2], gq[0:2, qi, :],
                                         ident_fs[0:2, 0:2])
                        if qi == 11:
                            I(TE, mi)

                @blk.vector
                def _(v, sw=snap()):
                    waits(v, sw, ["TE"])
                    I(VE, v.tensor_copy(col_s[:, :, :], P[0][0:128, 0:24]))
                ss_col = snap()

                if debug and k == 0:
                    @blk.sync
                    def _(sy, ss_col=ss_col):
                        waits(sy, ss_col, ["VE"])
                        I(DB, sy.dma_start(out=dbg["dbg_col"][:, :, :], in_=col_s[:, :, :]), 16)
                        I(DB, sy.dma_start(out=dbg["dbg_gw3"][:, :, :], in_=gw3[0:2, :, :]), 16)
                        I(DB, sy.dma_start(out=dbg["dbg_gq"][:, :, :], in_=gq[0:2, :, :]), 16)
                    _ssdbg = snap()

                    @blk.vector
                    def _(v, _ssdbg=_ssdbg):
                        waits(v, _ssdbg, ["DB"])
                        I(VE, v.engine_nop())

                if k + 1 < NBLK:
                    emit_F1(k + 1)

                ss_mix = ss_col
                prevu = None
                for c in range(CPB):
                    for h in range(2):
                        q0 = 6 * h

                        @blk.tensor
                        def _(t, c=c, h=h, ssm=ss_mix, prevu=prevu):
                            if prevu is None:
                                waits(t, ssm, ["VE", "SE"])
                            else:
                                waits(t, prevu[0], ["VE"])
                                waits(t, prevu[1], ["VE"])
                            I(TE, t.matmul(P[2][0:128, 0:128], kb[:, h, c * 128:(c + 1) * 128],
                                           qb[:, h, c * 128:(c + 1) * 128],
                                           start=True, stop=True))
                            I(TE, t.transpose(PBt[0:128, 0:128],
                                              kb[:, h, c * 128:(c + 1) * 128], ident_fs[:, :]))
                        ss_m1 = snap()

                        @blk.vector
                        def _(v, c=c, q0=q0, ss_m1=ss_m1):
                            waits(v, ss_m1, ["TE"])
                            I(VE, v.scalar_tensor_tensor(pp_s[:, :], P[2][0:128, 0:128],
                                                         col_s[:, q0 + EXPU, c:c + 1],
                                                         masks[:, :], OP.mult, OP.mult))
                        ss_pp = snap()

                        @blk.vector
                        def _(v, c=c, q0=q0, ss_m1=ss_m1):
                            waits(v, ss_m1, ["TE"])
                            I(VE, v.tensor_scalar(ak_s[:, :], PBt[0:128, 0:128],
                                                  col_s[:, q0 + AAQ, c:c + 1], None, OP.mult))
                        ss_ak = snap()
                        prevu = (ss_pp, ss_ak)

                        @blk.tensor
                        def _(t, c=c, h=h, ss_pp=ss_pp, ss_ak=ss_ak):
                            waits(t, ss_pp, ["VE"])
                            waits(t, ss_ak, ["VE"])
                            I(TE, t.matmul(P[3][0:128, 0:257], pp_s[:, :], vb[:, c, h, :],
                                           start=True, stop=True))
                            I(TE, t.matmul(P[4][0:128, 0:257],
                                           qb[:, h, c * 128:(c + 1) * 128],
                                           state[:, h, :], start=True, stop=True))
                            I(TE, t.matmul(P[5][0:128, 0:257], ak_s[:, :], vb[:, c, h, :],
                                           start=True, stop=True))
                        ss_m2 = snap()

                        @blk.vector
                        def _(v, c=c, h=h, q0=q0, ss_m2=ss_m2):
                            waits(v, ss_m2, ["TE"])
                            I(VE, v.tensor_scalar(tmp1[:, :], P[3][0:128, 0:257],
                                                  col_s[:, q0 + S1Q, c:c + 1], None, OP.mult))
                            v.wait_ge(VE.h, VE.n)
                            I(VE, v.scalar_tensor_tensor(hq_s[:, :], P[4][0:128, 0:257],
                                                         col_s[:, q0 + S2Q, c:c + 1],
                                                         tmp1[:, :], OP.mult, OP.add))
                            I(VE, v.scalar_tensor_tensor(state[:, h, :], state[:, h, :],
                                                         col_s[:, q0 + LAMQ, c:c + 1],
                                                         P[5][0:128, 0:257], OP.mult, OP.add))
                            v.wait_ge(VE.h, VE.n)
                            I(VE, v.tensor_scalar(nstat[:, 0:1], hq_s[:, 256:257],
                                                  -1.0, None, OP.mult))
                            v.wait_ge(VE.h, VE.n)
                            I(VE, v.tensor_max(nstat[:, 0:1], nstat[:, 0:1],
                                               hq_s[:, 256:257]))
                            v.wait_ge(VE.h, VE.n)
                            I(VE, v.tensor_scalar(nstat[:, 0:1], nstat[:, 0:1],
                                                  col_s[:, q0 + EMQ, c:c + 1], EPSD,
                                                  OP.max, OP.add))
                            v.wait_ge(VE.h, VE.n)
                            I(VE, v.reciprocal(nstat[:, 1:2], nstat[:, 0:1]))
                            v.wait_ge(VE.h, VE.n)
                            I(VE, v.tensor_scalar(hNb[:, c, h, :], hq_s[:, 0:256],
                                                  nstat[:, 1:2], None, OP.mult))
                            v.wait_ge(VE.h, VE.n)
                        ss_m3 = snap()

                        if debug and k == 1 and c == 0 and h == 0:
                            @blk.sync
                            def _(sy, ss_m3=ss_m3):
                                waits(sy, ss_m3, ["VE"])
                                I(DB, sy.dma_start(out=dbg["dbg_hq"][:, :], in_=hq_s[:, :]), 16)
                            _ssdbg = snap()

                            @blk.vector
                            def _(v, _ssdbg=_ssdbg):
                                waits(v, _ssdbg, ["DB"])
                                I(VE, v.engine_nop())

                        ss_mix = snap()

                if k + 1 < NBLK:
                    emit_F2(k + 1, ss_col)

                @blk.vector
                def _(v, ss_mix=ss_mix):
                    waits(v, ss_mix, ["VE", "SE"])
                    hv = hNb[:, :, :, :].rearrange("p a b c -> p (a b c)")
                    I(VE, v.tensor_mul(nsqb[:, :], hv, hv))
                    v.wait_ge(VE.h, VE.n)
                    I(VE, v.reduce_sum(nst2[:, 0:4],
                                       nsqb[:, :].rearrange("p (a c) -> p a c", a=4), axis=AX))
                    I(VE, v.reduce_sum(nst2[:, 4:8],
                                       hNb[:, :, :, :].rearrange("p a b c -> p (a b) c"), axis=AX))
                    v.wait_ge(VE.h, VE.n)
                    I(VE, v.tensor_scalar(nst2[:, 8:12], nst2[:, 4:8], 1.0 / DV, None, OP.mult))
                    I(VE, v.tensor_scalar(nst2[:, 12:16], nst2[:, 0:4], 1.0 / DV, None, OP.mult))
                    v.wait_ge(VE.h, VE.n)
                    I(VE, v.tensor_mul(nst2[:, 0:4], nst2[:, 8:12], nst2[:, 8:12]))
                    v.wait_ge(VE.h, VE.n)
                    I(VE, v.tensor_sub(nst2[:, 0:4], nst2[:, 0:4], nst2[:, 12:16]))
                    v.wait_ge(VE.h, VE.n)
                ss_nb1 = snap()

                @blk.scalar
                def _(s, ss_nb1=ss_nb1):
                    waits(s, ss_nb1, ["VE"])
                    I(SE, s.activation(nst2[:, 0:4], nst2[:, 0:4], AF.Ln,
                                       bias=epsc[:, 1:2], scale=-1.0))
                    s.wait_ge(SE.h, SE.n)
                    I(SE, s.activation(nst2[:, 0:4], nst2[:, 0:4], AF.Exp,
                                       bias=0.0, scale=-0.5))
                    s.wait_ge(SE.h, SE.n)
                ss_nb2 = snap()

                @blk.vector
                def _(v, ss_nb2=ss_nb2):
                    waits(v, ss_nb2, ["SE"])
                    for c in range(CPB):
                        for h in range(2):
                            cl = c * 2 + h
                            I(VE, v.tensor_scalar(hNb[:, c, h, :], hNb[:, c, h, :],
                                                  nst2[:, 8 + cl:9 + cl], nst2[:, cl:cl + 1],
                                                  OP.subtract, OP.mult))
                    v.wait_ge(VE.h, VE.n)
                    for c in range(CPB):
                        hv2 = hNb[:, c, :, :].rearrange("p b c -> p (b c)")
                        I(VE, v.tensor_mul(hv2, hv2, gamr[:, :]))
                        v.wait_ge(VE.h, VE.n)
                        I(VE, v.tensor_mul(hv2, hv2,
                                           sgb[:, c, :, :].rearrange("p b c -> p (b c)")))
                        v.wait_ge(VE.h, VE.n)
                ss_nb3 = snap()

                te_tr = TE.n
                ve_tr = VE.n

                @blk.tensor
                def _(t, ss_nb3=ss_nb3, ve_tr=ve_tr):
                    waits(t, ss_nb3, ["VE"])
                    i = 0
                    for c in range(CPB):
                        for h in range(2):
                            for dvt in range(2):
                                if i >= 2:
                                    t.wait_ge(VE.h, ve_tr + i - 1)
                                I(TE, t.transpose(P[3 + i % 2][0:128, 0:128],
                                                  hNb[:, c, h, dvt * 128:(dvt + 1) * 128],
                                                  ident_fs[:, :]))
                                i += 1

                @blk.vector
                def _(v, te_tr=te_tr, ssb=ssb):
                    waits(v, ssb, ["EX"])
                    i = 0
                    for c in range(CPB):
                        for h in range(2):
                            for dvt in range(2):
                                v.wait_ge(TE.h, te_tr + i + 1)
                                I(VE, v.tensor_copy(exb[:, h * 2 + dvt, c * 128:(c + 1) * 128],
                                                    P[3 + i % 2][0:128, 0:128]))
                                i += 1
                ss_mix = snap()

                if debug and k == 0:
                    @blk.sync
                    def _(sy, ss_mix=ss_mix):
                        waits(sy, ss_mix, ["VE"])
                        I(DB, sy.dma_start(out=dbg["dbg_state"][:, :], in_=state[:, 0, :]), 16)
                    _ssdbg2 = snap()

                    @blk.vector
                    def _(v, _ssdbg2=_ssdbg2):
                        waits(v, _ssdbg2, ["DB"])
                        I(VE, v.engine_nop())

                @blk.sync
                def _(sy, T0=T0, ss_mix=ss_mix):
                    waits(sy, ss_mix, ["VE"])
                    for vc in range(4):
                        I(EX, sy.dma_start(out=hout_in.ap()[vc * 128:(vc + 1) * 128, T0:T0 + BLK],
                                           in_=exb[:, vc, :]), 16)
                prev = snap()

            ss_p1 = prev

        @blk.gpsimd
        def _(gp):
            gp.wait_ge(EX.h, ss_p1["EX"])
            I(CC, gp.collective_compute("AllGather", OP.bypass,
                                        replica_groups=[[0, 1, 2, 3], [4, 5, 6, 7]],
                                        ins=[hout_in.ap().opt()], outs=[hout_out.ap().opt()]))
        ss_hag = snap()

        x2t = [Et(nc.sbuf_tensor(f"x2t{i}", [128, D], F32)) for i in range(2)]
        outs2 = [Et(nc.sbuf_tensor(f"outs{i}", [128, 1024], F32)) for i in range(2)]
        xsc = Et(nc.sbuf_tensor("xsc", [128, D], F32))
        sts2 = Et(nc.sbuf_tensor("sts2", [128, 10], F32))

        with ExitStack() as p2a:
            E2 = p2a.enter_context
            woutb = E2(nc.sbuf_tensor("woutb", [128, NKD, D], BF16))
            houtmy = E2(nc.sbuf_tensor("houtmy", [128, NKD, TOK], BF16))

            @blk.sync
            def _(sy):
                waits(sy, ss_wfag, ["CC"])
                wflat = wffn_bf.ap()[WOUTOFF:WOUTOFF + 128 * NKD * D]
                wpk = wflat.rearrange("(p k w) -> p k w", p=128, k=NKD)
                for i in range(4):
                    I(LH, sy.dma_start(out=woutb[:, 4 * i:4 * (i + 1), :],
                                       in_=wpk[:, 4 * i:4 * (i + 1), :]), 16)
            ss_wout = snap()

            @blk.sync
            def _(sy, ss_hag=ss_hag, ss_wout=ss_wout):
                waits(sy, ss_hag, ["CC"])
                waits(sy, ss_wout, ["LH"])
                pid = nc.partition_id(engines=[SP_ENG])
                for vc in range(NKD):
                    I(LH, sy.dma_start(out=houtmy[:, vc, :],
                                       in_=hout_out.ap()[vc * 128:(vc + 1) * 128,
                                                         bass.ts(pid % 4, TOK)]), 16)
            ss_hmy = snap()

            @blk.vector
            def _(v, ss_wout=ss_wout, ss_p1=ss_p1):
                waits(v, ss_wout, ["LH"])
                waits(v, ss_p1, ["VE"])
                for kk in range(NKD):
                    I(VE, v.tensor_mul(woutb[:, kk, :], woutb[:, kk, :], garep[:, :]))
            ss_wfold = snap()

            prev_w = ss_wfold
            x2st_ck = {}
            for tt in range(NTT):
                @blk.sync
                def _(sy, tt=tt, pw=prev_w):
                    if tt >= 2:
                        waits(sy, pw, ["VE"])
                    I(LT[tt % 2], sy.dma_start(out=x2t[tt % 2][:, :],
                                               in_=xresd[tt * 128:(tt + 1) * 128, :]), 16)
                lt_ck_w = LT[tt % 2].n
                ss_xl = snap()

                @blk.tensor
                def _(t, tt=tt, pw=prev_w, ss_hmy=ss_hmy, ss_wfold=ss_wfold):
                    waits(t, pw, ["VE"])
                    if tt == 0:
                        waits(t, ss_hmy, ["LH"])
                        waits(t, ss_wfold, ["VE"])
                    for dc in range(4):
                        for vc in range(NKD):
                            mi = t.matmul(P[dc][:, :], houtmy[:, vc, tt * 128:(tt + 1) * 128],
                                          woutb[:, vc, dc * 512:(dc + 1) * 512],
                                          start=(vc == 0), stop=(vc == NKD - 1))
                            if vc == NKD - 1:
                                I(TE, mi)
                ss_wmm = snap()

                @blk.vector
                def _(v, tt=tt, ss_wmm=ss_wmm, ss_xl=ss_xl):
                    waits(v, ss_wmm, ["TE"])
                    v.wait_ge(LT[tt % 2].h, ss_xl["LT%d" % (tt % 2)])
                    if tt >= 2:
                        v.wait_ge(SX[tt % 2].h, x2st_ck[tt - 2])
                    for dc in range(4):
                        I(VE, v.tensor_add(x2t[tt % 2][:, dc * 512:(dc + 1) * 512],
                                           x2t[tt % 2][:, dc * 512:(dc + 1) * 512],
                                           P[dc][:, :]))
                        v.wait_ge(VE.h, VE.n)
                ss_we = snap()

                @blk.sync
                def _(sy, tt=tt, ss_we=ss_we):
                    waits(sy, ss_we, ["VE"])
                    I(SX[tt % 2], sy.dma_start(out=x2d.ap()[tt * 128:(tt + 1) * 128, :],
                                               in_=x2t[tt % 2][:, :]), 16)
                x2st_ck[tt] = SX[tt % 2].n
                prev_w = snap()
            ss_x2 = prev_w

            if debug:
                @blk.sync
                def _(sy, ss_x2=ss_x2):
                    waits(sy, ss_x2, ["VE"])
                    I(DB, sy.dma_start(out=dbg["dbg_x2"][:, :], in_=x2t[0][:, :]), 16)
                _ssdbg = snap()

                @blk.vector
                def _(v, _ssdbg=_ssdbg):
                    waits(v, _ssdbg, ["DB"])
                    I(VE, v.engine_nop())

        with ExitStack() as p2b:
            E3 = p2b.enter_context
            h2 = E3(nc.sbuf_tensor("h2", [128, NKD, THALF], BF16))
            hmc = E3(nc.sbuf_tensor("hmc", [128, NKH, THALF], BF16))
            w1b = [E3(nc.sbuf_tensor(f"w1b{i}", [128, NKD, 128], BF16)) for i in range(2)]
            w2b = [E3(nc.sbuf_tensor(f"w2b{i}", [128, 2048], BF16)) for i in range(2)]
            combs = E3(nc.sbuf_tensor("combs", [128, TTH, E_], F32))
            combrow = E3(nc.sbuf_tensor("combrow", [1, E_, THALF], BF16))
            comb8 = E3(nc.sbuf_tensor("comb8", [E_, THALF], BF16))
            combrep = E3(nc.sbuf_tensor("combrep", [128, THALF], BF16))
            P7 = E3(nc.psum_tensor("P7", [128, 512], F32))
            PF = P + [P7]

            prev_half = snap()
            for hh in range(2):
                prev_ln = prev_half
                for tl in range(TTH):
                    tt = hh * TTH + tl

                    @blk.sync
                    def _(sy, tt=tt, tl=tl, pl=prev_ln):
                        waits(sy, pl, ["VE", "TE", "SE", "SX0", "SX1", "SO0", "SO1"])
                        I(LT[tl % 2], sy.dma_start(out=x2t[tl % 2][:, :],
                                                   in_=x2d.ap()[tt * 128:(tt + 1) * 128, :]), 16)
                    ss_xld = snap()

                    @blk.scalar
                    def _(s, tl=tl, pl=prev_ln, ss_xld=ss_xld):
                        waits(s, pl, ["VE", "TE", "SE", "GC", "CC"])
                        s.wait_ge(LT[tl % 2].h, ss_xld["LT%d" % (tl % 2)])
                        I(SE, s.activation(xsc[:, :], x2t[tl % 2][:, :], AF.Square,
                                           accum_out=sts2[:, 0:1]))
                        s.wait_ge(SE.h, SE.n)
                        I(SE, s.activation(xsc[:, :], x2t[tl % 2][:, :], AF.Copy,
                                           accum_out=sts2[:, 1:2]))
                        s.wait_ge(SE.h, SE.n)
                    ss_a = snap()

                    @blk.vector
                    def _(v, ss_a=ss_a):
                        waits(v, ss_a, ["SE"])
                        I(VE, v.tensor_scalar(sts2[:, 4:5], sts2[:, 1:2], 1.0 / D, None, OP.mult))
                        I(VE, v.tensor_scalar(sts2[:, 8:9], sts2[:, 0:1], 1.0 / D, None, OP.mult))
                        v.wait_ge(VE.h, VE.n)
                        I(VE, v.scalar_tensor_tensor(sts2[:, 5:6], sts2[:, 4:5], sts2[:, 4:5],
                                                     sts2[:, 8:9], OP.mult, OP.subtract))
                        v.wait_ge(VE.h, VE.n)
                    ss_b = snap()

                    @blk.scalar
                    def _(s, ss_b=ss_b):
                        waits(s, ss_b, ["VE"])
                        I(SE, s.activation(sts2[:, 5:6], sts2[:, 5:6], AF.Ln,
                                           bias=epsc[:, 0:1], scale=-1.0))
                        s.wait_ge(SE.h, SE.n)
                        I(SE, s.activation(sts2[:, 5:6], sts2[:, 5:6], AF.Exp,
                                           bias=0.0, scale=-0.5))
                        s.wait_ge(SE.h, SE.n)
                    ss_c = snap()

                    @blk.vector
                    def _(v, tl=tl, ss_c=ss_c):
                        waits(v, ss_c, ["SE"])
                        I(VE, v.tensor_scalar(xsc[:, :], x2t[tl % 2][:, :], sts2[:, 4:5],
                                              sts2[:, 5:6], OP.subtract, OP.mult))
                    ss_d = snap()

                    te2 = TE.n
                    se2 = SE.n

                    @blk.tensor
                    def _(t, ss_d=ss_d, se2=se2):
                        waits(t, ss_d, ["VE", "SE"])
                        for kk in range(NKD):
                            if kk >= 2:
                                t.wait_ge(SE.h, se2 + kk - 1)
                            I(TE, t.transpose(P[kk % 2][0:128, 0:128],
                                              xsc[:, kk * 128:(kk + 1) * 128], ident_fs[:, :]))

                    @blk.scalar
                    def _(s, tl=tl, te2=te2):
                        for kk in range(NKD):
                            s.wait_ge(TE.h, te2 + kk + 1)
                            I(SE, s.activation(h2[:, kk, tl * 128:(tl + 1) * 128],
                                               P[kk % 2][0:128, 0:128], AF.Identity,
                                               bias=modcol[:, 2, kk:kk + 1],
                                               scale=modcol[:, 3, kk:kk + 1]))
                    prev_ln = snap()
                ss_h2 = prev_ln

                prev_cb = ss_h2
                for tl in range(TTH):
                    @blk.tensor
                    def _(t, tl=tl, pc=prev_cb, ss_h2=ss_h2):
                        waits(t, ss_h2, ["SE"])
                        waits(t, pc, ["VE"])
                        for kk in range(NKD):
                            t.matmul(P[2][0:128, 0:E_], h2[:, kk, tl * 128:(tl + 1) * 128],
                                     gw_s[:, kk, :], start=(kk == 0), stop=False)
                        I(TE, t.matmul(P[2][0:128, 0:E_], onesf[0:1, 0:128], gbs[0:1, 0:E_],
                                       start=False, stop=True))
                    ss_sc = snap()

                    @blk.vector
                    def _(v, tl=tl, ss_sc=ss_sc):
                        waits(v, ss_sc, ["TE"])
                        I(VE, v.tensor_copy(combs[:, tl, :], P[2][0:128, 0:E_]))
                        v.wait_ge(VE.h, VE.n)
                        I(VE, v.reduce_max(sts2[:, 0:1], combs[:, tl, :], axis=AX, negate=True))
                        v.wait_ge(VE.h, VE.n)
                        I(VE, v.tensor_scalar(sts2[:, 1:2], sts2[:, 0:1], -1.0, None, OP.mult))
                        v.wait_ge(VE.h, VE.n)
                        I(VE, v.tensor_scalar(xsc[:, 0:E_], combs[:, tl, :], sts2[:, 1:2],
                                              None, OP.is_ge))
                        v.wait_ge(VE.h, VE.n)
                        I(VE, v.reduce_sum(sts2[:, 2:3], xsc[:, 0:E_], axis=AX))
                        I(VE, v.scalar_tensor_tensor(xsc[:, E_:2 * E_], xsc[:, 0:E_], NEG,
                                                     combs[:, tl, :], OP.mult, OP.add))
                        v.wait_ge(VE.h, VE.n)
                        I(VE, v.reduce_max(sts2[:, 3:4], xsc[:, E_:2 * E_], axis=AX))
                        I(VE, v.tensor_scalar(sts2[:, 2:3], sts2[:, 2:3], 2.0, None, OP.is_ge))
                        v.wait_ge(VE.h, VE.n)
                        I(VE, v.tensor_tensor(sts2[:, 4:5], sts2[:, 1:2], sts2[:, 3:4],
                                              OP.subtract))
                        v.wait_ge(VE.h, VE.n)
                        I(VE, v.tensor_mul(sts2[:, 4:5], sts2[:, 4:5], sts2[:, 2:3]))
                        v.wait_ge(VE.h, VE.n)
                        I(VE, v.tensor_add(sts2[:, 4:5], sts2[:, 4:5], sts2[:, 3:4]))
                        v.wait_ge(VE.h, VE.n)
                        I(VE, v.tensor_scalar(xsc[:, 0:E_], combs[:, tl, :], sts2[:, 4:5],
                                              None, OP.is_ge))
                        v.wait_ge(VE.h, VE.n)
                    ss_c1 = snap()

                    @blk.scalar
                    def _(s, tl=tl, ss_c1=ss_c1):
                        waits(s, ss_c1, ["VE"])
                        I(SE, s.activation(combs[:, tl, :], combs[:, tl, :], AF.Exp,
                                           bias=sts2[:, 0:1], scale=1.0))
                        s.wait_ge(SE.h, SE.n)
                    ss_c2 = snap()

                    @blk.vector
                    def _(v, tl=tl, ss_c2=ss_c2):
                        waits(v, ss_c2, ["SE"])
                        I(VE, v.tensor_mul(combs[:, tl, :], combs[:, tl, :], xsc[:, 0:E_]))
                        v.wait_ge(VE.h, VE.n)
                        I(VE, v.reduce_sum(sts2[:, 5:6], combs[:, tl, :], axis=AX))
                        v.wait_ge(VE.h, VE.n)
                        I(VE, v.reciprocal(sts2[:, 6:7], sts2[:, 5:6]))
                        v.wait_ge(VE.h, VE.n)
                        I(VE, v.tensor_scalar(combs[:, tl, :], combs[:, tl, :], sts2[:, 6:7],
                                              None, OP.mult))
                    ss_c3 = snap()

                    if debug and hh == 0 and tl == 0:
                        @blk.sync
                        def _(sy, ss_c3=ss_c3):
                            waits(sy, ss_c3, ["VE"])
                            I(DB, sy.dma_start(out=dbg["dbg_comb"][:, :], in_=combs[:, 0, :]), 16)
                        _ssdbg = snap()

                        @blk.vector
                        def _(v, _ssdbg=_ssdbg):
                            waits(v, _ssdbg, ["DB"])
                            I(VE, v.engine_nop())

                    for grp in range(2):
                        @blk.tensor
                        def _(t, tl=tl, grp=grp, sw=snap()):
                            waits(t, sw, ["VE"])
                            for ei in range(4):
                                e = grp * 4 + ei
                                mi = t.transpose(P[3][0:1, ei * 128:(ei + 1) * 128],
                                                 combs[:, tl, e:e + 1], ident_fs[:, :])
                                if ei == 3:
                                    I(TE, mi)
                            if grp == 0:
                                I(TE, t.transpose(P[4][0:E_, 0:128], combs[:, tl, :],
                                                  ident_fs[:, :]))
                        ss_ct = snap()

                        @blk.vector
                        def _(v, tl=tl, grp=grp, ss_ct=ss_ct):
                            waits(v, ss_ct, ["TE"])
                            for ei in range(4):
                                e = grp * 4 + ei
                                I(VE, v.tensor_copy(combrow[0:1, e, tl * 128:(tl + 1) * 128],
                                                    P[3][0:1, ei * 128:(ei + 1) * 128]))
                            if grp == 0:
                                I(VE, v.tensor_copy(comb8[:, tl * 128:(tl + 1) * 128],
                                                    P[4][0:E_, 0:128]))
                    prev_cb = snap()
                ss_comb = prev_cb

                prev_hm = ss_comb
                mkL = []
                sil_ck = []
                sil_se_ck = []
                for e in range(E_):
                    @blk.tensor
                    def _(t, e=e, pm=prev_hm):
                        waits(t, pm, ["VE", "SE"])
                        I(TE, t.matmul(P[6][:, 0:THALF], onesb[0:1, 0:128],
                                       combrow[0:1, e, :], start=True, stop=True))
                    ss_cr = snap()

                    @blk.vector
                    def _(v, e=e, ss_cr=ss_cr):
                        waits(v, ss_cr, ["TE"])
                        if e > 0:
                            v.wait_ge(TE.h, mkL[-1])
                        I(VE, v.tensor_copy(combrep[:, :], P[6][:, 0:THALF]))
                    ss_cre = snap()

                    for ht in range(NHT):
                        L = e * NHT + ht
                        PH = [P[5], P7][L % 2]

                        @blk.sync
                        def _(sy, e=e, ht=ht, L=L, ss_comb=ss_comb):
                            if L < 2:
                                waits(sy, ss_comb, ["TE", "VE"])
                            else:
                                sy.wait_ge(TE.h, mkL[L - 2])
                            off = L * W1BLK
                            I(W1S[L % 2], sy.dma_start(out=w1b[L % 2][:, :, :],
                                              in_=wffn_bf.ap()[off:off + W1BLK]
                                              .rearrange("(p k h) -> p k h", p=128, k=NKD)), 16)
                        ss_w1 = snap()

                        @blk.tensor
                        def _(t, ht=ht, L=L, PH=PH, ss_w1=ss_w1, ss_cre=ss_cre):
                            t.wait_ge(W1S[L % 2].h, ss_w1["W1%d" % (L % 2)])
                            if ht == 0:
                                waits(t, ss_cre, ["VE"])
                            if L >= 2:
                                t.wait_ge(SE.h, sil_se_ck[L - 2])
                            for kk in range(NKD):
                                mi = t.matmul(PH[:, 0:THALF], w1b[L % 2][:, kk, :],
                                              h2[:, kk, :], start=(kk == 0),
                                              stop=(kk == NKD - 1))
                                if kk == NKD - 1:
                                    I(TE, mi)
                        mkL.append(TE.n)
                        ss_h1 = snap()

                        @blk.scalar
                        def _(s, e=e, ht=ht, L=L, PH=PH, ss_h1=ss_h1):
                            waits(s, ss_h1, ["TE"])
                            if L >= 1:
                                s.wait_ge(VE.h, sil_ck[L - 1])
                            I(SE, s.activation(xsc[:, 0:THALF], PH[:, 0:THALF], AF.Silu,
                                               bias=b1col[:, e, ht:ht + 1], scale=1.0))
                        sil_se_ck.append(SE.n)
                        ss_h2e = snap()

                        @blk.vector
                        def _(v, e=e, ht=ht, ss_h2e=ss_h2e):
                            waits(v, ss_h2e, ["SE"])
                            I(VE, v.tensor_mul(hmc[:, e * NHT + ht, :], xsc[:, 0:THALF],
                                               combrep[:, :]))
                        sil_ck.append(VE.n)
                        prev_hm = snap()
                ss_hmid = prev_hm

                for dp in range(2):
                    swd = snap()
                    te3 = TE.n

                    w2_base = {0: W20.n, 1: W21.n}

                    @blk.sync
                    def _(sy, dp=dp, swd=swd, te3=te3):
                        waits(sy, swd, ["VE", "TE"])
                        for kk in range(NKH):
                            if kk >= 2:
                                sy.wait_ge(TE.h, te3 + kk - 1)
                            off = W1TOT + kk * W2BLK
                            I(W2S[kk % 2], sy.dma_start(out=w2b[kk % 2][:, :],
                                              in_=wffn_bf.ap()[off:off + W2BLK]
                                              .rearrange("(p w) -> p w", p=128)), 16)

                    @blk.tensor
                    def _(t, dp=dp, w2_base=w2_base, swd=swd):
                        waits(t, swd, ["VE"])
                        for tl in range(TTH):
                            for dc in range(2):
                                t.matmul(PF[tl * 2 + dc][:, :], comb8[:, tl * 128:(tl + 1) * 128],
                                         b2p[:, dp * 1024 + dc * 512:dp * 1024 + (dc + 1) * 512],
                                         start=True, stop=False)
                        for kk in range(NKH):
                            t.wait_ge(W2S[kk % 2].h, w2_base[kk % 2] + 16 * (kk // 2 + 1))
                            for tl in range(TTH):
                                for dc in range(2):
                                    mi = t.matmul(PF[tl * 2 + dc][:, :],
                                                  hmc[:, kk, tl * 128:(tl + 1) * 128],
                                                  w2b[kk % 2][:, dp * 1024 + dc * 512:
                                                              dp * 1024 + (dc + 1) * 512],
                                                  start=False, stop=(kk == NKH - 1))
                                    if tl == TTH - 1 and dc == 1:
                                        I(TE, mi)
                    ss_y2 = snap()

                    st_ck = {}
                    for tl in range(TTH):
                        @blk.sync
                        def _(sy, dp=dp, hh=hh, tl=tl, ss_y2=ss_y2, st_ck=dict(st_ck)):
                            waits(sy, ss_y2, ["TE"])
                            if tl >= 2:
                                sy.wait_ge(VE.h, st_ck[(tl - 2, "ve")])
                            tt = hh * TTH + tl
                            I(LT[tl % 2], sy.dma_start(out=x2t[tl % 2][:, 0:1024],
                                               in_=x2d.ap()[tt * 128:(tt + 1) * 128,
                                                            dp * 1024:(dp + 1) * 1024]), 16)
                        ss_xh2 = snap()

                        @blk.vector
                        def _(v, dp=dp, tl=tl, ss_y2=ss_y2, ss_xh2=ss_xh2,
                              st_ck=dict(st_ck)):
                            waits(v, ss_y2, ["TE", "SO0", "SO1"])
                            v.wait_ge(LT[tl % 2].h, ss_xh2["LT%d" % (tl % 2)])
                            if tl >= 2:
                                v.wait_ge(SO[tl % 2].h, st_ck[(tl - 2, "st")])
                            ob = outs2[tl % 2]
                            for dc in range(2):
                                I(VE, v.tensor_mul(ob[:, dc * 512:(dc + 1) * 512],
                                                   PF[tl * 2 + dc][:, :],
                                                   gbrep[:, dp * 1024 + dc * 512:
                                                          dp * 1024 + (dc + 1) * 512]))
                                v.wait_ge(VE.h, VE.n)
                                I(VE, v.tensor_add(ob[:, dc * 512:(dc + 1) * 512],
                                                   ob[:, dc * 512:(dc + 1) * 512],
                                                   x2t[tl % 2][:, dc * 512:(dc + 1) * 512]))
                                v.wait_ge(VE.h, VE.n)
                        ss_ev = snap()
                        st_ck[(tl, "ve")] = VE.n

                        @blk.sync
                        def _(sy, dp=dp, hh=hh, tl=tl, ss_ev=ss_ev):
                            waits(sy, ss_ev, ["VE"])
                            tt = hh * TTH + tl
                            I(SO[tl % 2], sy.dma_start(out=out[tt * 128:(tt + 1) * 128,
                                                       dp * 1024:(dp + 1) * 1024],
                                               in_=outs2[tl % 2][:, :]), 16)
                        st_ck[(tl, "st")] = SO[tl % 2].n
                    prev_half = snap()

        @blk.sync
        def _(sy):
            for sm in [SO0, SO1, SX0, SX1, EX, CC, DB]:
                if sm.n > 0:
                    sy.wait_ge(sm.h, sm.n)

    return nc


def _prep(inputs, cfg):
    S, TOK = cfg.S, cfg.TOK
    x = np.asarray(inputs["L0_attn_x"], np.float32)
    Wq = np.asarray(inputs["L0_attn_Wq"], np.float32)
    Wk = np.asarray(inputs["L0_attn_Wk"], np.float32)
    Wv = np.asarray(inputs["L0_attn_Wv"], np.float32)
    Wog = np.asarray(inputs["L0_attn_Wog"], np.float32)
    Wig = np.asarray(inputs["L0_attn_Wig"], np.float32)
    big = np.asarray(inputs["L0_attn_big"], np.float32)
    Wfg = np.asarray(inputs["L0_attn_Wfg"], np.float32)
    bfg = np.asarray(inputs["L0_attn_bfg"], np.float32)
    gamma = np.asarray(inputs["L0_attn_gamma"], np.float32)
    Wout = np.asarray(inputs["L0_attn_Wout"], np.float32)
    gate_w = np.asarray(inputs["L0_ffn_gate_w"], np.float32)
    gate_b = np.asarray(inputs["L0_ffn_gate_b"], np.float32)
    bias = np.asarray(inputs["L0_ffn_bias"], np.float32)
    w1 = np.asarray(inputs["L0_ffn_w1"], np.float32)
    b1 = np.asarray(inputs["L0_ffn_b1"], np.float32)
    w2 = np.asarray(inputs["L0_ffn_w2"], np.float32)
    b2 = np.asarray(inputs["L0_ffn_b2"], np.float32)
    cc = np.asarray(inputs["L0_c"], np.float32)
    wmod = np.asarray(inputs["L0_wmod"], np.float32)

    C = np.ascontiguousarray

    def PK(a):
        a = np.asarray(a)
        w = a.shape[1]
        return C(a.reshape(16, 128, w).transpose(1, 0, 2).reshape(128, 16 * w))
    WqT, WkT, WvT, WogT = C(Wq.T), C(Wk.T), C(Wv.T), C(Wog.T)
    WigT, WfgT = C(Wig.T), C(Wfg.T)
    WoutT = C(Wout.T)
    gwT = C(gate_w.T)
    w1T = w1.transpose(0, 2, 1)
    w1flat = C(w1T.reshape(E_, NKD, 128, NHT, 128).transpose(0, 3, 2, 1, 4)).reshape(-1)
    w2flat = C(w2.transpose(0, 2, 1)).reshape(-1)
    wffn_full = np.concatenate([w1flat, w2flat, PK(WoutT).reshape(-1)]).reshape(8, -1)

    in_maps = []
    for c in range(8):
        b, g = c // 4, c % 4
        hg = 2 * g
        m = {
            "xb": C(x[b, :S]),
            "xresd": C(x[b, g * TOK:(g + 1) * TOK]),
            "wqT": PK(WqT[:, hg * 128:(hg + 2) * 128]),
            "wkT": PK(WkT[:, hg * 128:(hg + 2) * 128]),
            "wvT": PK(WvT[:, hg * 256:(hg + 2) * 256]),
            "wogT": PK(WogT[:, hg * 256:(hg + 2) * 256]),
            "wifT": PK(np.stack([WigT[:, hg], WfgT[:, hg],
                                WigT[:, hg + 1], WfgT[:, hg + 1]], axis=1)),
            "bifr": np.array([[big[hg], bfg[hg], big[hg + 1], bfg[hg + 1]]], np.float32),
            "gammag": C(gamma[hg:hg + 2].reshape(1, 512)),
            "wmod_sl": C(np.concatenate([wmod[:, (g + 4 * i) * 512:(g + 4 * i + 1) * 512]
                                         for i in range(6)], axis=1)),
            "gwTd": gwT,
            "gb2d": gate_b.reshape(1, E_),
            "gbiasd": bias.reshape(1, E_),
            "b1e": b1,
            "b2e": b2,
            "cvecd": cc[b].reshape(1, D),
            "wffn": wffn_full[c],
        }
        in_maps.append(m)
    return in_maps


def kernel(**inputs):
    cfg = CFG(int(np.asarray(inputs["L0_attn_x"]).shape[1]))
    in_maps = _prep(inputs, cfg)
    nc = build_program(cfg, debug=False)
    res = run_bass_kernel_spmd(nc, in_maps, core_ids=list(range(8)))
    B_, S_ = np.asarray(inputs["L0_attn_x"]).shape[0], cfg.S
    outa = np.empty((B_, S_, D), np.float32)
    for c in range(8):
        b, g = c // 4, c % 4
        outa[b, g * cfg.TOK:(g + 1) * cfg.TOK] = res.results[c]["out"]
    return outa


# version 24
# speedup vs baseline: 1.0240x; 1.0124x over previous
import numpy as np
import ml_dtypes
from contextlib import ExitStack

import concourse.bass as bass
import concourse.mybir as mybir
from concourse.bass_utils import run_bass_kernel_spmd

F32 = mybir.dt.float32
BF16 = mybir.dt.bfloat16
AF = mybir.ActivationFunctionType
OP = mybir.AluOpType
AX = mybir.AxisListType.X
SP_ENG = mybir.EngineType.SP

D = 2048
DQ, DV = 128, 256
E_, H_ = 8, 1408
NHT = H_ // 128
NKD = D // 128
NKH = E_ * NHT
QKSCALE = float(DQ) ** -0.5
LNSC = float(np.log(QKSCALE))
EPS1, EPS2, EPSD = 1e-5, 1e-6, 1e-6
NEG = -1e30

W1BLK = NKD * 128 * 128
W2BLK = 128 * D
W1TOT = E_ * NHT * W1BLK
W2TOT = E_ * NHT * W2BLK
WOUTOFF = W1TOT + W2TOT
WFTOT = WOUTOFF + 128 * 16 * 2048
WFSL = WFTOT // 8

(TI0, TF0, TI1, TF1, FL0, FL1, I50, I51, BL0, BL1, ML0, ML1,
 BG0, BG1, SC0, SC1, UU0, UU1, GG0, GG1, MM0, MM1, LR0, LR1) = range(24)
NSLOT = 24
EXPU, S1Q, S2Q, AAQ, LAMQ, EMQ = range(6)


class CFG:
    def __init__(self, S):
        self.S = S
        self.TOK = S // 4
        self.BLK = 256
        self.NBLK = S // self.BLK
        self.CPB = 2
        self.NTT = self.TOK // 128
        self.TTH = self.NTT // 2
        self.THALF = self.TOK // 2


def build_program(cfg, debug=False):
    S, TOK, BLK, NBLK, CPB, NTT, TTH, THALF = (cfg.S, cfg.TOK, cfg.BLK, cfg.NBLK,
                                               cfg.CPB, cfg.NTT, cfg.TTH, cfg.THALF)
    nc = bass.Bass("TRN2", target_bir_lowering=False)

    def din(name, shape, dt=F32):
        return nc.dram_tensor(name, shape, dt, kind="ExternalInput").ap()

    xb = din("xb", [S, D])
    xresd = din("xresd", [TOK, D])
    wqT = din("wqT", [128, NKD * 256])
    wkT = din("wkT", [128, NKD * 256])
    wvT = din("wvT", [128, NKD * 512])
    wogT = din("wogT", [128, NKD * 512])
    wifT = din("wifT", [128, NKD * 4])
    bifr = din("bifr", [1, 4])
    gammag = din("gammag", [1, 512])
    wmod_sl = din("wmod_sl", [D, 6 * 512])
    gwTd = din("gwTd", [D, E_])
    gb2d = din("gb2d", [1, E_])
    gbiasd = din("gbiasd", [1, E_])
    b1e = din("b1e", [E_, H_])
    b2e = din("b2e", [E_, D])
    cvecd = din("cvecd", [1, D])
    wffn = din("wffn", [WFSL])

    out = nc.dram_tensor("out", [TOK, D], F32, kind="ExternalOutput").ap()
    dbg = {}
    if debug:
        for nm, shp, dt in [("dbg_xhat", [128, D], F32),
                            ("dbg_hq", [128, 257], F32),
                            ("dbg_x2", [128, D], F32),
                            ("dbg_col", [128, 12, 2], F32),
                            ("dbg_gw3", [2, NSLOT, 128], F32),
                            ("dbg_state", [128, 257], F32),
                            ("dbg_gq", [2, 12, 128], F32),
                            ("dbg_comb", [128, E_], F32)]:
            dbg[nm] = nc.dram_tensor(nm, shp, dt, kind="ExternalOutput").ap()

    mod_in = nc.dram_tensor("mod_in", [1, 6 * 512], F32, kind="Internal")
    mod_out = nc.dram_tensor("mod_out", [4, 6 * 512], F32, kind="Internal")
    hout_in = nc.dram_tensor("hout_in", [512, S], BF16, kind="Internal")
    hout_out = nc.dram_tensor("hout_out", [2048, S], BF16, kind="Internal")
    x2d = nc.dram_tensor("x2d", [cfg.TOK, D], F32, kind="Internal")
    wffn_my = nc.dram_tensor("wffn_my", [WFSL], BF16, kind="Internal")
    wffn_bf = nc.dram_tensor("wffn_bf", [WFTOT], BF16, kind="Internal")

    ident_f = nc.inline_tensor(np.eye(128, dtype=np.float32), "ident_f")
    ident_b = nc.inline_tensor(np.eye(128, dtype=np.float32).astype(ml_dtypes.bfloat16), "ident_b")
    masksc_t = nc.inline_tensor(np.triu(np.full((128, 128), QKSCALE, np.float32)), "masksc")

    with ExitStack() as ctx:
        Et = ctx.enter_context

        class Sem:
            def __init__(self, name):
                self.h = Et(nc.semaphore(name))
                self.n = 0

        _sem_names = ["LC", "LCP", "LX0", "LX1", "EX", "LH", "LT0", "LT1",
                      "SX0", "SX1", "SO0", "SO1", "GC", "GM0", "GM1",
                      "W10", "W11", "W20", "W21", "TE", "VE", "SE", "CC", "DB"]
        SEMS = {k: Sem(k) for k in _sem_names}
        (LC, LCP, LX0, LX1, EX, LH, LT0, LT1, SX0, SX1, SO0, SO1, GC, GM0,
         GM1, W10, W11, W20, W21, TE, VE, SE, CC, DB) = (SEMS[k] for k in _sem_names)
        LX = [LX0, LX1]
        LT = [LT0, LT1]
        SX = [SX0, SX1]
        SO = [SO0, SO1]
        GM = [GM0, GM1]
        W1S = [W10, W11]
        W2S = [W20, W21]

        def snap():
            return {k: s.n for k, s in SEMS.items()}

        def waits(eng, ss, only):
            for k in only:
                if ss[k] > 0:
                    eng.wait_ge(SEMS[k].h, ss[k])

        def I(sem, instr, k=1):
            instr.then_inc(sem.h, k)
            sem.n += k
            return instr

        onesf = Et(nc.sbuf_tensor("onesf", [1, 512], F32))
        onesb = Et(nc.sbuf_tensor("onesb", [1, 128], BF16))
        ones2 = Et(nc.sbuf_tensor("ones2", [2, 128], F32))
        epsc = Et(nc.sbuf_tensor("epsc", [128, 2], F32))
        ident_fs = Et(nc.sbuf_tensor("ident_fs", [128, 128], F32))
        ident_bs = Et(nc.sbuf_tensor("ident_bs", [128, 128], BF16))
        modcol = Et(nc.sbuf_tensor("modcol", [128, 4, NKD], F32))
        garep = Et(nc.sbuf_tensor("garep", [128, D], F32))
        gbrep = Et(nc.sbuf_tensor("gbrep", [128, D], F32))
        b1col = Et(nc.sbuf_tensor("b1col", [128, E_, NHT], F32))
        b2p = Et(nc.sbuf_tensor("b2p", [E_, D], BF16))
        gbs = Et(nc.sbuf_tensor("gbs", [1, 2 * E_], F32))
        gw_s = Et(nc.sbuf_tensor("gw_s", [128, NKD, E_], BF16))
        garow = Et(nc.sbuf_tensor("garow", [1, 512], F32))
        gascr = Et(nc.sbuf_tensor("gascr", [1, 512], F32))
        garo2 = Et(nc.sbuf_tensor("garo2", [1, 512], F32))
        gamrow = Et(nc.sbuf_tensor("gamrow", [1, 512], F32))
        rowcb = Et(nc.sbuf_tensor("rowcb", [1, 4], BF16))

        P = [Et(nc.psum_tensor(f"P{i}", [128, 512], F32)) for i in range(7)]

        blk = Et(nc.Block())

        @blk.vector
        def _(v):
            I(VE, v.memset(onesf[:, :], 1.0))
            I(VE, v.memset(onesb[:, :], 1.0))
            I(VE, v.memset(ones2[:, :], 1.0))
            I(VE, v.memset(epsc[:, 0:1], EPS1))
            I(VE, v.memset(epsc[:, 1:2], EPS2))

        rowc = Et(nc.sbuf_tensor("rowc", [1, 4], F32))

        @blk.sync
        def _(sy):
            I(LC, sy.dma_start(out=ident_fs[:, :], in_=ident_f.ap()), 16)
            I(LC, sy.dma_start(out=ident_bs[:, :], in_=ident_b.ap()), 16)
            I(LC, sy.dma_start(out=gbs[0:1, 0:E_], in_=gb2d[:, :]), 16)
            I(LC, sy.dma_start(out=gbs[0:1, E_:2 * E_], in_=gbiasd[:, :]), 16)
            I(LC, sy.dma_start(out=gamrow[0:1, :], in_=gammag[:, :]), 16)
            I(LC, sy.dma_start(out=rowc[:, :], in_=bifr[:, :]), 16)

        cvcol = Et(nc.sbuf_tensor("cvcol", [128, NKD], F32))

        @blk.sync
        def _(sy):
            I(LC, sy.dma_start(out=cvcol[:, :],
                               in_=cvecd[0, :].rearrange("(k p) -> p k", p=128),
                               allow_slow_non_contiguous=True), 16)

        @blk.sync
        def _(sy):
            for e in range(E_):
                I(LC, sy.dma_start(out=b1col[:, e, :],
                                   in_=b1e[e, :].rearrange("(h p) -> p h", p=128),
                                   allow_slow_non_contiguous=True), 16)
        ss_consts = snap()

        @blk.vector
        def _(v):
            waits(v, ss_consts, ["LC"])
            I(VE, v.tensor_copy(rowcb[:, :], rowc[:, :]))


        siluc = Et(nc.sbuf_tensor("siluc", [128, NKD], F32))
        modmy = Et(nc.sbuf_tensor("modmy", [1, 2, 512], F32))

        @blk.scalar
        def _(s):
            waits(s, ss_consts, ["LC"])
            I(SE, s.activation(siluc[:, :], cvcol[:, :], AF.Exp, bias=0.0, scale=-1.0))
        ss1 = snap()

        @blk.vector
        def _(v):
            waits(v, ss1, ["SE", "LC"])
            I(VE, v.tensor_scalar(siluc[:, :], siluc[:, :], 1.0, None, OP.add))
            v.wait_ge(VE.h, VE.n)
            I(VE, v.reciprocal(siluc[:, :], siluc[:, :]))
            v.wait_ge(VE.h, VE.n)
            I(VE, v.tensor_mul(siluc[:, :], siluc[:, :], cvcol[:, :]))
        ss_silu = snap()

        with nc.sbuf_tensor("modw0", [128, 6 * 512], F32) as modw0, \
             nc.sbuf_tensor("modw1", [128, 6 * 512], F32) as modw1:
            modw = [modw0, modw1]
            te0m = TE.n

            @blk.sync
            def _(sy):
                for k in range(NKD):
                    if k >= 2:
                        sy.wait_ge(TE.h, te0m + (k - 1))
                    I(GM[k % 2], sy.dma_start(out=modw[k % 2][:, :],
                                              in_=wmod_sl[k * 128:(k + 1) * 128, :]), 16)

            @blk.tensor
            def _(t):
                waits(t, ss_silu, ["VE"])
                for k in range(NKD):
                    t.wait_ge(GM[k % 2].h, 16 * (k // 2 + 1))
                    for cc in range(6):
                        mi = t.matmul(P[cc][0:1, :], siluc[:, k:k + 1],
                                      modw[k % 2][:, cc * 512:(cc + 1) * 512],
                                      start=(k == 0), stop=(k == NKD - 1))
                        if cc == 5:
                            I(TE, mi)
            ss_modmm = snap()

            for cc in range(6):
                @blk.vector
                def _(v, cc=cc, sw=snap()):
                    waits(v, sw, ["TE"])
                    if cc >= 2:
                        v.wait_ge(LX[cc % 2].h, mod_st_ck[cc - 2])
                    I(VE, v.tensor_copy(modmy[0:1, cc % 2, :], P[cc][0:1, :]))
                sse = snap()

                @blk.sync
                def _(sy, cc=cc, sse=sse):
                    waits(sy, sse, ["VE"])
                    I(LX[cc % 2], sy.dma_start(out=mod_in.ap()[0, cc * 512:(cc + 1) * 512][None, :],
                                               in_=modmy[0:1, cc % 2, :]), 16)
                if cc == 0:
                    mod_st_ck = {}
                mod_st_ck[cc] = LX[cc % 2].n
            ss_modst = snap()

            @blk.gpsimd
            def _(gp):
                gp.wait_ge(LX0.h, ss_modst["LX0"])
                gp.wait_ge(LX1.h, ss_modst["LX1"])
                I(CC, gp.collective_compute("AllGather", OP.bypass,
                                            replica_groups=[[0, 1, 2, 3], [4, 5, 6, 7]],
                                            ins=[mod_in.ap().opt()], outs=[mod_out.ap().opt()]))
            ss_modag = snap()

            @blk.sync
            def _(sy):
                waits(sy, ss_modag, ["CC"])
                for slot, q in [(0, 0), (1, 1), (2, 3), (3, 4)]:
                    for jj in range(4):
                        I(LC, sy.dma_start(out=modcol[:, slot, jj * 4:(jj + 1) * 4],
                                           in_=mod_out.ap()[jj, q * 512:(q + 1) * 512]
                                           .rearrange("(k p) -> p k", p=128),
                                           allow_slow_non_contiguous=True), 16)
            prev_mq = snap()

            @blk.vector
            def _(v, pm=prev_mq):
                waits(v, pm, ["VE", "LC"])
                I(VE, v.tensor_scalar(modcol[:, 1, :], modcol[:, 1, :], 1.0, None, OP.add))
                I(VE, v.tensor_scalar(modcol[:, 3, :], modcol[:, 3, :], 1.0, None, OP.add))
            ss_ca1 = snap()

            prev_g = ss_ca1
            for qi, q in [(0, 2), (1, 5)]:
                for jj in range(4):
                    @blk.sync
                    def _(sy, q=q, jj=jj, pg=prev_g):
                        waits(sy, pg, ["VE"])
                        I(LC, sy.dma_start(out=garow[0:1, :],
                                           in_=mod_out.ap()[jj, q * 512:(q + 1) * 512][None, :]), 16)
                    ssg1 = snap()

                    @blk.vector
                    def _(v, ssg1=ssg1):
                        waits(v, ssg1, ["LC"])
                        I(VE, v.tensor_scalar(gascr[0:1, :], garow[0:1, :], 0.0, None, OP.is_ge))
                        v.wait_ge(VE.h, VE.n)
                        I(VE, v.tensor_scalar(gascr[0:1, :], gascr[0:1, :], 2.0, None, OP.mult))
                        v.wait_ge(VE.h, VE.n)
                        I(VE, v.tensor_scalar(gascr[0:1, :], gascr[0:1, :], -1.0, None, OP.add))
                        I(VE, v.tensor_scalar(garo2[0:1, :], garow[0:1, :], -1.0, None, OP.mult))
                        v.wait_ge(VE.h, VE.n)
                        I(VE, v.tensor_max(garow[0:1, :], garow[0:1, :], garo2[0:1, :]))
                        v.wait_ge(VE.h, VE.n)
                        I(VE, v.tensor_scalar(garow[0:1, :], garow[0:1, :], 0.25, None, OP.max))
                        v.wait_ge(VE.h, VE.n)
                        I(VE, v.tensor_mul(garow[0:1, :], garow[0:1, :], gascr[0:1, :]))
                    ssg2 = snap()

                    @blk.tensor
                    def _(t, ssg2=ssg2):
                        waits(t, ssg2, ["VE"])
                        I(TE, t.matmul(P[0][:, :], onesf[0:1, 0:128], garow[0:1, :],
                                       start=True, stop=True))
                    ssg3 = snap()

                    @blk.vector
                    def _(v, qi=qi, jj=jj, ssg3=ssg3):
                        waits(v, ssg3, ["TE"])
                        if qi == 0:
                            I(VE, v.tensor_copy(garep[:, jj * 512:(jj + 1) * 512], P[0][:, :]))
                        else:
                            I(VE, v.tensor_copy(gbrep[:, jj * 512:(jj + 1) * 512], P[0][:, :]))
                    prev_g = snap()
            ss_garep = prev_g

            @blk.tensor
            def _(t):
                waits(t, ss_garep, ["VE"])
                I(TE, t.matmul(P[1][:, :], onesf[0:1, 0:128], gamrow[0:1, :],
                               start=True, stop=True))
            ss_rep2 = snap()

        gamr = Et(nc.sbuf_tensor("gamr", [128, 512], F32))

        @blk.vector
        def _(v):
            waits(v, ss_rep2, ["TE"])
            I(VE, v.tensor_copy(gamr[:, :], P[1][:, :]))
            I(VE, v.tensor_add(gbs[0:1, 0:E_], gbs[0:1, 0:E_], gbs[0:1, E_:2 * E_]))
            v.wait_ge(VE.h, VE.n)
        ss_prol = snap()

        with ExitStack() as p1:
            E1 = p1.enter_context
            b2f = E1(nc.sbuf_tensor("b2f", [E_, D], F32))

            @blk.sync
            def _(sy, sw=snap()):
                waits(sy, sw, ["LC"])
                I(LC, sy.dma_start(out=b2f[:, :], in_=b2e[:, :]), 16)
            ss_b2l = snap()

            @blk.vector
            def _(v, ss_b2l=ss_b2l):
                waits(v, ss_b2l, ["LC"])
                waits(v, ss_prol, ["VE"])
                I(VE, v.tensor_mul(b2p[:, :], b2f[:, :], gbrep[0:E_, :]))

            xblk = [E1(nc.sbuf_tensor(f"xblk{i}", [128, CPB, D], F32)) for i in range(2)]
            hx = [E1(nc.sbuf_tensor(f"hx{i}", [128, NKD, BLK], BF16)) for i in range(2)]
            masks = E1(nc.sbuf_tensor("masks", [128, 128], F32))
            wq_s = E1(nc.sbuf_tensor("wq_s", [128, NKD, 256], BF16))
            wk_s = E1(nc.sbuf_tensor("wk_s", [128, NKD, 256], BF16))
            wv_s = E1(nc.sbuf_tensor("wv_s", [128, NKD, 512], BF16))
            wo_s = E1(nc.sbuf_tensor("wo_s", [128, NKD, 512], BF16))
            wif_s = E1(nc.sbuf_tensor("wif_s", [128, NKD, 4], BF16))
            qb = E1(nc.sbuf_tensor("qb", [128, 2, BLK], F32))
            kb = E1(nc.sbuf_tensor("kb", [128, 2, BLK], F32))
            vb = E1(nc.sbuf_tensor("vb", [128, CPB, 2, 257], F32))
            sgb = E1(nc.sbuf_tensor("sgb", [128, CPB, 2, DV], F32))
            sqs = E1(nc.sbuf_tensor("sqs", [128, D], BF16))
            sts = E1(nc.sbuf_tensor("sts", [128, 10], F32))
            trin = E1(nc.sbuf_tensor("trin", [128, 128], F32))
            gw3 = E1(nc.sbuf_tensor("gw3", [2, NSLOT, 128], F32))
            gq = E1(nc.sbuf_tensor("gq", [2, 12, 128], F32))
            col_s = E1(nc.sbuf_tensor("col_s", [128, 12, CPB], F32))
            cin = E1(nc.sbuf_tensor("cin", [2, 2, 4], F32))
            r1 = E1(nc.sbuf_tensor("r1", [1, 2, 4], F32))
            scr0 = E1(nc.sbuf_tensor("scr0", [1, 2, 8], F32))
            colv = E1(nc.sbuf_tensor("colv", [2, 2, 4], F32))
            carry = E1(nc.sbuf_tensor("carry", [1, 2, 2], F32))
            state = E1(nc.sbuf_tensor("state", [128, 2, 257], F32))
            pp_s = E1(nc.sbuf_tensor("pp_s", [128, 128], F32))
            ak_s = E1(nc.sbuf_tensor("ak_s", [128, 128], F32))
            hq_s = E1(nc.sbuf_tensor("hq_s", [128, 257], F32))
            tmp1 = E1(nc.sbuf_tensor("tmp1", [128, 257], F32))
            hNb = E1(nc.sbuf_tensor("hNb", [128, CPB, 2, DV], F32))
            nsqb = E1(nc.sbuf_tensor("nsqb", [128, CPB * 2 * DV], F32))
            nstat = E1(nc.sbuf_tensor("nstat", [128, 8], F32))
            nst2 = E1(nc.sbuf_tensor("nst2", [128, 16], F32))
            exb = E1(nc.sbuf_tensor("exb", [128, 4, BLK], BF16))
            PBt = E1(nc.psum_tensor("PBt", [128, 256], F32))

            @blk.sync
            def _(sy):
                I(LCP, sy.dma_start(out=masks[:, :], in_=masksc_t.ap()), 16)

            @blk.gpsimd
            def _(gp):
                for dst, src in [(wq_s, wqT), (wk_s, wkT), (wv_s, wvT),
                                 (wo_s, wogT), (wif_s, wifT)]:
                    I(GC, gp.dma_start(out=dst[:, :, :],
                                       in_=src.rearrange("p (k w) -> p k w", k=NKD)), 16)
                I(GC, gp.dma_start(out=gw_s[:, :, :],
                                   in_=gwTd.rearrange("(k p) w -> p k w", p=128)), 16)
            ss_wattn = snap()

            @blk.gpsimd
            def _(gp):
                waits(gp, ss_wattn, ["GC"])
                q4 = WFSL // 4
                for i in range(4):
                    I(GC, gp.dma_start(out=wffn_my.ap()[i * q4:(i + 1) * q4],
                                       in_=wffn[i * q4:(i + 1) * q4]), 16)
            ss_wcast2 = snap()

            @blk.gpsimd
            def _(gp):
                gp.wait_ge(GC.h, ss_wcast2["GC"])
                I(CC, gp.collective_compute("AllGather", OP.bypass,
                                            replica_groups=[list(range(8))],
                                            ins=[wffn_my.ap().opt()], outs=[wffn_bf.ap().opt()]))
            ss_wfag = snap()

            @blk.vector
            def _(v):
                I(VE, v.memset(gw3[0:2, :, :], 0.0))
                I(VE, v.memset(gq[0:2, :, :], 0.0))
                I(VE, v.memset(trin[:, :], 0.0))
                I(VE, v.memset(carry[0:1, :, :], 0.0))
                I(VE, v.memset(state[:, :, :], 0.0))
                I(VE, v.memset(cin[0:2, :, :], 0.0))
                for j in range(CPB):
                    for h in range(2):
                        I(VE, v.memset(vb[:, j, h, 256:257], 1.0))

            xld_ck = {}

            @blk.sync
            def _(sy):
                for j in range(CPB):
                    I(LX0, sy.dma_start(out=xblk[0][:, j, :], in_=xb[j * 128:(j + 1) * 128, :]), 16)
            xld_ck[0] = LX0.n
            ss_p1init = snap()


            Fsn = {}

            def emit_F1(k, full_ssb=None):
                par = k % 2

                @blk.scalar
                def _(s, par=par, k=k, full_ssb=full_ssb):
                    if full_ssb is not None:
                        waits(s, full_ssb, ["LC", "LCP", "VE", "TE", "EX"])
                    s.wait_ge(LX[k % 2].h, xld_ck[k])
                    for j in range(CPB):
                        I(SE, s.activation(sqs[:, :], xblk[par][:, j, :], AF.Square,
                                           accum_out=sts[:, j:j + 1]))
                        s.wait_ge(SE.h, SE.n)
                ss_ac = snap()

                @blk.vector
                def _(v, par=par, k=k, ss_ac=ss_ac, full_ssb=full_ssb):
                    if full_ssb is not None:
                        waits(v, full_ssb, ["LC", "LCP", "SE", "TE"])
                    v.wait_ge(LX[k % 2].h, xld_ck[k])
                    for j in range(CPB):
                        I(VE, v.reduce_sum(sts[:, 2 + j:3 + j], xblk[par][:, j, :], axis=AX))
                    v.wait_ge(VE.h, VE.n)
                    waits(v, ss_ac, ["SE"])
                    I(VE, v.tensor_scalar(sts[:, 4:6], sts[:, 2:4], 1.0 / D, None, OP.mult))
                    I(VE, v.tensor_scalar(sts[:, 8:10], sts[:, 0:2], 1.0 / D, None, OP.mult))
                    v.wait_ge(VE.h, VE.n)
                    I(VE, v.tensor_mul(sts[:, 6:8], sts[:, 4:6], sts[:, 4:6]))
                    v.wait_ge(VE.h, VE.n)
                    I(VE, v.tensor_sub(sts[:, 6:8], sts[:, 6:8], sts[:, 8:10]))
                    v.wait_ge(VE.h, VE.n)
                ss_mv = snap()

                @blk.scalar
                def _(s, ss_mv=ss_mv):
                    waits(s, ss_mv, ["VE"])
                    I(SE, s.activation(sts[:, 6:8], sts[:, 6:8],
                                       AF.Ln, bias=epsc[:, 0:1], scale=-1.0))
                    s.wait_ge(SE.h, SE.n)
                    I(SE, s.activation(sts[:, 6:8], sts[:, 6:8],
                                       AF.Exp, bias=0.0, scale=-0.5))
                    s.wait_ge(SE.h, SE.n)
                ss_r = snap()

                @blk.vector
                def _(v, par=par, ss_r=ss_r):
                    waits(v, ss_r, ["SE"])
                    for j in range(CPB):
                        I(VE, v.tensor_scalar(xblk[par][:, j, :], xblk[par][:, j, :],
                                              sts[:, 4 + j:5 + j],
                                              sts[:, 6 + j:7 + j],
                                              OP.subtract, OP.mult))
                Fsn[k] = {"xh": snap()}

            def emit_F2(k, ss_colev):
                par = k % 2
                ss_xh = Fsn[k]["xh"]

                if debug and k == 0:
                    @blk.sync
                    def _(sy, ss_xh=ss_xh):
                        waits(sy, ss_xh, ["VE"])
                        I(DB, sy.dma_start(out=dbg["dbg_xhat"][:, :], in_=xblk[0][:, 0, :]), 16)
                    _ssdbg = snap()

                    @blk.vector
                    def _(v, _ssdbg=_ssdbg):
                        waits(v, _ssdbg, ["DB"])
                        I(VE, v.engine_nop())

                te0 = TE.n
                se0 = SE.n

                @blk.tensor
                def _(t, par=par, ss_xh=ss_xh, ss_colev=ss_colev, te0=te0, se0=se0):
                    waits(t, ss_xh, ["VE", "SE"])
                    waits(t, ss_colev, ["VE"])
                    i = 0
                    for j in range(CPB):
                        for kk in range(NKD):
                            if i >= 2:
                                t.wait_ge(SE.h, se0 + i - 1)
                            I(TE, t.transpose(P[i % 2][0:128, 0:128],
                                              xblk[par][:, j, kk * 128:(kk + 1) * 128],
                                              ident_fs[:, :]))
                            i += 1

                @blk.scalar
                def _(s, par=par, te0=te0):
                    i = 0
                    for j in range(CPB):
                        for kk in range(NKD):
                            s.wait_ge(TE.h, te0 + i + 1)
                            I(SE, s.activation(hx[par][:, kk, j * 128:(j + 1) * 128],
                                               P[i % 2][0:128, 0:128], AF.Identity,
                                               bias=modcol[:, 0, kk:kk + 1],
                                               scale=modcol[:, 1, kk:kk + 1]))
                            i += 1
                assert SE.n == se0 + CPB * NKD
                Fsn[k]["hx"] = snap()

            prev = ss_p1init
            for k in range(NBLK):
                par = k % 2
                T0 = k * BLK
                ssb = prev

                if k + 1 < NBLK:
                    @blk.sync
                    def _(sy, k=k, ssb=ssb):
                        waits(sy, ssb, ["TE"])
                        for j in range(CPB):
                            I(LX[(k + 1) % 2], sy.dma_start(out=xblk[(k + 1) % 2][:, j, :],
                                               in_=xb[(k + 1) * BLK + j * 128:
                                                      (k + 1) * BLK + (j + 1) * 128, :]), 16)
                    xld_ck[k + 1] = LX[(k + 1) % 2].n

                if k == 0:
                    emit_F1(0, full_ssb=ssb)
                    emit_F2(0, ssb)

                @blk.tensor
                def _(t, par=par, k=k, ssb=ssb):
                    waits(t, Fsn[k]["hx"], ["SE"])
                    waits(t, ssb, ["VE"])
                    waits(t, ss_wattn, ["GC"])
                    for h in range(2):
                        for kk in range(NKD):
                            mi = t.matmul(P[2][:, h * 256:h * 256 + BLK],
                                          wq_s[:, kk, h * 128:(h + 1) * 128], hx[par][:, kk, :],
                                          start=(kk == 0), stop=(kk == NKD - 1))
                            if kk == NKD - 1:
                                I(TE, mi)
                        for kk in range(NKD):
                            mi = t.matmul(P[3][:, h * 256:h * 256 + BLK],
                                          wk_s[:, kk, h * 128:(h + 1) * 128], hx[par][:, kk, :],
                                          start=(kk == 0), stop=(kk == NKD - 1))
                            if kk == NKD - 1:
                                I(TE, mi)
                ss_qk = snap()

                @blk.vector
                def _(v, ss_qk=ss_qk):
                    waits(v, ss_qk, ["TE"])
                    for h in range(2):
                        I(VE, v.tensor_copy(qb[:, h, :], P[2][:, h * 256:h * 256 + BLK]))
                        I(VE, v.tensor_copy(kb[:, h, :], P[3][:, h * 256:h * 256 + BLK]))
                ss_qke = snap()

                prev_vog = ss_qke
                for j in range(CPB):
                    @blk.tensor
                    def _(t, par=par, j=j, k=k, ssb=ssb, pv=prev_vog):
                        if j == 0:
                            waits(t, Fsn[k]["hx"], ["SE"])
                            waits(t, ssb, ["VE"])
                        else:
                            waits(t, pv, ["VE", "SE"])
                        waits(t, ss_wattn, ["GC"])
                        for kk in range(NKD):
                            mi = t.matmul(P[4][:, 0:512], hx[par][:, kk, j * 128:(j + 1) * 128],
                                          wv_s[:, kk, :], start=(kk == 0), stop=(kk == NKD - 1))
                            if kk == NKD - 1:
                                I(TE, mi)
                            mi = t.matmul(P[5][:, 0:512], hx[par][:, kk, j * 128:(j + 1) * 128],
                                          wo_s[:, kk, :], start=(kk == 0), stop=(kk == NKD - 1))
                            if kk == NKD - 1:
                                I(TE, mi)
                            t.matmul(P[6][0:128, 0:4], hx[par][:, kk, j * 128:(j + 1) * 128],
                                     wif_s[:, kk, :], start=(kk == 0), stop=False)
                        I(TE, t.matmul(P[6][0:128, 0:4], onesb[0:1, 0:128], rowcb[0:1, :],
                                       start=False, stop=True))
                    ss_v1 = snap()

                    @blk.vector
                    def _(v, j=j, ss_v1=ss_v1):
                        waits(v, ss_v1, ["TE"])
                        for h in range(2):
                            I(VE, v.tensor_copy(vb[:, j, h, 0:256], P[4][:, h * 256:(h + 1) * 256]))
                    ss_v2 = snap()

                    @blk.scalar
                    def _(s, j=j, ss_v1=ss_v1):
                        waits(s, ss_v1, ["TE"])
                        I(SE, s.activation(sgb[:, j, 0, :], P[5][:, 0:256], AF.Sigmoid,
                                           bias=0.0, scale=1.0))
                        I(SE, s.activation(sgb[:, j, 1, :], P[5][:, 256:512], AF.Sigmoid,
                                           bias=0.0, scale=1.0))
                        I(SE, s.activation(trin[:, j:j + 97:32], P[6][0:128, 0:4],
                                           AF.Tanh, bias=0.0, scale=1.0 / 15.0))
                    prev_vog = snap()
                ss_vog = prev_vog

                @blk.tensor
                def _(t, ss_vog=ss_vog):
                    waits(t, ss_vog, ["SE", "VE"])
                    I(TE, t.transpose(P[0][0:128, 0:128], trin[:, :], ident_fs[:, :]))
                ss_gt = snap()

                @blk.vector
                def _(v, ss_gt=ss_gt):
                    waits(v, ss_gt, ["TE"])
                    I(VE, v.tensor_copy(gw3[0:2, TI0, :], P[0][0:2, 0:128]))
                    I(VE, v.tensor_copy(gw3[0:2, TF0, :], P[0][32:34, 0:128]))
                    I(VE, v.tensor_copy(gw3[0:2, TI1, :], P[0][64:66, 0:128]))
                    I(VE, v.tensor_copy(gw3[0:2, TF1, :], P[0][96:98, 0:128]))
                ss_gr = snap()

                @blk.scalar
                def _(s, ss_gr=ss_gr):
                    waits(s, ss_gr, ["VE"])
                    for h in range(2):
                        I(SE, s.activation(gw3[0:2, SC0 + h, :], gw3[0:2, TF0 + 2 * h, :],
                                           AF.Exp, bias=0.0, scale=-15.0))
                        s.wait_ge(SE.h, SE.n)
                        I(SE, s.activation(gw3[0:2, SC0 + h, :], gw3[0:2, SC0 + h, :],
                                           AF.Ln, bias=1.0, scale=1.0))
                        s.wait_ge(SE.h, SE.n)
                ss_nb = snap()

                @blk.vector
                def _(v, ss_nb=ss_nb):
                    waits(v, ss_nb, ["SE"])
                    for h in range(2):
                        I(VE, v.tensor_scalar(gw3[0:2, FL0 + h, :], gw3[0:2, SC0 + h, :],
                                              -1.0, None, OP.mult))
                        I(VE, v.tensor_scalar(gw3[0:2, I50 + h, :], gw3[0:2, TI0 + 2 * h, :],
                                              15.0, None, OP.mult))
                    v.wait_ge(VE.h, VE.n)
                    for h in range(2):
                        I(VE, v.tensor_tensor_scan(gw3[0:2, BL0 + h, :], gw3[0:2, FL0 + h, :],
                                                   gw3[0:2, FL0 + h, :], 0.0, OP.add, OP.bypass))
                        I(VE, v.tensor_tensor_scan(gw3[0:2, ML0 + h, :], gw3[0:2, FL0 + h, :],
                                                   gw3[0:2, I50 + h, :], NEG, OP.add, OP.max))
                    v.wait_ge(VE.h, VE.n)
                    for h in range(2):
                        I(VE, v.tensor_copy(cin[0:2, h, 0:1], gw3[0:2, BL0 + h, 127:128]))
                        I(VE, v.tensor_copy(cin[0:2, h, 1:2], gw3[0:2, ML0 + h, 127:128]))
                ss_ce = snap()

                @blk.tensor
                def _(t, ss_ce=ss_ce):
                    waits(t, ss_ce, ["VE"])
                    I(TE, t.matmul(P[1][0:1, 0:8], ident_fs[0:2, 1:2],
                                   cin[0:2, :, :].rearrange("p a b -> p (a b)"),
                                   start=True, stop=True))
                ss_cx = snap()

                @blk.vector
                def _(v, ss_cx=ss_cx):
                    waits(v, ss_cx, ["TE"])
                    I(VE, v.tensor_copy(r1[0:1, :, :],
                                        P[1][0:1, 0:8].rearrange("p (a b) -> p a b", a=2)))
                    v.wait_ge(VE.h, VE.n)
                    for h in range(2):
                        be0 = cin[0:1, h, 0:1]
                        me0 = cin[0:1, h, 1:2]
                        be1 = r1[0:1, h, 0:1]
                        me1 = r1[0:1, h, 1:2]
                        Bc = carry[0:1, h, 0:1]
                        mc = carry[0:1, h, 1:2]
                        W = scr0[0:1, h, :]
                        I(VE, v.tensor_copy(W[:, 0:1], Bc))
                        I(VE, v.tensor_copy(W[:, 2:3], mc))
                        v.wait_ge(VE.h, VE.n)
                        I(VE, v.tensor_scalar(W[:, 1:2], be0, W[:, 0:1], None, OP.add))
                        I(VE, v.tensor_scalar(W[:, 3:4], be0, W[:, 2:3], None, OP.add))
                        v.wait_ge(VE.h, VE.n)
                        I(VE, v.tensor_tensor(W[:, 3:4], W[:, 3:4], me0, OP.max))
                        v.wait_ge(VE.h, VE.n)
                        I(VE, v.tensor_scalar(W[:, 6:7], be1, W[:, 1:2], None, OP.add))
                        I(VE, v.tensor_scalar(W[:, 7:8], be1, W[:, 3:4], None, OP.add))
                        v.wait_ge(VE.h, VE.n)
                        I(VE, v.tensor_tensor(W[:, 7:8], W[:, 7:8], me1, OP.max))
                        I(VE, v.tensor_tensor(W[:, 4:5], W[:, 2:3], W[:, 0:1], OP.subtract))
                        I(VE, v.tensor_tensor(W[:, 5:6], W[:, 3:4], W[:, 1:2], OP.subtract))
                        v.wait_ge(VE.h, VE.n)
                        I(VE, v.tensor_scalar(W[:, 4:6], W[:, 4:6], LNSC, None, OP.add))
                        v.wait_ge(VE.h, VE.n)
                        I(VE, v.tensor_copy(carry[0:1, h, 0:1], W[:, 6:7]))
                        I(VE, v.tensor_copy(carry[0:1, h, 1:2], W[:, 7:8]))
                ss_cw = snap()

                @blk.tensor
                def _(t, ss_cw=ss_cw):
                    waits(t, ss_cw, ["VE"])
                    for h in range(2):
                        t.matmul(P[1][0:2, 4 * h + 0:4 * h + 1], scr0[0:1, h, 0:2],
                                 onesf[0:1, 0:1], start=True, stop=True)
                        t.matmul(P[1][0:2, 4 * h + 1:4 * h + 2], scr0[0:1, h, 2:4],
                                 onesf[0:1, 0:1], start=True, stop=True)
                        mi = t.matmul(P[1][0:2, 4 * h + 2:4 * h + 3], scr0[0:1, h, 4:6],
                                      onesf[0:1, 0:1], start=True, stop=True)
                        if h == 1:
                            I(TE, mi)
                ss_cb = snap()

                @blk.vector
                def _(v, ss_cb=ss_cb):
                    waits(v, ss_cb, ["TE"])
                    I(VE, v.tensor_copy(colv[0:2, :, :],
                                        P[1][0:2, 0:8].rearrange("p (a b) -> p a b", a=2)))
                ss_colv = snap()

                @blk.vector
                def _(v, ss_colv=ss_colv):
                    waits(v, ss_colv, ["VE"])
                    for h in range(2):
                        I(VE, v.tensor_scalar(gw3[0:2, BG0 + h, :], gw3[0:2, BL0 + h, :],
                                              colv[0:2, h, 0:1], None, OP.add))
                        I(VE, v.tensor_scalar(gw3[0:2, SC0 + h, :], gw3[0:2, BL0 + h, :],
                                              colv[0:2, h, 1:2], None, OP.add))
                        v.wait_ge(VE.h, VE.n)
                        I(VE, v.tensor_tensor(gw3[0:2, MM0 + h, :], gw3[0:2, SC0 + h, :],
                                              gw3[0:2, ML0 + h, :], OP.max))
                        v.wait_ge(VE.h, VE.n)
                        I(VE, v.tensor_tensor(gw3[0:2, UU0 + h, :], gw3[0:2, I50 + h, :],
                                              gw3[0:2, BG0 + h, :], OP.subtract))
                        I(VE, v.tensor_tensor(gw3[0:2, GG0 + h, :], gw3[0:2, MM0 + h, :],
                                              gw3[0:2, BG0 + h, :], OP.subtract))
                        v.wait_ge(VE.h, VE.n)
                        I(VE, v.reduce_max(colv[0:2, h, 3:4], gw3[0:2, UU0 + h, :], axis=AX,
                                           negate=True))
                        v.wait_ge(VE.h, VE.n)
                ss_rows = snap()

                @blk.scalar
                def _(s, ss_rows=ss_rows):
                    waits(s, ss_rows, ["VE"])
                    for h in range(2):
                        q0 = 6 * h
                        I(SE, s.activation(gq[0:2, q0 + EXPU, :], gw3[0:2, UU0 + h, :],
                                           AF.Exp, bias=colv[0:2, h, 3:4], scale=1.0))
                        I(SE, s.activation(gq[0:2, q0 + S2Q, :], gw3[0:2, GG0 + h, :],
                                           AF.Exp, bias=colv[0:2, h, 2:3], scale=-1.0))
                        I(SE, s.activation(gq[0:2, q0 + EMQ, :], gw3[0:2, MM0 + h, :],
                                           AF.Exp, bias=0.0, scale=-1.0))
                ss_e1 = snap()

                @blk.vector
                def _(v, ss_e1=ss_e1):
                    waits(v, ss_e1, ["SE"])
                    for h in range(2):
                        I(VE, v.tensor_scalar(nstat[0:2, 4 + h:5 + h], colv[0:2, h, 3:4],
                                              -1.0, None, OP.mult))
                        I(VE, v.tensor_scalar(nstat[0:2, 6 + h:7 + h], gw3[0:2, GG0 + h, 127:128],
                                              -1.0, None, OP.mult))
                        I(VE, v.tensor_scalar(gw3[0:2, LR0 + h, 0:1], gw3[0:2, GG0 + h, 127:128],
                                              colv[0:2, h, 1:2], colv[0:2, h, 0:1],
                                              OP.subtract, OP.add))
                        v.wait_ge(VE.h, VE.n)
                ss_sm = snap()

                @blk.scalar
                def _(s, ss_sm=ss_sm):
                    waits(s, ss_sm, ["VE"])
                    for h in range(2):
                        q0 = 6 * h
                        I(SE, s.activation(gq[0:2, q0 + S1Q, :], gw3[0:2, GG0 + h, :],
                                           AF.Exp, bias=nstat[0:2, 4 + h:5 + h], scale=-1.0))
                        I(SE, s.activation(gq[0:2, q0 + AAQ, :], gw3[0:2, UU0 + h, :],
                                           AF.Exp, bias=nstat[0:2, 6 + h:7 + h], scale=1.0))
                        I(SE, s.activation(gw3[0:2, LR0 + h, 0:1], gw3[0:2, LR0 + h, 0:1],
                                           AF.Exp, bias=0.0, scale=-1.0))
                        s.wait_ge(SE.h, SE.n)
                ss_e2 = snap()

                @blk.vector
                def _(v, ss_e2=ss_e2):
                    waits(v, ss_e2, ["SE"])
                    for h in range(2):
                        q0 = 6 * h
                        I(VE, v.tensor_scalar(gq[0:2, q0 + LAMQ, :], ones2[0:2, :],
                                              gw3[0:2, LR0 + h, 0:1], None, OP.mult))
                ss_lam = snap()

                te1 = TE.n
                ve1 = VE.n

                @blk.tensor
                def _(t, ss_lam=ss_lam, ve1=ve1):
                    waits(t, ss_lam, ["VE", "SE"])
                    for qi in range(12):
                        if qi >= 2:
                            t.wait_ge(VE.h, ve1 + qi - 1)
                        I(TE, t.transpose(P[qi % 2][0:128, 0:2], gq[0:2, qi, :],
                                          ident_fs[0:2, 0:2]))

                @blk.vector
                def _(v, te1=te1):
                    for qi in range(12):
                        v.wait_ge(TE.h, te1 + qi + 1)
                        I(VE, v.tensor_copy(col_s[:, qi, :], P[qi % 2][0:128, 0:2]))
                assert VE.n == ve1 + 12
                ss_col = snap()

                if debug and k == 0:
                    @blk.sync
                    def _(sy, ss_col=ss_col):
                        waits(sy, ss_col, ["VE"])
                        I(DB, sy.dma_start(out=dbg["dbg_col"][:, :, :], in_=col_s[:, :, :]), 16)
                        I(DB, sy.dma_start(out=dbg["dbg_gw3"][:, :, :], in_=gw3[0:2, :, :]), 16)
                        I(DB, sy.dma_start(out=dbg["dbg_gq"][:, :, :], in_=gq[0:2, :, :]), 16)
                    _ssdbg = snap()

                    @blk.vector
                    def _(v, _ssdbg=_ssdbg):
                        waits(v, _ssdbg, ["DB"])
                        I(VE, v.engine_nop())

                if k + 1 < NBLK:
                    emit_F1(k + 1)

                ss_mix = ss_col
                prevu = None
                for c in range(CPB):
                    for h in range(2):
                        q0 = 6 * h

                        @blk.tensor
                        def _(t, c=c, h=h, ssm=ss_mix, prevu=prevu):
                            if prevu is None:
                                waits(t, ssm, ["VE", "SE"])
                            else:
                                waits(t, prevu[0], ["VE"])
                                waits(t, prevu[1], ["VE"])
                            I(TE, t.matmul(P[2][0:128, 0:128], kb[:, h, c * 128:(c + 1) * 128],
                                           qb[:, h, c * 128:(c + 1) * 128],
                                           start=True, stop=True))
                            I(TE, t.transpose(PBt[0:128, 0:128],
                                              kb[:, h, c * 128:(c + 1) * 128], ident_fs[:, :]))
                        ss_m1 = snap()

                        @blk.vector
                        def _(v, c=c, q0=q0, ss_m1=ss_m1):
                            waits(v, ss_m1, ["TE"])
                            I(VE, v.scalar_tensor_tensor(pp_s[:, :], P[2][0:128, 0:128],
                                                         col_s[:, q0 + EXPU, c:c + 1],
                                                         masks[:, :], OP.mult, OP.mult))
                        ss_pp = snap()

                        @blk.vector
                        def _(v, c=c, q0=q0, ss_m1=ss_m1):
                            waits(v, ss_m1, ["TE"])
                            I(VE, v.tensor_scalar(ak_s[:, :], PBt[0:128, 0:128],
                                                  col_s[:, q0 + AAQ, c:c + 1], None, OP.mult))
                        ss_ak = snap()
                        prevu = (ss_pp, ss_ak)

                        @blk.tensor
                        def _(t, c=c, h=h, ss_pp=ss_pp, ss_ak=ss_ak):
                            waits(t, ss_pp, ["VE"])
                            waits(t, ss_ak, ["VE"])
                            I(TE, t.matmul(P[3][0:128, 0:257], pp_s[:, :], vb[:, c, h, :],
                                           start=True, stop=True))
                            I(TE, t.matmul(P[4][0:128, 0:257],
                                           qb[:, h, c * 128:(c + 1) * 128],
                                           state[:, h, :], start=True, stop=True))
                            I(TE, t.matmul(P[5][0:128, 0:257], ak_s[:, :], vb[:, c, h, :],
                                           start=True, stop=True))
                        ss_m2 = snap()

                        @blk.vector
                        def _(v, c=c, h=h, q0=q0, ss_m2=ss_m2):
                            waits(v, ss_m2, ["TE"])
                            I(VE, v.tensor_scalar(tmp1[:, :], P[3][0:128, 0:257],
                                                  col_s[:, q0 + S1Q, c:c + 1], None, OP.mult))
                            v.wait_ge(VE.h, VE.n)
                            I(VE, v.scalar_tensor_tensor(hq_s[:, :], P[4][0:128, 0:257],
                                                         col_s[:, q0 + S2Q, c:c + 1],
                                                         tmp1[:, :], OP.mult, OP.add))
                            I(VE, v.scalar_tensor_tensor(state[:, h, :], state[:, h, :],
                                                         col_s[:, q0 + LAMQ, c:c + 1],
                                                         P[5][0:128, 0:257], OP.mult, OP.add))
                            v.wait_ge(VE.h, VE.n)
                            I(VE, v.tensor_scalar(nstat[:, 0:1], hq_s[:, 256:257],
                                                  -1.0, None, OP.mult))
                            v.wait_ge(VE.h, VE.n)
                            I(VE, v.tensor_max(nstat[:, 0:1], nstat[:, 0:1],
                                               hq_s[:, 256:257]))
                            v.wait_ge(VE.h, VE.n)
                            I(VE, v.tensor_scalar(nstat[:, 0:1], nstat[:, 0:1],
                                                  col_s[:, q0 + EMQ, c:c + 1], EPSD,
                                                  OP.max, OP.add))
                            v.wait_ge(VE.h, VE.n)
                            I(VE, v.reciprocal(nstat[:, 1:2], nstat[:, 0:1]))
                            v.wait_ge(VE.h, VE.n)
                            I(VE, v.tensor_scalar(hNb[:, c, h, :], hq_s[:, 0:256],
                                                  nstat[:, 1:2], None, OP.mult))
                            v.wait_ge(VE.h, VE.n)
                        ss_m3 = snap()

                        if debug and k == 1 and c == 0 and h == 0:
                            @blk.sync
                            def _(sy, ss_m3=ss_m3):
                                waits(sy, ss_m3, ["VE"])
                                I(DB, sy.dma_start(out=dbg["dbg_hq"][:, :], in_=hq_s[:, :]), 16)
                            _ssdbg = snap()

                            @blk.vector
                            def _(v, _ssdbg=_ssdbg):
                                waits(v, _ssdbg, ["DB"])
                                I(VE, v.engine_nop())

                        ss_mix = snap()

                if k + 1 < NBLK:
                    emit_F2(k + 1, ss_col)

                @blk.vector
                def _(v, ss_mix=ss_mix):
                    waits(v, ss_mix, ["VE", "SE"])
                    hv = hNb[:, :, :, :].rearrange("p a b c -> p (a b c)")
                    I(VE, v.tensor_mul(nsqb[:, :], hv, hv))
                    v.wait_ge(VE.h, VE.n)
                    I(VE, v.reduce_sum(nst2[:, 0:4],
                                       nsqb[:, :].rearrange("p (a c) -> p a c", a=4), axis=AX))
                    I(VE, v.reduce_sum(nst2[:, 4:8],
                                       hNb[:, :, :, :].rearrange("p a b c -> p (a b) c"), axis=AX))
                    v.wait_ge(VE.h, VE.n)
                    I(VE, v.tensor_scalar(nst2[:, 8:12], nst2[:, 4:8], 1.0 / DV, None, OP.mult))
                    I(VE, v.tensor_scalar(nst2[:, 12:16], nst2[:, 0:4], 1.0 / DV, None, OP.mult))
                    v.wait_ge(VE.h, VE.n)
                    I(VE, v.tensor_mul(nst2[:, 0:4], nst2[:, 8:12], nst2[:, 8:12]))
                    v.wait_ge(VE.h, VE.n)
                    I(VE, v.tensor_sub(nst2[:, 0:4], nst2[:, 0:4], nst2[:, 12:16]))
                    v.wait_ge(VE.h, VE.n)
                ss_nb1 = snap()

                @blk.scalar
                def _(s, ss_nb1=ss_nb1):
                    waits(s, ss_nb1, ["VE"])
                    I(SE, s.activation(nst2[:, 0:4], nst2[:, 0:4], AF.Ln,
                                       bias=epsc[:, 1:2], scale=-1.0))
                    s.wait_ge(SE.h, SE.n)
                    I(SE, s.activation(nst2[:, 0:4], nst2[:, 0:4], AF.Exp,
                                       bias=0.0, scale=-0.5))
                    s.wait_ge(SE.h, SE.n)
                ss_nb2 = snap()

                @blk.vector
                def _(v, ss_nb2=ss_nb2):
                    waits(v, ss_nb2, ["SE"])
                    for c in range(CPB):
                        for h in range(2):
                            cl = c * 2 + h
                            I(VE, v.tensor_scalar(hNb[:, c, h, :], hNb[:, c, h, :],
                                                  nst2[:, 8 + cl:9 + cl], nst2[:, cl:cl + 1],
                                                  OP.subtract, OP.mult))
                    v.wait_ge(VE.h, VE.n)
                    for c in range(CPB):
                        hv2 = hNb[:, c, :, :].rearrange("p b c -> p (b c)")
                        I(VE, v.tensor_mul(hv2, hv2, gamr[:, :]))
                        v.wait_ge(VE.h, VE.n)
                        I(VE, v.tensor_mul(hv2, hv2,
                                           sgb[:, c, :, :].rearrange("p b c -> p (b c)")))
                        v.wait_ge(VE.h, VE.n)
                ss_nb3 = snap()

                te_tr = TE.n
                ve_tr = VE.n

                @blk.tensor
                def _(t, ss_nb3=ss_nb3, ve_tr=ve_tr):
                    waits(t, ss_nb3, ["VE"])
                    i = 0
                    for c in range(CPB):
                        for h in range(2):
                            for dvt in range(2):
                                if i >= 2:
                                    t.wait_ge(VE.h, ve_tr + i - 1)
                                I(TE, t.transpose(P[3 + i % 2][0:128, 0:128],
                                                  hNb[:, c, h, dvt * 128:(dvt + 1) * 128],
                                                  ident_fs[:, :]))
                                i += 1

                @blk.vector
                def _(v, te_tr=te_tr, ssb=ssb):
                    waits(v, ssb, ["EX"])
                    i = 0
                    for c in range(CPB):
                        for h in range(2):
                            for dvt in range(2):
                                v.wait_ge(TE.h, te_tr + i + 1)
                                I(VE, v.tensor_copy(exb[:, h * 2 + dvt, c * 128:(c + 1) * 128],
                                                    P[3 + i % 2][0:128, 0:128]))
                                i += 1
                ss_mix = snap()

                if debug and k == 0:
                    @blk.sync
                    def _(sy, ss_mix=ss_mix):
                        waits(sy, ss_mix, ["VE"])
                        I(DB, sy.dma_start(out=dbg["dbg_state"][:, :], in_=state[:, 0, :]), 16)
                    _ssdbg2 = snap()

                    @blk.vector
                    def _(v, _ssdbg2=_ssdbg2):
                        waits(v, _ssdbg2, ["DB"])
                        I(VE, v.engine_nop())

                @blk.sync
                def _(sy, T0=T0, ss_mix=ss_mix):
                    waits(sy, ss_mix, ["VE"])
                    for vc in range(4):
                        I(EX, sy.dma_start(out=hout_in.ap()[vc * 128:(vc + 1) * 128, T0:T0 + BLK],
                                           in_=exb[:, vc, :]), 16)
                prev = snap()

            ss_p1 = prev

        @blk.gpsimd
        def _(gp):
            gp.wait_ge(EX.h, ss_p1["EX"])
            I(CC, gp.collective_compute("AllGather", OP.bypass,
                                        replica_groups=[[0, 1, 2, 3], [4, 5, 6, 7]],
                                        ins=[hout_in.ap().opt()], outs=[hout_out.ap().opt()]))
        ss_hag = snap()

        x2t = [Et(nc.sbuf_tensor(f"x2t{i}", [128, D], F32)) for i in range(2)]
        outs2 = [Et(nc.sbuf_tensor(f"outs{i}", [128, 1024], F32)) for i in range(2)]
        xsc = Et(nc.sbuf_tensor("xsc", [128, D], F32))
        sts2 = Et(nc.sbuf_tensor("sts2", [128, 10], F32))

        with ExitStack() as p2a:
            E2 = p2a.enter_context
            woutb = E2(nc.sbuf_tensor("woutb", [128, NKD, D], BF16))
            houtmy = E2(nc.sbuf_tensor("houtmy", [128, NKD, TOK], BF16))

            @blk.sync
            def _(sy):
                waits(sy, ss_wfag, ["CC"])
                wflat = wffn_bf.ap()[WOUTOFF:WOUTOFF + 128 * NKD * D]
                wpk = wflat.rearrange("(p k w) -> p k w", p=128, k=NKD)
                for i in range(4):
                    I(LH, sy.dma_start(out=woutb[:, 4 * i:4 * (i + 1), :],
                                       in_=wpk[:, 4 * i:4 * (i + 1), :]), 16)
            ss_wout = snap()

            @blk.sync
            def _(sy, ss_hag=ss_hag, ss_wout=ss_wout):
                waits(sy, ss_hag, ["CC"])
                waits(sy, ss_wout, ["LH"])
                pid = nc.partition_id(engines=[SP_ENG])
                for vc in range(NKD):
                    I(LH, sy.dma_start(out=houtmy[:, vc, :],
                                       in_=hout_out.ap()[vc * 128:(vc + 1) * 128,
                                                         bass.ts(pid % 4, TOK)]), 16)
            ss_hmy = snap()

            @blk.vector
            def _(v, ss_wout=ss_wout, ss_p1=ss_p1):
                waits(v, ss_wout, ["LH"])
                waits(v, ss_p1, ["VE"])
                for kk in range(NKD):
                    I(VE, v.tensor_mul(woutb[:, kk, :], woutb[:, kk, :], garep[:, :]))
            ss_wfold = snap()

            prev_w = ss_wfold
            x2st_ck = {}
            for tt in range(NTT):
                @blk.sync
                def _(sy, tt=tt, pw=prev_w):
                    if tt >= 2:
                        waits(sy, pw, ["VE"])
                    I(LT[tt % 2], sy.dma_start(out=x2t[tt % 2][:, :],
                                               in_=xresd[tt * 128:(tt + 1) * 128, :]), 16)
                lt_ck_w = LT[tt % 2].n
                ss_xl = snap()

                @blk.tensor
                def _(t, tt=tt, pw=prev_w, ss_hmy=ss_hmy, ss_wfold=ss_wfold):
                    waits(t, pw, ["VE"])
                    if tt == 0:
                        waits(t, ss_hmy, ["LH"])
                        waits(t, ss_wfold, ["VE"])
                    for dc in range(4):
                        for vc in range(NKD):
                            mi = t.matmul(P[dc][:, :], houtmy[:, vc, tt * 128:(tt + 1) * 128],
                                          woutb[:, vc, dc * 512:(dc + 1) * 512],
                                          start=(vc == 0), stop=(vc == NKD - 1))
                            if vc == NKD - 1:
                                I(TE, mi)
                ss_wmm = snap()

                @blk.vector
                def _(v, tt=tt, ss_wmm=ss_wmm, ss_xl=ss_xl):
                    waits(v, ss_wmm, ["TE"])
                    v.wait_ge(LT[tt % 2].h, ss_xl["LT%d" % (tt % 2)])
                    if tt >= 2:
                        v.wait_ge(SX[tt % 2].h, x2st_ck[tt - 2])
                    for dc in range(4):
                        I(VE, v.tensor_add(x2t[tt % 2][:, dc * 512:(dc + 1) * 512],
                                           x2t[tt % 2][:, dc * 512:(dc + 1) * 512],
                                           P[dc][:, :]))
                        v.wait_ge(VE.h, VE.n)
                ss_we = snap()

                @blk.sync
                def _(sy, tt=tt, ss_we=ss_we):
                    waits(sy, ss_we, ["VE"])
                    I(SX[tt % 2], sy.dma_start(out=x2d.ap()[tt * 128:(tt + 1) * 128, :],
                                               in_=x2t[tt % 2][:, :]), 16)
                x2st_ck[tt] = SX[tt % 2].n
                prev_w = snap()
            ss_x2 = prev_w

            if debug:
                @blk.sync
                def _(sy, ss_x2=ss_x2):
                    waits(sy, ss_x2, ["VE"])
                    I(DB, sy.dma_start(out=dbg["dbg_x2"][:, :], in_=x2t[0][:, :]), 16)
                _ssdbg = snap()

                @blk.vector
                def _(v, _ssdbg=_ssdbg):
                    waits(v, _ssdbg, ["DB"])
                    I(VE, v.engine_nop())

        with ExitStack() as p2b:
            E3 = p2b.enter_context
            h2 = E3(nc.sbuf_tensor("h2", [128, NKD, THALF], BF16))
            hmc = E3(nc.sbuf_tensor("hmc", [128, NKH, THALF], BF16))
            w1b = [E3(nc.sbuf_tensor(f"w1b{i}", [128, NKD, 128], BF16)) for i in range(2)]
            w2b = [E3(nc.sbuf_tensor(f"w2b{i}", [128, 2048], BF16)) for i in range(2)]
            combs = E3(nc.sbuf_tensor("combs", [128, TTH, E_], F32))
            combrow = E3(nc.sbuf_tensor("combrow", [1, E_, THALF], BF16))
            comb8 = E3(nc.sbuf_tensor("comb8", [E_, THALF], BF16))
            combrep = E3(nc.sbuf_tensor("combrep", [128, THALF], BF16))
            P7 = E3(nc.psum_tensor("P7", [128, 512], F32))
            PF = P + [P7]

            prev_half = snap()
            for hh in range(2):
                prev_ln = prev_half
                for tl in range(TTH):
                    tt = hh * TTH + tl

                    @blk.sync
                    def _(sy, tt=tt, tl=tl, pl=prev_ln):
                        waits(sy, pl, ["VE", "TE", "SE", "SX0", "SX1", "SO0", "SO1"])
                        I(LT[tl % 2], sy.dma_start(out=x2t[tl % 2][:, :],
                                                   in_=x2d.ap()[tt * 128:(tt + 1) * 128, :]), 16)
                    ss_xld = snap()

                    @blk.scalar
                    def _(s, tl=tl, pl=prev_ln, ss_xld=ss_xld):
                        waits(s, pl, ["VE", "TE", "SE", "GC", "CC"])
                        s.wait_ge(LT[tl % 2].h, ss_xld["LT%d" % (tl % 2)])
                        I(SE, s.activation(xsc[:, :], x2t[tl % 2][:, :], AF.Square,
                                           accum_out=sts2[:, 0:1]))
                        s.wait_ge(SE.h, SE.n)
                        I(SE, s.activation(xsc[:, :], x2t[tl % 2][:, :], AF.Copy,
                                           accum_out=sts2[:, 1:2]))
                        s.wait_ge(SE.h, SE.n)
                    ss_a = snap()

                    @blk.vector
                    def _(v, ss_a=ss_a):
                        waits(v, ss_a, ["SE"])
                        I(VE, v.tensor_scalar(sts2[:, 4:5], sts2[:, 1:2], 1.0 / D, None, OP.mult))
                        I(VE, v.tensor_scalar(sts2[:, 8:9], sts2[:, 0:1], 1.0 / D, None, OP.mult))
                        v.wait_ge(VE.h, VE.n)
                        I(VE, v.scalar_tensor_tensor(sts2[:, 5:6], sts2[:, 4:5], sts2[:, 4:5],
                                                     sts2[:, 8:9], OP.mult, OP.subtract))
                        v.wait_ge(VE.h, VE.n)
                    ss_b = snap()

                    @blk.scalar
                    def _(s, ss_b=ss_b):
                        waits(s, ss_b, ["VE"])
                        I(SE, s.activation(sts2[:, 5:6], sts2[:, 5:6], AF.Ln,
                                           bias=epsc[:, 0:1], scale=-1.0))
                        s.wait_ge(SE.h, SE.n)
                        I(SE, s.activation(sts2[:, 5:6], sts2[:, 5:6], AF.Exp,
                                           bias=0.0, scale=-0.5))
                        s.wait_ge(SE.h, SE.n)
                    ss_c = snap()

                    @blk.vector
                    def _(v, tl=tl, ss_c=ss_c):
                        waits(v, ss_c, ["SE"])
                        I(VE, v.tensor_scalar(xsc[:, :], x2t[tl % 2][:, :], sts2[:, 4:5],
                                              sts2[:, 5:6], OP.subtract, OP.mult))
                    ss_d = snap()

                    te2 = TE.n
                    se2 = SE.n

                    @blk.tensor
                    def _(t, ss_d=ss_d, se2=se2):
                        waits(t, ss_d, ["VE", "SE"])
                        for kk in range(NKD):
                            if kk >= 2:
                                t.wait_ge(SE.h, se2 + kk - 1)
                            I(TE, t.transpose(P[kk % 2][0:128, 0:128],
                                              xsc[:, kk * 128:(kk + 1) * 128], ident_fs[:, :]))

                    @blk.scalar
                    def _(s, tl=tl, te2=te2):
                        for kk in range(NKD):
                            s.wait_ge(TE.h, te2 + kk + 1)
                            I(SE, s.activation(h2[:, kk, tl * 128:(tl + 1) * 128],
                                               P[kk % 2][0:128, 0:128], AF.Identity,
                                               bias=modcol[:, 2, kk:kk + 1],
                                               scale=modcol[:, 3, kk:kk + 1]))
                    prev_ln = snap()
                ss_h2 = prev_ln

                prev_cb = ss_h2
                for tl in range(TTH):
                    @blk.tensor
                    def _(t, tl=tl, pc=prev_cb, ss_h2=ss_h2):
                        waits(t, ss_h2, ["SE"])
                        waits(t, pc, ["VE"])
                        for kk in range(NKD):
                            t.matmul(P[2][0:128, 0:E_], h2[:, kk, tl * 128:(tl + 1) * 128],
                                     gw_s[:, kk, :], start=(kk == 0), stop=False)
                        I(TE, t.matmul(P[2][0:128, 0:E_], onesf[0:1, 0:128], gbs[0:1, 0:E_],
                                       start=False, stop=True))
                    ss_sc = snap()

                    @blk.vector
                    def _(v, tl=tl, ss_sc=ss_sc):
                        waits(v, ss_sc, ["TE"])
                        I(VE, v.tensor_copy(combs[:, tl, :], P[2][0:128, 0:E_]))
                        v.wait_ge(VE.h, VE.n)
                        I(VE, v.reduce_max(sts2[:, 0:1], combs[:, tl, :], axis=AX, negate=True))
                        v.wait_ge(VE.h, VE.n)
                        I(VE, v.tensor_scalar(sts2[:, 1:2], sts2[:, 0:1], -1.0, None, OP.mult))
                        v.wait_ge(VE.h, VE.n)
                        I(VE, v.tensor_scalar(xsc[:, 0:E_], combs[:, tl, :], sts2[:, 1:2],
                                              None, OP.is_ge))
                        v.wait_ge(VE.h, VE.n)
                        I(VE, v.reduce_sum(sts2[:, 2:3], xsc[:, 0:E_], axis=AX))
                        I(VE, v.scalar_tensor_tensor(xsc[:, E_:2 * E_], xsc[:, 0:E_], NEG,
                                                     combs[:, tl, :], OP.mult, OP.add))
                        v.wait_ge(VE.h, VE.n)
                        I(VE, v.reduce_max(sts2[:, 3:4], xsc[:, E_:2 * E_], axis=AX))
                        I(VE, v.tensor_scalar(sts2[:, 2:3], sts2[:, 2:3], 2.0, None, OP.is_ge))
                        v.wait_ge(VE.h, VE.n)
                        I(VE, v.tensor_tensor(sts2[:, 4:5], sts2[:, 1:2], sts2[:, 3:4],
                                              OP.subtract))
                        v.wait_ge(VE.h, VE.n)
                        I(VE, v.tensor_mul(sts2[:, 4:5], sts2[:, 4:5], sts2[:, 2:3]))
                        v.wait_ge(VE.h, VE.n)
                        I(VE, v.tensor_add(sts2[:, 4:5], sts2[:, 4:5], sts2[:, 3:4]))
                        v.wait_ge(VE.h, VE.n)
                        I(VE, v.tensor_scalar(xsc[:, 0:E_], combs[:, tl, :], sts2[:, 4:5],
                                              None, OP.is_ge))
                        v.wait_ge(VE.h, VE.n)
                    ss_c1 = snap()

                    @blk.scalar
                    def _(s, tl=tl, ss_c1=ss_c1):
                        waits(s, ss_c1, ["VE"])
                        I(SE, s.activation(combs[:, tl, :], combs[:, tl, :], AF.Exp,
                                           bias=sts2[:, 0:1], scale=1.0))
                        s.wait_ge(SE.h, SE.n)
                    ss_c2 = snap()

                    @blk.vector
                    def _(v, tl=tl, ss_c2=ss_c2):
                        waits(v, ss_c2, ["SE"])
                        I(VE, v.tensor_mul(combs[:, tl, :], combs[:, tl, :], xsc[:, 0:E_]))
                        v.wait_ge(VE.h, VE.n)
                        I(VE, v.reduce_sum(sts2[:, 5:6], combs[:, tl, :], axis=AX))
                        v.wait_ge(VE.h, VE.n)
                        I(VE, v.reciprocal(sts2[:, 6:7], sts2[:, 5:6]))
                        v.wait_ge(VE.h, VE.n)
                        I(VE, v.tensor_scalar(combs[:, tl, :], combs[:, tl, :], sts2[:, 6:7],
                                              None, OP.mult))
                    ss_c3 = snap()

                    if debug and hh == 0 and tl == 0:
                        @blk.sync
                        def _(sy, ss_c3=ss_c3):
                            waits(sy, ss_c3, ["VE"])
                            I(DB, sy.dma_start(out=dbg["dbg_comb"][:, :], in_=combs[:, 0, :]), 16)
                        _ssdbg = snap()

                        @blk.vector
                        def _(v, _ssdbg=_ssdbg):
                            waits(v, _ssdbg, ["DB"])
                            I(VE, v.engine_nop())

                    for grp in range(2):
                        @blk.tensor
                        def _(t, tl=tl, grp=grp, sw=snap()):
                            waits(t, sw, ["VE"])
                            for ei in range(4):
                                e = grp * 4 + ei
                                mi = t.transpose(P[3][0:1, ei * 128:(ei + 1) * 128],
                                                 combs[:, tl, e:e + 1], ident_fs[:, :])
                                if ei == 3:
                                    I(TE, mi)
                            if grp == 0:
                                I(TE, t.transpose(P[4][0:E_, 0:128], combs[:, tl, :],
                                                  ident_fs[:, :]))
                        ss_ct = snap()

                        @blk.vector
                        def _(v, tl=tl, grp=grp, ss_ct=ss_ct):
                            waits(v, ss_ct, ["TE"])
                            for ei in range(4):
                                e = grp * 4 + ei
                                I(VE, v.tensor_copy(combrow[0:1, e, tl * 128:(tl + 1) * 128],
                                                    P[3][0:1, ei * 128:(ei + 1) * 128]))
                            if grp == 0:
                                I(VE, v.tensor_copy(comb8[:, tl * 128:(tl + 1) * 128],
                                                    P[4][0:E_, 0:128]))
                    prev_cb = snap()
                ss_comb = prev_cb

                prev_hm = ss_comb
                w1free_te = mkL[-1] if hh > 0 else None
                mkL = []
                sil_ck = []
                sil_se_ck = []
                for e in range(E_):
                    @blk.tensor
                    def _(t, e=e, pm=prev_hm):
                        waits(t, pm, ["VE", "SE"])
                        I(TE, t.matmul(P[6][:, 0:THALF], onesb[0:1, 0:128],
                                       combrow[0:1, e, :], start=True, stop=True))
                    ss_cr = snap()

                    @blk.vector
                    def _(v, e=e, ss_cr=ss_cr):
                        waits(v, ss_cr, ["TE"])
                        if e > 0:
                            v.wait_ge(TE.h, mkL[-1])
                        I(VE, v.tensor_copy(combrep[:, :], P[6][:, 0:THALF]))
                    ss_cre = snap()

                    for ht in range(NHT):
                        L = e * NHT + ht
                        PH = [P[5], P7][L % 2]

                        @blk.sync
                        def _(sy, e=e, ht=ht, L=L, w1free_te=w1free_te):
                            if L < 2:
                                if w1free_te is not None:
                                    sy.wait_ge(TE.h, w1free_te)
                            else:
                                sy.wait_ge(TE.h, mkL[L - 2])
                            off = L * W1BLK
                            I(W1S[L % 2], sy.dma_start(out=w1b[L % 2][:, :, :],
                                              in_=wffn_bf.ap()[off:off + W1BLK]
                                              .rearrange("(p k h) -> p k h", p=128, k=NKD)), 16)
                        ss_w1 = snap()

                        @blk.tensor
                        def _(t, ht=ht, L=L, PH=PH, ss_w1=ss_w1, ss_cre=ss_cre):
                            t.wait_ge(W1S[L % 2].h, ss_w1["W1%d" % (L % 2)])
                            if ht == 0:
                                waits(t, ss_cre, ["VE"])
                            if L >= 2:
                                t.wait_ge(SE.h, sil_se_ck[L - 2])
                            for kk in range(NKD):
                                mi = t.matmul(PH[:, 0:THALF], w1b[L % 2][:, kk, :],
                                              h2[:, kk, :], start=(kk == 0),
                                              stop=(kk == NKD - 1))
                                if kk == NKD - 1:
                                    I(TE, mi)
                        mkL.append(TE.n)
                        ss_h1 = snap()

                        @blk.scalar
                        def _(s, e=e, ht=ht, L=L, PH=PH, ss_h1=ss_h1):
                            waits(s, ss_h1, ["TE"])
                            if L >= 1:
                                s.wait_ge(VE.h, sil_ck[L - 1])
                            I(SE, s.activation(xsc[:, 0:THALF], PH[:, 0:THALF], AF.Silu,
                                               bias=b1col[:, e, ht:ht + 1], scale=1.0))
                        sil_se_ck.append(SE.n)
                        ss_h2e = snap()

                        @blk.vector
                        def _(v, e=e, ht=ht, ss_h2e=ss_h2e):
                            waits(v, ss_h2e, ["SE"])
                            I(VE, v.tensor_mul(hmc[:, e * NHT + ht, :], xsc[:, 0:THALF],
                                               combrep[:, :]))
                        sil_ck.append(VE.n)
                        prev_hm = snap()
                ss_hmid = prev_hm

                for dp in range(2):
                    swd = snap()
                    te3 = TE.n

                    w2_base = {0: W20.n, 1: W21.n}

                    @blk.sync
                    def _(sy, dp=dp, swd=swd, te3=te3):
                        waits(sy, swd, ["VE", "TE"])
                        for kk in range(NKH):
                            if kk >= 2:
                                sy.wait_ge(TE.h, te3 + kk - 1)
                            off = W1TOT + kk * W2BLK
                            I(W2S[kk % 2], sy.dma_start(out=w2b[kk % 2][:, :],
                                              in_=wffn_bf.ap()[off:off + W2BLK]
                                              .rearrange("(p w) -> p w", p=128)), 16)

                    @blk.tensor
                    def _(t, dp=dp, w2_base=w2_base, swd=swd):
                        waits(t, swd, ["VE"])
                        for tl in range(TTH):
                            for dc in range(2):
                                t.matmul(PF[tl * 2 + dc][:, :], comb8[:, tl * 128:(tl + 1) * 128],
                                         b2p[:, dp * 1024 + dc * 512:dp * 1024 + (dc + 1) * 512],
                                         start=True, stop=False)
                        for kk in range(NKH):
                            t.wait_ge(W2S[kk % 2].h, w2_base[kk % 2] + 16 * (kk // 2 + 1))
                            for tl in range(TTH):
                                for dc in range(2):
                                    mi = t.matmul(PF[tl * 2 + dc][:, :],
                                                  hmc[:, kk, tl * 128:(tl + 1) * 128],
                                                  w2b[kk % 2][:, dp * 1024 + dc * 512:
                                                              dp * 1024 + (dc + 1) * 512],
                                                  start=False, stop=(kk == NKH - 1))
                                    if tl == TTH - 1 and dc == 1:
                                        I(TE, mi)
                    ss_y2 = snap()

                    st_ck = {}
                    for tl in range(TTH):
                        @blk.sync
                        def _(sy, dp=dp, hh=hh, tl=tl, ss_y2=ss_y2, st_ck=dict(st_ck)):
                            waits(sy, ss_y2, ["TE"])
                            if tl >= 2:
                                sy.wait_ge(VE.h, st_ck[(tl - 2, "ve")])
                            tt = hh * TTH + tl
                            I(LT[tl % 2], sy.dma_start(out=x2t[tl % 2][:, 0:1024],
                                               in_=x2d.ap()[tt * 128:(tt + 1) * 128,
                                                            dp * 1024:(dp + 1) * 1024]), 16)
                        ss_xh2 = snap()

                        @blk.vector
                        def _(v, dp=dp, tl=tl, ss_y2=ss_y2, ss_xh2=ss_xh2,
                              st_ck=dict(st_ck)):
                            waits(v, ss_y2, ["TE", "SO0", "SO1"])
                            v.wait_ge(LT[tl % 2].h, ss_xh2["LT%d" % (tl % 2)])
                            if tl >= 2:
                                v.wait_ge(SO[tl % 2].h, st_ck[(tl - 2, "st")])
                            ob = outs2[tl % 2]
                            for dc in range(2):
                                I(VE, v.tensor_mul(ob[:, dc * 512:(dc + 1) * 512],
                                                   PF[tl * 2 + dc][:, :],
                                                   gbrep[:, dp * 1024 + dc * 512:
                                                          dp * 1024 + (dc + 1) * 512]))
                                v.wait_ge(VE.h, VE.n)
                                I(VE, v.tensor_add(ob[:, dc * 512:(dc + 1) * 512],
                                                   ob[:, dc * 512:(dc + 1) * 512],
                                                   x2t[tl % 2][:, dc * 512:(dc + 1) * 512]))
                                v.wait_ge(VE.h, VE.n)
                        ss_ev = snap()
                        st_ck[(tl, "ve")] = VE.n

                        @blk.sync
                        def _(sy, dp=dp, hh=hh, tl=tl, ss_ev=ss_ev):
                            waits(sy, ss_ev, ["VE"])
                            tt = hh * TTH + tl
                            I(SO[tl % 2], sy.dma_start(out=out[tt * 128:(tt + 1) * 128,
                                                       dp * 1024:(dp + 1) * 1024],
                                               in_=outs2[tl % 2][:, :]), 16)
                        st_ck[(tl, "st")] = SO[tl % 2].n
                    prev_half = snap()

        @blk.sync
        def _(sy):
            for sm in [SO0, SO1, SX0, SX1, EX, CC, DB]:
                if sm.n > 0:
                    sy.wait_ge(sm.h, sm.n)

    return nc


def _prep(inputs, cfg):
    S, TOK = cfg.S, cfg.TOK
    x = np.asarray(inputs["L0_attn_x"], np.float32)
    Wq = np.asarray(inputs["L0_attn_Wq"], np.float32)
    Wk = np.asarray(inputs["L0_attn_Wk"], np.float32)
    Wv = np.asarray(inputs["L0_attn_Wv"], np.float32)
    Wog = np.asarray(inputs["L0_attn_Wog"], np.float32)
    Wig = np.asarray(inputs["L0_attn_Wig"], np.float32)
    big = np.asarray(inputs["L0_attn_big"], np.float32)
    Wfg = np.asarray(inputs["L0_attn_Wfg"], np.float32)
    bfg = np.asarray(inputs["L0_attn_bfg"], np.float32)
    gamma = np.asarray(inputs["L0_attn_gamma"], np.float32)
    Wout = np.asarray(inputs["L0_attn_Wout"], np.float32)
    gate_w = np.asarray(inputs["L0_ffn_gate_w"], np.float32)
    gate_b = np.asarray(inputs["L0_ffn_gate_b"], np.float32)
    bias = np.asarray(inputs["L0_ffn_bias"], np.float32)
    w1 = np.asarray(inputs["L0_ffn_w1"], np.float32)
    b1 = np.asarray(inputs["L0_ffn_b1"], np.float32)
    w2 = np.asarray(inputs["L0_ffn_w2"], np.float32)
    b2 = np.asarray(inputs["L0_ffn_b2"], np.float32)
    cc = np.asarray(inputs["L0_c"], np.float32)
    wmod = np.asarray(inputs["L0_wmod"], np.float32)

    C = np.ascontiguousarray

    def PK(a):
        a = np.asarray(a)
        w = a.shape[1]
        return C(a.reshape(16, 128, w).transpose(1, 0, 2).reshape(128, 16 * w))
    WqT, WkT, WvT, WogT = C(Wq.T), C(Wk.T), C(Wv.T), C(Wog.T)
    WigT, WfgT = C(Wig.T), C(Wfg.T)
    WoutT = C(Wout.T)
    gwT = C(gate_w.T)
    w1T = w1.transpose(0, 2, 1)
    w1flat = C(w1T.reshape(E_, NKD, 128, NHT, 128).transpose(0, 3, 2, 1, 4)).reshape(-1)
    w2flat = C(w2.transpose(0, 2, 1)).reshape(-1)
    wffn_full = np.concatenate([w1flat, w2flat, PK(WoutT).reshape(-1)]).reshape(8, -1)

    in_maps = []
    for c in range(8):
        b, g = c // 4, c % 4
        hg = 2 * g
        m = {
            "xb": C(x[b, :S]),
            "xresd": C(x[b, g * TOK:(g + 1) * TOK]),
            "wqT": PK(WqT[:, hg * 128:(hg + 2) * 128]),
            "wkT": PK(WkT[:, hg * 128:(hg + 2) * 128]),
            "wvT": PK(WvT[:, hg * 256:(hg + 2) * 256]),
            "wogT": PK(WogT[:, hg * 256:(hg + 2) * 256]),
            "wifT": PK(np.stack([WigT[:, hg], WfgT[:, hg],
                                WigT[:, hg + 1], WfgT[:, hg + 1]], axis=1)),
            "bifr": np.array([[big[hg], bfg[hg], big[hg + 1], bfg[hg + 1]]], np.float32),
            "gammag": C(gamma[hg:hg + 2].reshape(1, 512)),
            "wmod_sl": C(np.concatenate([wmod[:, (g + 4 * i) * 512:(g + 4 * i + 1) * 512]
                                         for i in range(6)], axis=1)),
            "gwTd": gwT,
            "gb2d": gate_b.reshape(1, E_),
            "gbiasd": bias.reshape(1, E_),
            "b1e": b1,
            "b2e": b2,
            "cvecd": cc[b].reshape(1, D),
            "wffn": wffn_full[c],
        }
        in_maps.append(m)
    return in_maps


def kernel(**inputs):
    cfg = CFG(int(np.asarray(inputs["L0_attn_x"]).shape[1]))
    in_maps = _prep(inputs, cfg)
    nc = build_program(cfg, debug=False)
    res = run_bass_kernel_spmd(nc, in_maps, core_ids=list(range(8)))
    B_, S_ = np.asarray(inputs["L0_attn_x"]).shape[0], cfg.S
    outa = np.empty((B_, S_, D), np.float32)
    for c in range(8):
        b, g = c // 4, c % 4
        outa[b, g * cfg.TOK:(g + 1) * cfg.TOK] = res.results[c]["out"]
    return outa
